# version 05
# speedup vs baseline: 1.0898x; 1.0819x over previous
import numpy as np
import ml_dtypes
import concourse.bass as bass
import concourse.mybir as mybir
from concourse.bass_utils import run_bass_kernel_spmd
from contextlib import ExitStack

F32 = mybir.dt.float32
BF16 = mybir.dt.bfloat16
AF = mybir.ActivationFunctionType
ALU = mybir.AluOpType
AX = mybir.AxisListType

NCORES = 8
T = 2048
D = 1024
NCH = D // 128
TT = 512
NTT = T // TT

ENGINES = ("tensor", "vector", "scalar", "gpsimd", "sync")


class Prog:

    def __init__(self, nc, es):
        self.nc = nc
        self.csem = {e: es.enter_context(nc.semaphore(f"c_{e}")) for e in ENGINES}
        self.dsem = {e: es.enter_context(nc.semaphore(f"d_{e}")) for e in ENGINES}
        self.ops = {e: [] for e in ENGINES}
        self.ndma = {e: 0 for e in ENGINES}
        self.writer = {}
        self.readers = {}
        self.waited = {e: {} for e in ENGINES}

    def _need(self, eng, kind, e2, idx, waits):
        import os
        if e2 == eng and kind == "c" and (eng == "tensor" or os.environ.get("NOSELF")):
            return
        got = self.waited[eng].get((kind, e2), 0)
        if idx > got:
            waits.append((kind, e2, idx))
            self.waited[eng][(kind, e2)] = idx

    def op(self, eng, fn, reads=(), writes=(), dma=False, tag=None):
        waits = []

        def dep(kind, e2, idx):
            if kind == "d":
                idx = self.ndma[e2]
            self._need(eng, kind, e2, idx, waits)

        for k in reads:
            w = self.writer.get(k)
            if w is not None:
                dep(w[0], w[1], w[2])
        for k in writes:
            w = self.writer.get(k)
            if w is not None:
                dep(w[0], w[1], w[2])
            for (kind, e2), idx in self.readers.get(k, {}).items():
                dep(kind, e2, idx)
        rec = {"fn": fn, "waits": waits, "flag": dma, "dma": dma, "tag": tag}
        self.ops[eng].append(rec)
        if dma:
            self.ndma[eng] += 1
            my = ("d", eng, self.ndma[eng])
        else:
            my = ("c", eng, len(self.ops[eng]))
        for k in writes:
            self.writer[k] = my
            self.readers.pop(k, None)
        for k in reads:
            d = self.readers.setdefault(k, {})
            kk = (my[0], my[1])
            d[kk] = max(d.get(kk, 0), my[2])
        for kind, e2, idx in waits:
            if kind == "c":
                self.ops[e2][idx - 1]["flag"] = True
        return my

    def fence(self, eng, reads=(), writes=()):
        self.op(eng, lambda x: x.nop(), reads=reads, writes=writes, tag="fence")

    def sub_barrier(self):
        pre_c = {e: len(self.ops[e]) for e in ENGINES}
        pre_d = {e: self.ndma[e] for e in ENGINES}
        for e in ENGINES:
            waits = []
            for e2 in ENGINES:
                if e2 != e and pre_c[e2] > 0:
                    got = self.waited[e].get(("c", e2), 0)
                    if pre_c[e2] > got:
                        waits.append(("c", e2, pre_c[e2]))
                        self.waited[e][("c", e2)] = pre_c[e2]
                    self.ops[e2][pre_c[e2] - 1]["flag"] = True
                if pre_d[e2] > 0:
                    got = self.waited[e].get(("d", e2), 0)
                    if pre_d[e2] > got:
                        waits.append(("d", e2, pre_d[e2]))
                        self.waited[e][("d", e2)] = pre_d[e2]
            self.ops[e].append({"fn": lambda x: x.nop(), "waits": waits, "flag": False,
                                "dma": False, "tag": "barrier"})
        self.writer = {}
        self.readers = {}

    def barrier_all_dma_done(self, eng):
        waits = []
        for e2 in ENGINES:
            if self.ndma[e2]:
                self._need(eng, "d", e2, self.ndma[e2], waits)
        self.ops[eng].append({"fn": lambda x: x.nop(), "waits": waits, "flag": False,
                              "dma": False, "tag": "final"})

    def finalize(self, block):
        cpref = {}
        for e in ENGINES:
            acc = 0
            pref = []
            for r in self.ops[e]:
                if r["flag"] and not r["dma"]:
                    acc += 1
                pref.append(acc)
            cpref[e] = pref

        def replay(e):
            def body(engobj):
                for r in self.ops[e]:
                    for kind, e2, idx in r["waits"]:
                        if kind == "c":
                            engobj.wait_ge(self.csem[e2], cpref[e2][idx - 1])
                        else:
                            engobj.wait_ge(self.dsem[e2], 16 * idx)
                    inst = r["fn"](engobj)
                    if r["dma"]:
                        inst.then_inc(self.dsem[e], 16)
                    elif r["flag"]:
                        inst.then_inc(self.csem[e], 1)
            return body

        for e in ENGINES:
            if self.ops[e]:
                getattr(block, e)(replay(e))



def host_constants():
    ones = np.ones((128, 128), dtype=ml_dtypes.bfloat16)
    ident = np.eye(128, dtype=ml_dtypes.bfloat16)
    p = np.arange(128)[:, None]
    j = np.arange(512)[None, :]
    masks = np.stack([(p <= j - 128 * m).astype(ml_dtypes.bfloat16) for m in range(4)])
    return ones, ident, masks



def emit_ln(P, H, xin_hbm, xkey, Y, ykey):
    XT, XB, XSQB, PS = H["XT"], H["XB"], H["XSQB"], H["PS"]
    RBB, MBB, ones, RBF = H["RBB"], H["MBB"], H["ONES"], H["ROWSBF"]

    for c in range(NCH):
        P.op("sync", lambda s, c=c: s.dma_start(out=XT[:], in_=xin_hbm[c * 128:(c + 1) * 128, :]),
             reads=[xkey], writes=["XT"], dma=True)
        P.op("vector", lambda v: v.tensor_copy(XB[:], XT[:]), reads=["XT"], writes=["XB"])
        P.op("scalar", lambda sc: sc.activation(XSQB[:], XB[:], AF.Square),
             reads=["XB"], writes=["XSQB"])
        for tt in range(NTT):
            sl = slice(tt * TT, (tt + 1) * TT)
            P.op("tensor", lambda t, sl=sl, tt=tt, c=c: t.matmul(
                PS[tt][0:1, :], ones[:, 0:1], XB[:, sl], start=(c == 0), stop=(c == NCH - 1)),
                reads=["XB", "ONES"], writes=[("ps", tt)])
            P.op("tensor", lambda t, sl=sl, tt=tt, c=c: t.matmul(
                PS[4 + tt][0:1, :], ones[:, 0:1], XSQB[:, sl], start=(c == 0), stop=(c == NCH - 1)),
                reads=["XSQB", "ONES"], writes=[("ps", 4 + tt)])
    for tt in range(NTT):
        sl = slice(tt * TT, (tt + 1) * TT)
        P.op("vector", lambda v, tt=tt: v.tensor_scalar_mul(RBF[0:1, 0:TT], PS[tt][0:1, :], 1.0 / D),
             reads=[("ps", tt)], writes=["rowsbf"])
        P.op("tensor", lambda t, tt=tt: t.matmul(PS[tt][:, :], ones[0:1, :], RBF[0:1, 0:TT],
                                                 start=True, stop=True),
             reads=["rowsbf", "ONES"], writes=[("ps", tt)])
        P.op("vector", lambda v, sl=sl, tt=tt: v.tensor_copy(MBB[:, sl], PS[tt][:, :]),
             reads=[("ps", tt)], writes=["MBB"])
        P.op("vector", lambda v, tt=tt: v.tensor_scalar_mul(RBF[0:1, 0:TT], PS[4 + tt][0:1, :], 1.0 / D),
             reads=[("ps", 4 + tt)], writes=["rowsbf"])
        P.op("tensor", lambda t, tt=tt: t.matmul(PS[4 + tt][:, :], ones[0:1, :], RBF[0:1, 0:TT],
                                                 start=True, stop=True),
             reads=["rowsbf", "ONES"], writes=[("ps", 4 + tt)])
        P.op("vector", lambda v, sl=sl, tt=tt: v.tensor_copy(XSQB[:, sl], PS[4 + tt][:, :]),
             reads=[("ps", 4 + tt)], writes=["XSQB"])
    P.op("vector", lambda v: v.tensor_mul(RBB[:], MBB[:], MBB[:]), reads=["MBB"], writes=["RBB"])
    P.op("vector", lambda v: v.tensor_sub(RBB[:], XSQB[:], RBB[:]), reads=["XSQB", "RBB"], writes=["RBB"])
    P.op("vector", lambda v: v.tensor_scalar_add(RBB[:], RBB[:], 1e-5), reads=["RBB"], writes=["RBB"])
    P.op("scalar", lambda sc: sc.activation(RBB[:], RBB[:], AF.Ln), reads=["RBB"], writes=["RBB"])
    P.op("scalar", lambda sc: sc.activation(RBB[:], RBB[:], AF.Exp, scale=-0.5),
         reads=["RBB"], writes=["RBB"])
    for c in range(NCH):
        P.op("sync", lambda s, c=c: s.dma_start(out=XT[:], in_=xin_hbm[c * 128:(c + 1) * 128, :]),
             reads=[xkey], writes=["XT"], dma=True)
        P.op("vector", lambda v: v.tensor_copy(XB[:], XT[:]), reads=["XT"], writes=["XB"])
        P.op("vector", lambda v: v.tensor_sub(XB[:], XB[:], MBB[:]),
             reads=["XB", "MBB"], writes=["XB"])
        P.op("vector", lambda v, c=c: v.tensor_mul(Y[:, c * T:(c + 1) * T], XB[:], RBB[:]),
             reads=["XB", "RBB"], writes=[(ykey, c)])


def emit_ln_stats32(P, H, xin_hbm, xkey, M32, R32, m32key, r32key, SB16A, SB16B,
                    ka="SB16A", kb="SB16B"):
    XT, PS = H["XT"], H["PS"]
    ones, RBF = H["ONES"], H["ROWSBF"]
    for c in range(NCH):
        P.op("sync", lambda s, c=c: s.dma_start(out=XT[:], in_=xin_hbm[c * 128:(c + 1) * 128, :]),
             reads=[xkey], writes=["XT"], dma=True)
        for tt in range(NTT):
            sl = slice(tt * TT, (tt + 1) * TT)
            P.op("vector", lambda v, sl=sl: v.tensor_copy(SB16A[:], XT[:, sl]),
                 reads=["XT"], writes=[ka])
            P.op("scalar", lambda sc: sc.activation(SB16B[:], SB16A[:], AF.Square),
                 reads=[ka], writes=[kb])
            P.op("tensor", lambda t, tt=tt, c=c: t.matmul(
                PS[tt][0:1, :], ones[:, 0:1], SB16A[:], start=(c == 0), stop=(c == NCH - 1)),
                reads=[ka, "ONES"], writes=[("ps", tt)])
            P.op("tensor", lambda t, tt=tt, c=c: t.matmul(
                PS[4 + tt][0:1, :], ones[:, 0:1], SB16B[:], start=(c == 0), stop=(c == NCH - 1)),
                reads=[kb, "ONES"], writes=[("ps", 4 + tt)])
    for tt in range(NTT):
        sl = slice(tt * TT, (tt + 1) * TT)
        P.op("vector", lambda v, tt=tt: v.tensor_scalar_mul(RBF[0:1, 0:TT], PS[tt][0:1, :], 1.0 / D),
             reads=[("ps", tt)], writes=["rowsbf"])
        P.op("tensor", lambda t, tt=tt: t.matmul(PS[tt][:, :], ones[0:1, :], RBF[0:1, 0:TT],
                                                 start=True, stop=True),
             reads=["rowsbf", "ONES"], writes=[("ps", tt)])
        P.op("vector", lambda v, sl=sl, tt=tt: v.tensor_copy(M32[:, sl], PS[tt][:, :]),
             reads=[("ps", tt)], writes=[m32key])
        P.op("vector", lambda v, tt=tt: v.tensor_scalar_mul(RBF[0:1, 0:TT], PS[4 + tt][0:1, :], 1.0 / D),
             reads=[("ps", 4 + tt)], writes=["rowsbf"])
        P.op("tensor", lambda t, tt=tt: t.matmul(PS[4 + tt][:, :], ones[0:1, :], RBF[0:1, 0:TT],
                                                 start=True, stop=True),
             reads=["rowsbf", "ONES"], writes=[("ps", 4 + tt)])
        P.op("vector", lambda v, sl=sl, tt=tt: v.tensor_copy(R32[:, sl], PS[4 + tt][:, :]),
             reads=[("ps", 4 + tt)], writes=[r32key])
    P.op("vector", lambda v: v.tensor_mul(XT[:], M32[:, 0:T], M32[:, 0:T]),
         reads=[m32key, "XT"], writes=["XT"])
    P.op("vector", lambda v: v.tensor_sub(R32[:], R32[:], XT[:]), reads=[r32key, "XT"], writes=[r32key])
    P.op("vector", lambda v: v.tensor_scalar_add(R32[:], R32[:], 1e-5), reads=[r32key], writes=[r32key])
    P.op("scalar", lambda sc: sc.activation(R32[:], R32[:], AF.Ln), reads=[r32key], writes=[r32key])
    P.op("scalar", lambda sc: sc.activation(R32[:], R32[:], AF.Exp, scale=-0.5),
         reads=[r32key], writes=[r32key])



def emit_load_weight_bf16(P, H, w_hbm, wkey, WBF, wbfkey, K, M):
    nch = K // 128
    WST = H["WST"]
    for ci in range(nch):
        for mo in range(0, M, 1024):
            mw = min(1024, M - mo)
            P.op("sync", lambda s, ci=ci, mo=mo, mw=mw: s.dma_start(
                out=WST[:, 0:mw], in_=w_hbm[ci * 128:(ci + 1) * 128, mo:mo + mw]),
                reads=[wkey], writes=["XT"], dma=True)
            P.op("vector", lambda v, ci=ci, mo=mo, mw=mw: v.tensor_copy(
                WBF[:, ci * M + mo: ci * M + mo + mw], WST[:, 0:mw]),
                reads=["XT"], writes=[(wbfkey, ci)])



def emit_proj_fm(P, H, WBF, wbfkey, Y, ykey, OUT, outkey, nch_in, mout):
    PS = H["PS"]
    for dj in range(mout // 128):
        for tt in range(NTT):
            pi = (dj * NTT + tt) % 2
            ps = PS[pi]
            for ci in range(nch_in):
                P.op("tensor", lambda t, ps=ps, ci=ci, dj=dj, tt=tt: t.matmul(
                    ps[:, :], WBF[:, ci * mout + dj * 128: ci * mout + (dj + 1) * 128],
                    Y[:, ci * T + tt * TT: ci * T + (tt + 1) * TT],
                    start=(ci == 0), stop=(ci == nch_in - 1)),
                    reads=[(wbfkey, ci), (ykey, ci)], writes=[("ps", pi)])
            P.op("vector", lambda v, ps=ps, dj=dj, tt=tt: v.tensor_copy(
                OUT[:, dj * T + tt * TT: dj * T + (tt + 1) * TT], ps[:, :]),
                reads=[("ps", pi)], writes=[(outkey, dj)])



def build_l0_attn_a(debug=False):
    nc = bass.Bass("TRN2", target_bir_lowering=False)
    es = ExitStack()
    xin = nc.dram_tensor("xin", [D, T], F32, kind="ExternalInput").ap()
    wq = nc.dram_tensor("wq", [D, D], F32, kind="ExternalInput").ap()
    wk = nc.dram_tensor("wk", [D, D], F32, kind="ExternalInput").ap()
    wv = nc.dram_tensor("wv", [D, D], F32, kind="ExternalInput").ap()
    onesC = nc.dram_tensor("onesC", [128, 128], BF16, kind="ExternalInput").ap()
    masksC = nc.dram_tensor("masksC", [4, 128, 512], BF16, kind="ExternalInput").ap()
    xout = nc.dram_tensor("xout", [D, T], F32, kind="ExternalOutput").ap()
    if debug:
        dR2 = nc.dram_tensor("dR2", [1, 8], F32, kind="ExternalOutput").ap()
        dBA = nc.dram_tensor("dBA", [128, 1], F32, kind="ExternalOutput").ap()
        dQ = nc.dram_tensor("dQ", [128, T], F32, kind="ExternalOutput").ap()
        dRows = nc.dram_tensor("dRows", [1, 2 * T], F32, kind="ExternalOutput").ap()

    sb = lambda name, shape, dt: es.enter_context(nc.sbuf_tensor(name, shape, dt))
    H = {}
    H["XT"] = sb("XT", [128, T], F32)
    H["XB"] = sb("XB", [128, T], BF16)
    H["XSQB"] = sb("XSQB", [128, T], BF16)
    H["ROWS"] = sb("ROWS", [1, 3 * T + 520], F32)
    H["ROWSBF"] = sb("ROWSBF", [1, TT], BF16)
    H["MBB"] = sb("MBB", [128, T], BF16)
    H["RBB"] = sb("RBB", [128, T], BF16)
    ONES = sb("ONESs", [128, 128], BF16)
    H["ONES"] = ONES
    YB = sb("YB", [128, NCH * T], BF16)
    Y = YB[:, :]
    ET = YB[:, 0:16 * TT]
    MASKS = YB[:, 16 * TT:16 * TT + 2048]
    ZIB = YB[:, 16 * TT + 2048:16 * TT + 2560]
    Q = sb("Q", [128, NCH * T], BF16)
    K = sb("Kt", [128, NCH * T], BF16)
    V = sb("V", [128, 16 * D], BF16)

    WBF = sb("WBF", [128, NCH * D], BF16)
    H["WST"] = H["XT"][:, 0:1024]
    BA = sb("BA", [128, 1], F32)
    OTMP = sb("OTMP", [128, TT], F32)
    XRL = sb("XRL", [128, 2 * TT], F32)
    PS = [es.enter_context(nc.psum_tensor(f"ps{i}", [128, 512], F32)) for i in range(8)]
    H["PS"] = PS

    P = Prog(nc, es)
    P.op("sync", lambda s: s.dma_start(out=ONES[:], in_=onesC), writes=["ONES"], dma=True)

    emit_ln(P, H, xin, "xin", Y, "Y")

    emit_load_weight_bf16(P, H, wq, "wq", WBF, "W", D, D)
    emit_proj_fm(P, H, WBF, "W", Y, "Y", Q, "Q", NCH, D)
    emit_load_weight_bf16(P, H, wk, "wk", WBF, "W", D, D)
    emit_proj_fm(P, H, WBF, "W", Y, "Y", K, "K", NCH, D)
    emit_load_weight_bf16(P, H, wv, "wv", WBF, "W", D, D)
    for tc in range(16):
        for dh in range(2):
            pi = (tc * 2 + dh) % 2
            ps = PS[pi]
            for ci in range(NCH):
                P.op("tensor", lambda t, ps=ps, ci=ci, tc=tc, dh=dh: t.matmul(
                    ps[:, :], Y[:, ci * T + tc * 128: ci * T + (tc + 1) * 128],
                    WBF[:, ci * D + dh * 512: ci * D + (dh + 1) * 512],
                    start=(ci == 0), stop=(ci == NCH - 1)),
                    reads=[("W", ci), ("Y", ci)], writes=[("ps", pi)])
            P.op("vector", lambda v, ps=ps, tc=tc, dh=dh: v.tensor_copy(
                V[:, tc * D + dh * 512: tc * D + (dh + 1) * 512], ps[:, :]),
                reads=[("ps", pi)], writes=[("V", tc)])

    ROWS = H["ROWS"]
    r0 = ROWS[0:1, 0:T]
    r1 = ROWS[0:1, T:2 * T]
    r2 = ROWS[0:1, 3 * T:3 * T + 8]
    for idx, (SRC, skey, dst) in enumerate(((Q, "Q", r0), (K, "K", r1))):
        for c in range(NCH):
            P.op("scalar", lambda sc, c=c, SRC=SRC: sc.activation(
                H["XSQB"][:], SRC[:, c * T:(c + 1) * T], AF.Square),
                reads=[(skey, c)], writes=["XSQB"])
            for tt in range(NTT):
                pi = 4 + tt
                P.op("tensor", lambda t, tt=tt, c=c, pi=pi: t.matmul(
                    PS[pi][0:1, :], ONES[:, 0:1], H["XSQB"][:, tt * TT:(tt + 1) * TT],
                    start=(c == 0), stop=(c == NCH - 1)),
                    reads=["XSQB", "ONES"], writes=[("ps", pi)])
        for tt in range(NTT):
            pi = 4 + tt
            P.op("vector", lambda v, tt=tt, pi=pi, dst=dst: v.tensor_copy(
                dst[0:1, tt * TT:(tt + 1) * TT], PS[pi][0:1, :]),
                reads=[("ps", pi)], writes=["rows"])
    P.op("vector", lambda v: v.reduce_max(r2[0:1, 0:1], r0, axis=AX.X), reads=["rows"], writes=["rows"])
    P.op("vector", lambda v: v.reduce_max(r2[0:1, 1:2], r1, axis=AX.X), reads=["rows"], writes=["rows"])
    P.op("vector", lambda v: v.tensor_add(r2[0:1, 2:3], r2[0:1, 0:1], r2[0:1, 1:2]),
         reads=["rows"], writes=["rows"])
    P.op("vector", lambda v: v.tensor_scalar_mul(r2[0:1, 3:4], r2[0:1, 2:3], -1.0 / 64.0),
         reads=["rows"], writes=["rows"])
    RBF = H["ROWSBF"]
    P.op("vector", lambda v: v.tensor_copy(RBF[0:1, 0:1], r2[0:1, 3:4]), reads=["rows"], writes=["rowsbf"])
    P.op("tensor", lambda t: t.matmul(PS[7][:, 0:1], ONES[0:1, :], RBF[0:1, 0:1], start=True, stop=True),
         reads=["rowsbf", "ONES"], writes=[("ps", 7)])
    P.op("vector", lambda v: v.tensor_copy(BA[:], PS[7][:, 0:1]), reads=[("ps", 7)], writes=["BA"])
    if debug:
        P.op("gpsimd", lambda g: g.dma_start(out=dR2, in_=r2), reads=["rows"], writes=[("dump",)], dma=True)
        P.op("gpsimd", lambda g: g.dma_start(out=dBA, in_=BA[:]), reads=["BA"], writes=[("dump",)], dma=True)
        P.op("vector", lambda v: v.tensor_copy(H["XT"][:], Q[:, 0:T]), reads=[("Q", 0)], writes=["XT"])
        P.op("gpsimd", lambda g: g.dma_start(out=dQ, in_=H["XT"][:]), reads=["XT"], writes=[("dump",)], dma=True)
        P.op("gpsimd", lambda g: g.dma_start(out=dRows, in_=ROWS[0:1, 0:2 * T]), reads=["rows"], writes=[("dump",)], dma=True)

    P.fence("scalar", reads=[("Y", c) for c in range(NCH)],
            writes=[("ET", kc) for kc in range(16)] + ["MASKS", "ZIB"])
    P.op("sync", lambda s: s.dma_start(
        out=MASKS.rearrange("p (m n) -> p m n", m=4), in_=masksC.rearrange("m p n -> p m n")),
        writes=["MASKS"], dma=True)

    scale = 1.0 / 32.0
    zr = ROWS[0:1, 3 * T + 8:3 * T + 8 + TT]
    for qt in range(NTT):
        nkc = min(16, 4 * qt + 4)
        for kc in range(nkc):
            pi = kc % 2
            ps = PS[pi]
            for ci in range(NCH):
                P.op("tensor", lambda t, ps=ps, ci=ci, kc=kc, qt=qt: t.matmul(
                    ps[:, :], K[:, ci * T + kc * 128: ci * T + (kc + 1) * 128],
                    Q[:, ci * T + qt * TT: ci * T + (qt + 1) * TT],
                    start=(ci == 0), stop=(ci == NCH - 1)),
                    reads=[("K", ci), ("Q", ci)], writes=[("ps", pi)])
            P.op("scalar", lambda sc, ps=ps, kc=kc: sc.activation(
                ET[:, kc * TT:(kc + 1) * TT], ps[:, :], AF.Exp, bias=BA[:], scale=scale),
                reads=[("ps", pi), "BA"], writes=[("ET", kc)])
            m = kc - 4 * qt
            if m >= 0:
                P.op("vector", lambda v, kc=kc, m=m: v.tensor_mul(
                    ET[:, kc * TT:(kc + 1) * TT], ET[:, kc * TT:(kc + 1) * TT],
                    MASKS[:, m * 512:(m + 1) * 512]),
                    reads=[("ET", kc), "MASKS"], writes=[("ET", kc)])
        for kc in range(nkc):
            P.op("tensor", lambda t, kc=kc, nkc=nkc: t.matmul(
                PS[6][0:1, :], ONES[:, 0:1], ET[:, kc * TT:(kc + 1) * TT],
                start=(kc == 0), stop=(kc == nkc - 1)),
                reads=[("ET", kc), "ONES"], writes=[("ps", 6)])
        P.op("vector", lambda v: v.tensor_copy(zr, PS[6][0:1, :]), reads=[("ps", 6)], writes=["zrow"])
        P.op("vector", lambda v: v.reciprocal(zr, zr), reads=["zrow"], writes=["zrow"])
        P.op("vector", lambda v: v.tensor_copy(RBF[0:1, 0:TT], zr), reads=["zrow"], writes=["rowsbf"])
        P.op("tensor", lambda t: t.matmul(PS[6][:, :], ONES[0:1, :], RBF[0:1, 0:TT],
                                          start=True, stop=True),
             reads=["rowsbf", "ONES"], writes=[("ps", 6)])
        P.op("vector", lambda v: v.tensor_copy(ZIB[:], PS[6][:, :]), reads=[("ps", 6)], writes=["ZIB"])
        for kc in range(nkc):
            P.op("vector", lambda v, kc=kc: v.tensor_mul(
                ET[:, kc * TT:(kc + 1) * TT], ET[:, kc * TT:(kc + 1) * TT], ZIB[:]),
                reads=[("ET", kc), "ZIB"], writes=[("ET", kc)])
        for dj in range(NCH):
            pi = 2 + dj % 2
            ps = PS[pi]
            for kc in range(nkc):
                P.op("tensor", lambda t, ps=ps, kc=kc, dj=dj, nkc=nkc: t.matmul(
                    ps[:, :], V[:, kc * D + dj * 128: kc * D + (dj + 1) * 128],
                    ET[:, kc * TT:(kc + 1) * TT],
                    start=(kc == 0), stop=(kc == nkc - 1)),
                    reads=[("V", kc), ("ET", kc)], writes=[("ps", pi)])
            b = dj % 2
            xrl = XRL[:, b * TT:(b + 1) * TT]
            P.op("sync", lambda s, dj=dj, qt=qt, xrl=xrl: s.dma_start(
                out=xrl, in_=xin[dj * 128:(dj + 1) * 128, qt * TT:(qt + 1) * TT]),
                reads=["xin"], writes=[("XRL", b)], dma=True)
            P.op("vector", lambda v, ps=ps, xrl=xrl: v.tensor_add(OTMP[:], ps[:, :], xrl),
                 reads=[("ps", pi), ("XRL", b)], writes=["OTMP"])
            P.op("gpsimd", lambda g, dj=dj, qt=qt: g.dma_start(
                out=xout[dj * 128:(dj + 1) * 128, qt * TT:(qt + 1) * TT], in_=OTMP[:]),
                reads=["OTMP"], writes=[("xout", dj, qt)], dma=True)

    P.barrier_all_dma_done("gpsimd")
    with nc.Block() as block:
        P.finalize(block)
    es.close()
    return nc



def build_l0_attn_b():
    nc = bass.Bass("TRN2", target_bir_lowering=False)
    es = ExitStack()
    xin = nc.dram_tensor("xin", [D, T], F32, kind="ExternalInput").ap()
    wv = nc.dram_tensor("wv", [D, D], F32, kind="ExternalInput").ap()
    bv = nc.dram_tensor("bv", [128, NCH], F32, kind="ExternalInput").ap()
    onesC = nc.dram_tensor("onesC", [128, 128], BF16, kind="ExternalInput").ap()
    xout = nc.dram_tensor("xout", [D, T], F32, kind="ExternalOutput").ap()

    sb = lambda name, shape, dt: es.enter_context(nc.sbuf_tensor(name, shape, dt))
    H = {}
    H["XT"] = sb("XT", [128, T], F32)
    H["XB"] = sb("XB", [128, T], BF16)
    H["XSQB"] = sb("XSQB", [128, T], BF16)
    H["ROWS"] = sb("ROWS", [1, 3 * T + 520], F32)
    H["ROWSBF"] = sb("ROWSBF", [1, TT], BF16)
    H["MBB"] = sb("MBB", [128, T], BF16)
    H["RBB"] = sb("RBB", [128, T], BF16)
    ONES = sb("ONESs", [128, 128], BF16)
    H["ONES"] = ONES
    Y = sb("Y", [128, NCH * T], BF16)
    H["WST"] = sb("WST", [128, 1024], F32)
    WBF = sb("WBF", [128, NCH * D], BF16)
    BIAS = sb("BIAS", [128, NCH], F32)
    OTMP = sb("OTMP", [128, TT], F32)
    XRL = sb("XRL", [128, 2 * TT], F32)
    PS = [es.enter_context(nc.psum_tensor(f"ps{i}", [128, 512], F32)) for i in range(8)]
    H["PS"] = PS

    P = Prog(nc, es)
    P.op("sync", lambda s: s.dma_start(out=ONES[:], in_=onesC), writes=["ONES"], dma=True)
    P.op("sync", lambda s: s.dma_start(out=BIAS[:], in_=bv), writes=["BIAS"], dma=True)

    emit_ln(P, H, xin, "xin", Y, "Y")
    emit_load_weight_bf16(P, H, wv, "wv", WBF, "W", D, D)

    for dj in range(NCH):
        for tt in range(NTT):
            pi = (dj * NTT + tt) % 2
            ps = PS[pi]
            for ci in range(NCH):
                P.op("tensor", lambda t, ps=ps, ci=ci, dj=dj, tt=tt: t.matmul(
                    ps[:, :], WBF[:, ci * D + dj * 128: ci * D + (dj + 1) * 128],
                    Y[:, ci * T + tt * TT: ci * T + (tt + 1) * TT],
                    start=(ci == 0), stop=(ci == NCH - 1)),
                    reads=[("W", ci), ("Y", ci)], writes=[("ps", pi)])
            b = (dj * NTT + tt) % 2
            xrl = XRL[:, b * TT:(b + 1) * TT]
            P.op("sync", lambda s, dj=dj, tt=tt, xrl=xrl: s.dma_start(
                out=xrl, in_=xin[dj * 128:(dj + 1) * 128, tt * TT:(tt + 1) * TT]),
                reads=["xin"], writes=[("XRL", b)], dma=True)
            P.op("scalar", lambda sc, ps=ps, dj=dj: sc.activation(
                OTMP[:], ps[:, :], AF.Identity, bias=BIAS[:, dj:dj + 1]),
                reads=[("ps", pi), "BIAS"], writes=["OTMP"])
            P.op("vector", lambda v, xrl=xrl: v.tensor_add(OTMP[:], OTMP[:], xrl),
                 reads=["OTMP", ("XRL", b)], writes=["OTMP"])
            P.op("gpsimd", lambda g, dj=dj, tt=tt: g.dma_start(
                out=xout[dj * 128:(dj + 1) * 128, tt * TT:(tt + 1) * TT], in_=OTMP[:]),
                reads=["OTMP"], writes=[("xout", dj, tt)], dma=True)

    P.barrier_all_dma_done("gpsimd")
    with nc.Block() as block:
        P.finalize(block)
    es.close()
    return nc



def _runs(cols):
    runs = []
    i = 0
    while i < len(cols):
        j = i
        if i + 1 < len(cols):
            step = cols[i + 1] - cols[i]
            while j + 1 < len(cols) and cols[j + 1] - cols[j] == step:
                j += 1
        runs.append((cols[i], step if j > i else 1, j - i + 1))
        i = j + 1
    return runs



def emit_ln_sbuf(P, H, SRC, skey, nch, width, nfeat, gain=None, gkey=None,
                 bias=None, bkey=None):
    PS, ones, RBF = H["PS"], H["ONES"], H["ROWSBF"]
    XSQB, MBB, RBB = H["XSQB"], H["MBB"], H["RBB"]
    ntt = width // TT
    for c in range(nch):
        sl = SRC[:, c * width:(c + 1) * width]
        P.op("scalar", lambda sc, sl=sl: sc.activation(XSQB[:, 0:width], sl, AF.Square),
             reads=[(skey, c)], writes=["XSQB"])
        for tt in range(ntt):
            tsl = slice(tt * TT, (tt + 1) * TT)
            P.op("tensor", lambda t, sl=sl, tsl=tsl, tt=tt, c=c: t.matmul(
                PS[tt][0:1, :], ones[:, 0:1], sl[:, tsl], start=(c == 0), stop=(c == nch - 1)),
                reads=[(skey, c), "ONES"], writes=[("ps", tt)])
            P.op("tensor", lambda t, tsl=tsl, tt=tt, c=c: t.matmul(
                PS[4 + tt][0:1, :], ones[:, 0:1], XSQB[:, tsl], start=(c == 0), stop=(c == nch - 1)),
                reads=["XSQB", "ONES"], writes=[("ps", 4 + tt)])
    for tt in range(ntt):
        sl = slice(tt * TT, (tt + 1) * TT)
        P.op("vector", lambda v, tt=tt: v.tensor_scalar_mul(RBF[0:1, 0:TT], PS[tt][0:1, :], 1.0 / nfeat),
             reads=[("ps", tt)], writes=["rowsbf"])
        P.op("tensor", lambda t, tt=tt: t.matmul(PS[tt][:, :], ones[0:1, :], RBF[0:1, 0:TT],
                                                 start=True, stop=True),
             reads=["rowsbf", "ONES"], writes=[("ps", tt)])
        P.op("vector", lambda v, sl=sl, tt=tt: v.tensor_copy(MBB[:, sl], PS[tt][:, :]),
             reads=[("ps", tt)], writes=["MBB"])
        P.op("vector", lambda v, tt=tt: v.tensor_scalar_mul(RBF[0:1, 0:TT], PS[4 + tt][0:1, :], 1.0 / nfeat),
             reads=[("ps", 4 + tt)], writes=["rowsbf"])
        P.op("tensor", lambda t, tt=tt: t.matmul(PS[4 + tt][:, :], ones[0:1, :], RBF[0:1, 0:TT],
                                                 start=True, stop=True),
             reads=["rowsbf", "ONES"], writes=[("ps", 4 + tt)])
        P.op("vector", lambda v, sl=sl, tt=tt: v.tensor_copy(XSQB[:, sl], PS[4 + tt][:, :]),
             reads=[("ps", 4 + tt)], writes=["XSQB"])
    P.op("vector", lambda v: v.tensor_mul(RBB[:, 0:width], MBB[:, 0:width], MBB[:, 0:width]),
         reads=["MBB"], writes=["RBB"])
    P.op("vector", lambda v: v.tensor_sub(RBB[:, 0:width], XSQB[:, 0:width], RBB[:, 0:width]),
         reads=["XSQB", "RBB"], writes=["RBB"])
    P.op("vector", lambda v: v.tensor_scalar_add(RBB[:, 0:width], RBB[:, 0:width], 1e-5),
         reads=["RBB"], writes=["RBB"])
    P.op("scalar", lambda sc: sc.activation(RBB[:, 0:width], RBB[:, 0:width], AF.Ln),
         reads=["RBB"], writes=["RBB"])
    P.op("scalar", lambda sc: sc.activation(RBB[:, 0:width], RBB[:, 0:width], AF.Exp, scale=-0.5),
         reads=["RBB"], writes=["RBB"])
    for c in range(nch):
        sl = SRC[:, c * width:(c + 1) * width]
        P.op("vector", lambda v, sl=sl: v.tensor_sub(sl, sl, MBB[:, 0:width]),
             reads=[(skey, c), "MBB"], writes=[(skey, c)])
        P.op("vector", lambda v, sl=sl: v.tensor_mul(sl, sl, RBB[:, 0:width]),
             reads=[(skey, c), "RBB"], writes=[(skey, c)])
        if gain is not None:
            P.op("vector", lambda v, sl=sl, c=c: v.tensor_scalar(
                sl, sl, gain[:, c:c + 1], bias[:, c:c + 1], ALU.mult, op1=ALU.add),
                reads=[(skey, c), gkey, bkey], writes=[(skey, c)])



def build_l0_ffn(pattern, debug=False):
    m = len(pattern)
    assert m % 512 == 0, m
    nmt = m // 128
    nmh = m // 512
    nc = bass.Bass("TRN2", target_bir_lowering=False)
    es = ExitStack()
    dram = lambda n, s, dt=F32: nc.dram_tensor(n, s, dt, kind="ExternalInput").ap()
    xin = dram("xin", [D, T])
    sw1t = dram("sw1t", [D, 512])
    sw2t = dram("sw2t", [D, 512])
    sw3t = dram("sw3t", [512, D])
    sb1c = dram("sb1c", [128, 4])
    sb2c = dram("sb2c", [128, 4])
    sb3c = dram("sb3c", [128, NCH])
    slng = dram("slng", [128, 4])
    slnb = dram("slnb", [128, 4])
    g9 = dram("g9", [D, 9])
    big9 = dram("big9", [128, 9])
    expb = dram("expb", [128, 8])
    ew1t = dram("ew1t", [8, D, 512])
    ew2t = dram("ew2t", [8, D, 512])
    ew3t = dram("ew3t", [8, 512, D])
    eb1c = dram("eb1c", [128, 32])
    eb2c = dram("eb2c", [128, 32])
    eb3r = dram("eb3r", [8, D])
    selg = dram("selg", [128, 4])
    selb = dram("selb", [128, 4])
    mskt = dram("mskt", [128, nmt])
    onesC = dram("onesC", [128, 128], BF16)
    identC = dram("identC", [128, 128], BF16)
    ehotC = dram("ehotC", [9, 9 * 128], BF16)
    xout = nc.dram_tensor("xout", [D, T], F32, kind="ExternalOutput").ap()
    if debug:
        dGLpre = nc.dram_tensor("dGLpre", [128, 9 * (m // 128)], F32, kind="ExternalOutput").ap()
        dGS = nc.dram_tensor("dGS", [128, 16], F32, kind="ExternalOutput").ap()
        dZH = nc.dram_tensor("dZH", [128, 32 * 512], F32, kind="ExternalOutput").ap()
        dWC9 = nc.dram_tensor("dWC9", [128, 9], F32, kind="ExternalOutput").ap()
        dG9H = nc.dram_tensor("dG9H", [128, NCH * 9], F32, kind="ExternalOutput").ap()
        dGL = nc.dram_tensor("dGL", [128, 9 * (m // 128)], F32, kind="ExternalOutput").ap()
        dSEL9 = nc.dram_tensor("dSEL9", [9, m], F32, kind="ExternalOutput").ap()
        dEXP = nc.dram_tensor("dEXP", [128, NCH * m], F32, kind="ExternalOutput").ap()
        dYMH = nc.dram_tensor("dYMH", [128, NCH * m], F32, kind="ExternalOutput").ap()

    sb = lambda name, shape, dt: es.enter_context(nc.sbuf_tensor(name, shape, dt))
    H = {}
    H["XT"] = sb("XT", [128, T], F32)
    H["XB"] = sb("XB", [128, T], BF16)
    H["XSQB"] = sb("XSQB", [128, T], BF16)
    H["ROWS"] = sb("ROWS", [1, 3 * T + 520], F32)
    H["ROWSBF"] = sb("ROWSBF", [1, TT], BF16)
    H["MBB"] = sb("MBB", [128, T], BF16)
    H["RBB"] = sb("RBB", [128, T], BF16)
    ONES = sb("ONESs", [128, 128], BF16); H["ONES"] = ONES
    IDENT = sb("IDENT", [128, 128], BF16)
    EHOT = sb("EHOT", [9, 9 * 128], BF16)
    YB = sb("YB", [128, NCH * T], BF16)
    Y = YB[:, :]
    ZH = YB[:, 0:32 * 512]
    H["WST"] = sb("WST", [128, 1024], F32)
    WBF = sb("WBF", [128, NCH * D], BF16)
    SH = sb("SH", [128, 4 * T], BF16)
    SX = sb("SX", [128, TT], BF16)
    HX = sb("HX", [128, TT], BF16)
    YMH = sb("YMH", [128, NCH * m], BF16)
    YML = sb("YML", [128, NCH * m], BF16)
    EXPOUT = YML
    SEL9 = sb("SEL9", [9, m], BF16)
    GL = sb("GL", [128, 9 * nmt], F32)
    GS = sb("GS", [128, 16], F32)
    GS2 = sb("GS2", [128, 8], F32)
    WC9 = sb("WC9", [128, 9], BF16)
    MSK = sb("MSK", [128, nmt], F32)
    WCBB = sb("WCBB", [128, 512], BF16)
    G9H = sb("G9H", [128, NCH * 9], BF16)
    G9L = sb("G9L", [128, NCH * 9], BF16)
    BIG9 = sb("BIG9", [128, 9], F32)
    EXPB = sb("EXPB", [128, 8], F32)
    EB1 = sb("EB1", [128, 32], F32)
    EB2 = sb("EB2", [128, 32], F32)
    EB3B = sb("EB3B", [8, D], BF16)
    SB1 = sb("SB1", [128, 4], F32)
    SB2 = sb("SB2", [128, 4], F32)
    SB3 = sb("SB3", [128, NCH], F32)
    SLNG = sb("SLNG", [128, 4], F32)
    SLNB = sb("SLNB", [128, 4], F32)
    SELG = sb("SELG", [128, 4], F32)
    SELB = sb("SELB", [128, 4], F32)
    XM32 = sb("XM32", [128, 1024], F32)
    XM32b = sb("XM32b", [128, 1024], F32)
    OT1 = sb("OT1", [128, TT], F32)
    OT2 = sb("OT2", [128, TT], F32)
    GT1 = sb("GT1", [128, TT], F32)
    GT2 = sb("GT2", [128, TT], F32)
    XRL = sb("XRL", [128, 2 * TT], F32)
    MROW = H["ROWS"][0:1, 2 * T:2 * T + 1024]
    MBG = sb("MBG", [128, m], F32)
    RBG = sb("RBG", [128, m], F32)
    SW1B = sb("SW1B", [128, m], F32)
    PS = [es.enter_context(nc.psum_tensor(f"ps{i}", [128, 512], F32)) for i in range(8)]
    H["PS"] = PS

    P = Prog(nc, es)
    ld = lambda dst, srcap, key: P.op(
        "sync", lambda s, dst=dst, srcap=srcap: s.dma_start(out=dst, in_=srcap),
        writes=[key], dma=True)
    ld(ONES[:], onesC, "ONES")
    ld(IDENT[:], identC, "IDENT")
    ld(EHOT[:], ehotC, "EHOT")
    ld(BIG9[:], big9, "BIG9")
    ld(EXPB[:], expb, "EXPB")
    ld(EB1[:], eb1c, "EB1")
    ld(EB2[:], eb2c, "EB2")
    ld(SB1[:], sb1c, "SB1")
    ld(SB2[:], sb2c, "SB2")
    ld(SB3[:], sb3c, "SB3")
    ld(SLNG[:], slng, "SLNG")
    ld(SLNB[:], slnb, "SLNB")
    ld(SELG[:], selg, "SELG")
    ld(SELB[:], selb, "SELB")
    ld(MSK[:], mskt, "MSK")
    P.op("sync", lambda s: s.dma_start(out=XM32[0:8, 0:D], in_=eb3r), writes=["XM32"], dma=True)
    P.op("vector", lambda v: v.tensor_copy(EB3B[:], XM32[0:8, 0:D]), reads=["XM32"], writes=["EB3B"])

    emit_ln(P, H, xin, "xin", Y, "Y")

    for srctile, skey2, dstt, dkey in ((H["MBB"], "MBB", MBG, "MBG"),
                                       (H["RBB"], "RBB", RBG, "RBG")):
        off = 0
        for (st, sp, n) in _runs(pattern):
            P.op("vector", lambda v, st=st, sp=sp, n=n, off=off, srctile=srctile,
                 dstt=dstt: v.tensor_copy(
                dstt[:, off:off + n],
                srctile[:, st:st + sp * n:sp] if sp > 1 else srctile[:, st:st + n]),
                reads=[skey2], writes=[dkey])
            off += n
    for c in range(NCH):
        P.op("sync", lambda s, c=c: s.dma_start(out=H["XT"][:], in_=xin[c * 128:(c + 1) * 128, :]),
             reads=["xin"], writes=["XT"], dma=True)
        off = 0
        for (st, sp, n) in _runs(pattern):
            P.op("vector", lambda v, st=st, sp=sp, n=n, off=off: v.tensor_copy(
                XM32[:, off:off + n],
                H["XT"][:, st:st + sp * n:sp] if sp > 1 else H["XT"][:, st:st + n]),
                reads=["XT"], writes=["XM32"])
            off += n
        P.op("vector", lambda v: v.tensor_sub(XM32[:, 0:m], XM32[:, 0:m], MBG[:, 0:m]),
             reads=["XM32", "MBG"], writes=["XM32"])
        P.op("vector", lambda v: v.tensor_mul(XM32[:, 0:m], XM32[:, 0:m], RBG[:, 0:m]),
             reads=["XM32", "RBG"], writes=["XM32"])
        P.op("vector", lambda v, c=c: v.tensor_copy(YMH[:, c * m:(c + 1) * m], XM32[:, 0:m]),
             reads=["XM32"], writes=[("YMH", c)])
        P.op("vector", lambda v, c=c: v.tensor_copy(XM32b[:, 0:m], YMH[:, c * m:(c + 1) * m]),
             reads=[("YMH", c)], writes=["XM32b"])
        P.op("vector", lambda v: v.tensor_sub(XM32[:, 0:m], XM32[:, 0:m], XM32b[:, 0:m]),
             reads=["XM32", "XM32b"], writes=["XM32"])
        P.op("vector", lambda v, c=c: v.tensor_copy(YML[:, c * m:(c + 1) * m], XM32[:, 0:m]),
             reads=["XM32"], writes=[("YML", c)])

    P.op("sync", lambda s: s.dma_start(
        out=XM32[:, 0:NCH * 9].rearrange("p (c n) -> p c n", c=NCH),
        in_=g9.rearrange("(c p) n -> p c n", p=128)),
        writes=["XM32"], dma=True)
    P.op("vector", lambda v: v.tensor_copy(G9H[:], XM32[:, 0:NCH * 9]), reads=["XM32"], writes=["G9H"])
    P.op("vector", lambda v: v.tensor_copy(XM32b[:, 0:NCH * 9], G9H[:]), reads=["G9H"], writes=["XM32b"])
    P.op("vector", lambda v: v.tensor_sub(XM32[:, 0:NCH * 9], XM32[:, 0:NCH * 9], XM32b[:, 0:NCH * 9]),
         reads=["XM32", "XM32b"], writes=["XM32"])
    P.op("vector", lambda v: v.tensor_copy(G9L[:], XM32[:, 0:NCH * 9]), reads=["XM32"], writes=["G9L"])
    for mt in range(nmt):
        pi = mt % 2
        ps = PS[pi]
        terms = [("YMH", G9H, "G9H"), ("YMH", G9L, "G9L"), ("YML", G9H, "G9H")]
        nsteps = NCH * len(terms)
        k = 0
        for c in range(NCH):
            for (Lk, R, Rk) in terms:
                L = YMH if Lk == "YMH" else YML
                P.op("tensor", lambda t, ps=ps, c=c, mt=mt, L=L, R=R, k=k: t.matmul(
                    ps[:, 0:9], L[:, c * m + mt * 128: c * m + (mt + 1) * 128],
                    R[:, c * 9:(c + 1) * 9], start=(k == 0), stop=(k == nsteps - 1)),
                    reads=[(Lk, c), Rk], writes=[("ps", pi)])
                k += 1
        P.op("vector", lambda v, ps=ps, mt=mt: v.tensor_copy(GL[:, mt * 9:(mt + 1) * 9], ps[:, 0:9]),
             reads=[("ps", pi)], writes=["GL"])
        if debug:
            P.op("gpsimd", lambda g, mt=mt: g.dma_start(
                out=dGLpre[:, mt * 9:(mt + 1) * 9], in_=GL[:, mt * 9:(mt + 1) * 9]),
                reads=["GL"], writes=[("dump",)], dma=True)
        P.op("vector", lambda v, mt=mt: v.tensor_add(
            GL[:, mt * 9:(mt + 1) * 9], GL[:, mt * 9:(mt + 1) * 9], BIG9[:]),
            reads=["GL", "BIG9"], writes=["GL"])
    P.op("scalar", lambda sc: sc.activation(GL[:], GL[:], AF.Sigmoid), reads=["GL"], writes=["GL"])
    for mt in range(nmt):
        g = GL[:, mt * 9:mt * 9 + 8]
        shs = GL[:, mt * 9 + 8:mt * 9 + 9]
        oh = GS[:, 0:8]
        mx = GS[:, 8:9]
        t1 = GS[:, 9:10]
        swc = GS[:, 10:11]
        ewc = GS[:, 11:12]
        P.op("vector", lambda v, g=g: v.tensor_add(oh, g, EXPB[:]), reads=["GL", "EXPB"], writes=["GS"])
        P.op("vector", lambda v: v.reduce_max(mx, oh, axis=AX.X), reads=["GS"], writes=["GS"])
        P.op("vector", lambda v: v.tensor_scalar(oh, oh, mx, None, ALU.is_ge),
             reads=["GS"], writes=["GS"])
        P.op("vector", lambda v, g=g: v.tensor_mul(GS2[:], oh, g), reads=["GS", "GL"], writes=["GS2"])
        P.op("vector", lambda v: v.reduce_sum(t1, GS2[:], axis=AX.X), reads=["GS2"], writes=["GS"])
        P.op("vector", lambda v, shs=shs: v.tensor_sub(swc, shs, t1), reads=["GS", "GL"], writes=["GS"])
        P.op("scalar", lambda sc: sc.activation(swc, swc, AF.Sigmoid), reads=["GS"], writes=["GS"])
        P.op("vector", lambda v: v.tensor_scalar(ewc, swc, -1.0, 1.0, ALU.mult, op1=ALU.add),
             reads=["GS"], writes=["GS"])
        P.op("vector", lambda v: v.tensor_scalar(WC9[:, 0:8], GS[:, 0:8], ewc, None, ALU.mult),
             reads=["GS"], writes=["WC9"])
        P.op("vector", lambda v: v.tensor_scalar(WC9[:, 8:9], ewc, -1.0, None, ALU.mult),
             reads=["GS"], writes=["WC9"])
        P.op("vector", lambda v, mt=mt: v.tensor_scalar(WC9[:], WC9[:], MSK[:, mt:mt + 1], None, ALU.mult),
             reads=["WC9", "MSK"], writes=["WC9"])
        if debug and mt == 0:
            P.op("gpsimd", lambda g: g.dma_start(out=dGS, in_=GS[:]),
                 reads=["GS"], writes=[("dump",)], dma=True)
            P.op("gpsimd", lambda g: g.dma_start(out=dWC9, in_=XM32[:, 0:9]),
                 reads=["XM32"], writes=[("dump",)], dma=True)
            P.op("vector", lambda v: v.tensor_copy(XM32[:, 0:9], WC9[:]),
                 reads=["WC9"], writes=["XM32"])
        pi = 2 + mt % 2
        P.op("tensor", lambda t, pi=pi: t.matmul(PS[pi][0:9, 0:128], WC9[:], IDENT[:],
                                                 start=True, stop=True),
             reads=["WC9", "IDENT"], writes=[("ps", pi)])
        P.op("vector", lambda v, mt=mt, pi=pi: v.tensor_copy(
            SEL9[:, mt * 128:(mt + 1) * 128], PS[pi][0:9, 0:128]),
            reads=[("ps", pi)], writes=["SEL9"])

    emit_load_weight_bf16(P, H, sw1t, "sw1t", WBF, "W", D, 512)
    for dj in range(4):
        for tt in range(NTT):
            pi = (dj * NTT + tt) % 2
            ps = PS[pi]
            for ci in range(NCH):
                P.op("tensor", lambda t, ps=ps, ci=ci, dj=dj, tt=tt: t.matmul(
                    ps[:, :], WBF[:, ci * 512 + dj * 128: ci * 512 + (dj + 1) * 128],
                    Y[:, ci * T + tt * TT: ci * T + (tt + 1) * TT],
                    start=(ci == 0), stop=(ci == NCH - 1)),
                    reads=[("W", ci), ("Y", ci)], writes=[("ps", pi)])
            P.op("scalar", lambda sc, ps=ps, dj=dj, tt=tt: sc.activation(
                SH[:, dj * T + tt * TT: dj * T + (tt + 1) * TT], ps[:, :], AF.Silu,
                bias=SB1[:, dj:dj + 1]),
                reads=[("ps", pi), "SB1"], writes=[("SH", dj)])
    emit_load_weight_bf16(P, H, sw2t, "sw2t", WBF, "W", D, 512)
    for dj in range(4):
        for tt in range(NTT):
            pi = (dj * NTT + tt) % 2
            ps = PS[pi]
            for ci in range(NCH):
                P.op("tensor", lambda t, ps=ps, ci=ci, dj=dj, tt=tt: t.matmul(
                    ps[:, :], WBF[:, ci * 512 + dj * 128: ci * 512 + (dj + 1) * 128],
                    Y[:, ci * T + tt * TT: ci * T + (tt + 1) * TT],
                    start=(ci == 0), stop=(ci == NCH - 1)),
                    reads=[("W", ci), ("Y", ci)], writes=[("ps", pi)])
            P.op("scalar", lambda sc, ps=ps, dj=dj, tt=tt: sc.activation(
                HX[:], ps[:, :], AF.Identity, bias=SB2[:, dj:dj + 1]),
                reads=[("ps", pi), "SB2"], writes=["HX"])
            P.op("vector", lambda v, dj=dj, tt=tt: v.tensor_mul(
                SH[:, dj * T + tt * TT: dj * T + (tt + 1) * TT],
                SH[:, dj * T + tt * TT: dj * T + (tt + 1) * TT], HX[:]),
                reads=[("SH", dj), "HX"], writes=[("SH", dj)])
    emit_ln_sbuf(P, H, SH, "SH", 4, T, 512, gain=SLNG, gkey="SLNG", bias=SLNB, bkey="SLNB")

    P.fence("scalar", reads=[("Y", c) for c in range(NCH)],
            writes=[("ZH", e, dj) for e in range(8) for dj in range(4)])

    P.fence("vector", reads=[("YML", c) for c in range(NCH)],
            writes=[("EXPOUT", dj) for dj in range(NCH)])
    for mh in range(nmh):
        mo = mh * 512
        for e in range(8):
            emit_load_weight_bf16(P, H, ew1t[e], ("ew1", e), WBF, "W", D, 512)
            for dj in range(4):
                pi = dj % 2
                ps = PS[pi]
                for ci in range(NCH):
                    P.op("tensor", lambda t, ps=ps, ci=ci, dj=dj, mo=mo: t.matmul(
                        ps[:, :], WBF[:, ci * 512 + dj * 128: ci * 512 + (dj + 1) * 128],
                        YMH[:, ci * m + mo: ci * m + mo + 512],
                        start=(ci == 0), stop=(ci == NCH - 1)),
                        reads=[("W", ci), ("YMH", ci)], writes=[("ps", pi)])
                P.op("scalar", lambda sc, ps=ps, dj=dj, e=e: sc.activation(
                    SX[:], ps[:, :], AF.Silu, bias=EB1[:, e * 4 + dj:e * 4 + dj + 1]),
                    reads=[("ps", pi), "EB1"], writes=["SX"])
                P.op("vector", lambda v, e=e, dj=dj: v.tensor_copy(
                    ZH[:, (e * 4 + dj) * 512:(e * 4 + dj + 1) * 512], SX[:]),
                    reads=["SX"], writes=[("ZH", e, dj)])
            emit_load_weight_bf16(P, H, ew2t[e], ("ew2", e), WBF, "W", D, 512)
            for dj in range(4):
                pi = dj % 2
                ps = PS[pi]
                for ci in range(NCH):
                    P.op("tensor", lambda t, ps=ps, ci=ci, dj=dj, mo=mo: t.matmul(
                        ps[:, :], WBF[:, ci * 512 + dj * 128: ci * 512 + (dj + 1) * 128],
                        YMH[:, ci * m + mo: ci * m + mo + 512],
                        start=(ci == 0), stop=(ci == NCH - 1)),
                        reads=[("W", ci), ("YMH", ci)], writes=[("ps", pi)])
                P.op("scalar", lambda sc, ps=ps, dj=dj, e=e: sc.activation(
                    HX[:], ps[:, :], AF.Identity, bias=EB2[:, e * 4 + dj:e * 4 + dj + 1]),
                    reads=[("ps", pi), "EB2"], writes=["HX"])
                P.op("vector", lambda v, e=e, dj=dj: v.tensor_mul(
                    ZH[:, (e * 4 + dj) * 512:(e * 4 + dj + 1) * 512],
                    ZH[:, (e * 4 + dj) * 512:(e * 4 + dj + 1) * 512], HX[:]),
                    reads=[("ZH", e, dj), "HX"], writes=[("ZH", e, dj)])
            for dj in range(4):
                P.op("tensor", lambda t, e=e, dj=dj: t.matmul(
                    PS[4][0:1, :], ONES[:, 0:1], ZH[:, (e * 4 + dj) * 512:(e * 4 + dj + 1) * 512],
                    start=(dj == 0), stop=(dj == 3)),
                    reads=[("ZH", e, dj), "ONES"], writes=[("ps", 4)])
                P.op("scalar", lambda sc, e=e, dj=dj: sc.activation(
                    SX[:], ZH[:, (e * 4 + dj) * 512:(e * 4 + dj + 1) * 512], AF.Square),
                    reads=[("ZH", e, dj)], writes=["SX"])
                P.op("tensor", lambda t, dj=dj: t.matmul(
                    PS[5][0:1, :], ONES[:, 0:1], SX[:], start=(dj == 0), stop=(dj == 3)),
                    reads=["SX", "ONES"], writes=[("ps", 5)])
            P.op("vector", lambda v: v.tensor_scalar_mul(H["ROWSBF"][0:1, 0:TT], PS[4][0:1, :], 1.0 / 512),
                 reads=[("ps", 4)], writes=["rowsbf"])
            P.op("tensor", lambda t: t.matmul(PS[6][:, :], ONES[0:1, :], H["ROWSBF"][0:1, 0:TT],
                                              start=True, stop=True),
                 reads=["rowsbf", "ONES"], writes=[("ps", 6)])
            P.op("vector", lambda v: v.tensor_copy(GT1[:], PS[6][:, :]), reads=[("ps", 6)], writes=["GT1"])
            P.op("vector", lambda v: v.tensor_scalar_mul(H["ROWSBF"][0:1, 0:TT], PS[5][0:1, :], 1.0 / 512),
                 reads=[("ps", 5)], writes=["rowsbf"])
            P.op("tensor", lambda t: t.matmul(PS[6][:, :], ONES[0:1, :], H["ROWSBF"][0:1, 0:TT],
                                              start=True, stop=True),
                 reads=["rowsbf", "ONES"], writes=[("ps", 6)])
            P.op("vector", lambda v: v.tensor_copy(GT2[:], PS[6][:, :]), reads=[("ps", 6)], writes=["GT2"])
            P.op("vector", lambda v: v.tensor_mul(OT1[:], GT1[:], GT1[:]), reads=["GT1"], writes=["OT1"])
            P.op("vector", lambda v: v.tensor_sub(GT2[:], GT2[:], OT1[:]), reads=["GT2", "OT1"], writes=["GT2"])
            P.op("vector", lambda v: v.tensor_scalar_add(GT2[:], GT2[:], 1e-5), reads=["GT2"], writes=["GT2"])
            P.op("scalar", lambda sc: sc.activation(GT2[:], GT2[:], AF.Ln), reads=["GT2"], writes=["GT2"])
            P.op("scalar", lambda sc: sc.activation(GT2[:], GT2[:], AF.Exp, scale=-0.5),
                 reads=["GT2"], writes=["GT2"])
            P.op("tensor", lambda t, e=e, mo=mo: t.matmul(
                PS[6][:, :], EHOT[:, e * 128:(e + 1) * 128], SEL9[:, mo:mo + 512],
                start=True, stop=True),
                reads=["EHOT", "SEL9"], writes=[("ps", 6)])
            P.op("vector", lambda v: v.tensor_copy(WCBB[:], PS[6][:, :]), reads=[("ps", 6)], writes=["WCBB"])
            for dj in range(4):
                zsl = ZH[:, (e * 4 + dj) * 512:(e * 4 + dj + 1) * 512]
                P.op("vector", lambda v, zsl=zsl: v.tensor_sub(zsl, zsl, GT1[:]),
                     reads=[("ZH", e, dj), "GT1"], writes=[("ZH", e, dj)])
                P.op("vector", lambda v, zsl=zsl: v.tensor_mul(zsl, zsl, GT2[:]),
                     reads=[("ZH", e, dj), "GT2"], writes=[("ZH", e, dj)])
                P.op("vector", lambda v, zsl=zsl, dj=dj: v.tensor_scalar(
                    zsl, zsl, SELG[:, dj:dj + 1], SELB[:, dj:dj + 1], ALU.mult, op1=ALU.add),
                    reads=[("ZH", e, dj), "SELG", "SELB"], writes=[("ZH", e, dj)])
                P.op("vector", lambda v, zsl=zsl: v.tensor_mul(zsl, zsl, WCBB[:]),
                     reads=[("ZH", e, dj), "WCBB"], writes=[("ZH", e, dj)])
        if debug and mh == 0:
            for e in range(8):
                for dj in range(4):
                    P.op("gpsimd", lambda g, e=e, dj=dj: g.dma_start(
                        out=dZH[:, (e * 4 + dj) * 512:(e * 4 + dj + 1) * 512],
                        in_=ZH[:, (e * 4 + dj) * 512:(e * 4 + dj + 1) * 512]),
                        reads=[("ZH", e, dj)], writes=[("dump",)], dma=True)
        for e in range(8):
            emit_load_weight_bf16(P, H, ew3t[e], ("ew3", e), WBF, "W", 512, D)
            for dj in range(NCH):
                ps = PS[dj]
                for hc in range(4):
                    P.op("tensor", lambda t, ps=ps, e=e, dj=dj, hc=hc: t.matmul(
                        ps[:, :], WBF[:, hc * D + dj * 128: hc * D + (dj + 1) * 128],
                        ZH[:, (e * 4 + hc) * 512:(e * 4 + hc + 1) * 512],
                        start=(e == 0 and hc == 0), stop=False),
                        reads=[("W", hc), ("ZH", e, hc)], writes=[("ps", dj)])
        for dj in range(NCH):
            ps = PS[dj]
            P.op("tensor", lambda t, ps=ps, dj=dj, mo=mo: t.matmul(
                ps[:, :], EB3B[:, dj * 128:(dj + 1) * 128], SEL9[0:8, mo:mo + 512],
                start=False, stop=True),
                reads=["EB3B", "SEL9"], writes=[("ps", dj)])
            P.op("vector", lambda v, ps=ps, dj=dj, mo=mo: v.tensor_copy(
                EXPOUT[:, dj * m + mo: dj * m + mo + 512], ps[:, :]),
                reads=[("ps", dj)], writes=[("EXPOUT", dj)])

    if debug:
        DBF = sb("DBF", [128, 1024], F32)
        def dump(src_ap, rk, dst, n, npart=128):
            for o in range(0, n, 1024):
                w = min(1024, n - o)
                P.op("vector", lambda v, o=o, w=w: v.tensor_copy(DBF[0:npart, 0:w], src_ap[0:npart, o:o + w]),
                     reads=[rk] if not isinstance(rk, list) else rk, writes=["DBF"])
                P.op("gpsimd", lambda g, o=o, w=w: g.dma_start(out=dst[0:npart, o:o + w], in_=DBF[0:npart, 0:w]),
                     reads=["DBF"], writes=[("dump",)], dma=True)
        dump(GL, "GL", dGL, 9 * nmt)
        dump(G9H, "G9H", dG9H, NCH * 9)
        dump(SEL9, "SEL9", dSEL9, m, npart=9)
        dump(EXPOUT, [("EXPOUT", dj) for dj in range(NCH)], dEXP, NCH * m)
        dump(YMH, [("YMH", c) for c in range(NCH)], dYMH, NCH * m)

    for mh in range(nmh):
        pi = mh % 2
        P.op("tensor", lambda t, mh=mh, pi=pi: t.matmul(
            PS[pi][:, :], EHOT[:, 8 * 128:9 * 128], SEL9[:, mh * 512:(mh + 1) * 512],
            start=True, stop=True),
            reads=["EHOT", "SEL9"], writes=[("ps", pi)])
        P.op("vector", lambda v, mh=mh, pi=pi: v.tensor_copy(
            SW1B[:, mh * 512:(mh + 1) * 512], PS[pi][:, :]),
            reads=[("ps", pi)], writes=["SW1B"])

    emit_load_weight_bf16(P, H, sw3t, "sw3t", WBF, "W", 512, D)
    pat = list(pattern)
    for dj in range(NCH):
        for tt in range(NTT):
            pi = (dj * NTT + tt) % 2
            ps = PS[pi]
            for hc in range(4):
                P.op("tensor", lambda t, ps=ps, hc=hc, dj=dj, tt=tt: t.matmul(
                    ps[:, :], WBF[:, hc * D + dj * 128: hc * D + (dj + 1) * 128],
                    SH[:, hc * T + tt * TT: hc * T + (tt + 1) * TT],
                    start=(hc == 0), stop=(hc == 3)),
                    reads=[("W", hc), ("SH", hc)], writes=[("ps", pi)])
            P.op("scalar", lambda sc, ps=ps, dj=dj: sc.activation(
                OT1[:], ps[:, :], AF.Identity, bias=SB3[:, dj:dj + 1]),
                reads=[("ps", pi), "SB3"], writes=["OT1"])
            b = (dj * NTT + tt) % 2
            xrl = XRL[:, b * TT:(b + 1) * TT]
            P.op("sync", lambda s, dj=dj, tt=tt, xrl=xrl: s.dma_start(
                out=xrl, in_=xin[dj * 128:(dj + 1) * 128, tt * TT:(tt + 1) * TT]),
                reads=["xin"], writes=[("XRL", b)], dma=True)
            P.op("vector", lambda v, xrl=xrl: v.tensor_add(OT2[:], OT1[:], xrl),
                 reads=["OT1", ("XRL", b)], writes=["OT2"])
            lo = tt * TT
            hi = (tt + 1) * TT
            mlo = int(np.searchsorted(pat, lo))
            mhi = int(np.searchsorted(pat, hi))
            ncm = mhi - mlo
            if ncm > 0:
                local = [c - lo for c in pat[mlo:mhi]]
                off = 0
                for (st, sp, n) in _runs(local):
                    P.op("vector", lambda v, st=st, sp=sp, n=n, off=off: v.tensor_copy(
                        GT1[:, off:off + n],
                        OT1[:, st:st + sp * n:sp] if sp > 1 else OT1[:, st:st + n]),
                        reads=["OT1"], writes=["GT1"])
                    off += n
                P.op("vector", lambda v, mlo=mlo, ncm=ncm: v.tensor_mul(
                    GT1[:, 0:ncm], GT1[:, 0:ncm], SW1B[:, mlo:mlo + ncm]),
                    reads=["GT1", "SW1B"], writes=["GT1"])
                off = 0
                for (st, sp, n) in _runs(local):
                    P.op("vector", lambda v, st=st, sp=sp, n=n, off=off: v.tensor_copy(
                        GT2[:, off:off + n],
                        OT2[:, st:st + sp * n:sp] if sp > 1 else OT2[:, st:st + n]),
                        reads=["OT2"], writes=["GT2"])
                    off += n
                P.op("vector", lambda v, ncm=ncm: v.tensor_add(GT1[:, 0:ncm], GT1[:, 0:ncm],
                                                               GT2[:, 0:ncm]),
                     reads=["GT1", "GT2"], writes=["GT1"])
                P.op("vector", lambda v, dj=dj, mlo=mlo, ncm=ncm: v.tensor_copy(
                    GT2[:, 0:ncm], EXPOUT[:, dj * m + mlo: dj * m + mlo + ncm]),
                    reads=[("EXPOUT", dj)], writes=["GT2"])
                P.op("vector", lambda v, ncm=ncm: v.tensor_add(GT1[:, 0:ncm], GT1[:, 0:ncm],
                                                               GT2[:, 0:ncm]),
                     reads=["GT1", "GT2"], writes=["GT1"])
                off = 0
                for (st, sp, n) in _runs(local):
                    P.op("vector", lambda v, st=st, sp=sp, n=n, off=off: v.tensor_copy(
                        OT2[:, st:st + sp * n:sp] if sp > 1 else OT2[:, st:st + n],
                        GT1[:, off:off + n]),
                        reads=["GT1"], writes=["OT2"])
                    off += n
            P.op("gpsimd", lambda g, dj=dj, tt=tt: g.dma_start(
                out=xout[dj * 128:(dj + 1) * 128, tt * TT:(tt + 1) * TT], in_=OT2[:]),
                reads=["OT2"], writes=[("xout", dj, tt)], dma=True)

    P.barrier_all_dma_done("gpsimd")
    with nc.Block() as block:
        P.finalize(block)
    es.close()
    return nc


def l0_ffn_host_inputs(INP, pattern, ones, ident):
    ehot = np.zeros((9, 9 * 128), dtype=ml_dtypes.bfloat16)
    for e in range(9):
        ehot[e, e * 128:(e + 1) * 128] = 1
    f = np.float32
    d = {
        "sw1t": np.ascontiguousarray(INP["L0_ffn_sw1"].T.astype(f)),
        "sw2t": np.ascontiguousarray(INP["L0_ffn_sw2"].T.astype(f)),
        "sw3t": np.ascontiguousarray(INP["L0_ffn_sw3"].T.astype(f)),
        "sb1c": np.ascontiguousarray(INP["L0_ffn_sb1"].reshape(4, 128).T),
        "sb2c": np.ascontiguousarray(INP["L0_ffn_sb2"].reshape(4, 128).T),
        "sb3c": np.ascontiguousarray(INP["L0_ffn_sb3"].reshape(NCH, 128).T),
        "slng": np.ascontiguousarray(INP["L0_ffn_s_ln_g"].reshape(4, 128).T),
        "slnb": np.ascontiguousarray(INP["L0_ffn_s_ln_b"].reshape(4, 128).T),
        "g9": np.ascontiguousarray(
            np.concatenate([INP["L0_ffn_eg_w"].T, INP["L0_ffn_sg_w"].T], axis=1)),
        "big9": np.tile(np.concatenate([INP["L0_ffn_eg_b"], INP["L0_ffn_sg_b"]]), (128, 1)).astype(f),
        "expb": np.tile(INP["L0_ffn_exp_bias"], (128, 1)).astype(f),
        "ew1t": np.ascontiguousarray(INP["L0_ffn_ew1"].transpose(0, 2, 1)),
        "ew2t": np.ascontiguousarray(INP["L0_ffn_ew2"].transpose(0, 2, 1)),
        "ew3t": np.ascontiguousarray(INP["L0_ffn_ew3"].transpose(0, 2, 1)),
        "eb1c": np.ascontiguousarray(INP["L0_ffn_eb1"].reshape(8, 4, 128).transpose(2, 0, 1).reshape(128, 32)),
        "eb2c": np.ascontiguousarray(INP["L0_ffn_eb2"].reshape(8, 4, 128).transpose(2, 0, 1).reshape(128, 32)),
        "eb3r": np.ascontiguousarray(INP["L0_ffn_eb3"]),
        "selg": np.ascontiguousarray(INP["L0_ffn_sel_ln_g"].reshape(4, 128).T),
        "selb": np.ascontiguousarray(INP["L0_ffn_sel_ln_b"].reshape(4, 128).T),
        "onesC": ones,
        "identC": ident,
        "ehotC": ehot,
    }
    return d


def l0_ffn_masks(INP, pattern):
    ib = np.asarray(INP["L0_ffn_index_b"]).astype(np.int64)
    it = np.asarray(INP["L0_ffn_index_t"]).astype(np.int64)
    pos = {c: i for i, c in enumerate(pattern)}
    nmt = len(pattern) // 128
    masks = []
    for b in range(NCORES):
        mk = np.zeros(len(pattern), np.float32)
        for bb, tt_ in zip(ib, it):
            if bb == b:
                mk[pos[int(tt_)]] = 1.0
        masks.append(np.ascontiguousarray(mk.reshape(nmt, 128).T))
    return masks



def build_l1_attn_b():
    nc = bass.Bass("TRN2", target_bir_lowering=False)
    es = ExitStack()
    dram = lambda n, s, dt=F32: nc.dram_tensor(n, s, dt, kind="ExternalInput").ap()
    xin = dram("xin", [D, T])
    wqt = dram("wqt", [D, D])
    wkt = dram("wkt", [D, D])
    wvt = dram("wvt", [D, D])
    wot = dram("wot", [D, D])
    bq = dram("bq", [128, NCH])
    bk = dram("bk", [128, NCH])
    bvrow = dram("bvrow", [1, D])
    bo = dram("bo", [128, NCH])
    ropeA = dram("ropeA", [128, T])
    ropeB = dram("ropeB", [128, T])
    rt = dram("rt", [128, 128], BF16)
    onesC = dram("onesC", [128, 128], BF16)
    masksC = nc.dram_tensor("masksC", [4, 128, 512], BF16, kind="ExternalInput").ap()
    xout = nc.dram_tensor("xout", [D, T], F32, kind="ExternalOutput").ap()

    sb = lambda name, shape, dt: es.enter_context(nc.sbuf_tensor(name, shape, dt))
    H = {}
    H["XT"] = sb("XT", [128, T], F32)
    SCR = sb("SCR", [128, 8192], BF16)
    H["XB"] = SCR[:, 0:T]
    H["XSQB"] = SCR[:, T:2 * T]
    H["MBB"] = SCR[:, 2 * T:3 * T]
    H["RBB"] = SCR[:, 3 * T:4 * T]
    ET = SCR[:, 0:8192]
    H["ROWS"] = sb("ROWS", [1, 2 * T + 520], F32)
    H["ROWSBF"] = sb("ROWSBF", [1, TT], BF16)
    ONES = sb("ONESs", [128, 128], BF16); H["ONES"] = ONES
    RT = sb("RT", [128, 128], BF16)
    RA = sb("RA", [128, T], BF16)
    RB2 = sb("RB2", [128, T], BF16)
    Y = sb("Y", [128, NCH * T], BF16)
    Q = sb("Q", [128, NCH * T], BF16)
    K = sb("Kt", [128, NCH * T], BF16)
    V = sb("V", [128, 16 * D], BF16)
    H["WST"] = H["XT"][:, 0:1024]
    WBF = sb("WBF", [128, NCH * D], BF16)
    MASKS = WBF[:, 0:2048]
    BQ = sb("BQ", [128, NCH], F32)
    BK = sb("BK", [128, NCH], F32)
    BO = sb("BO", [128, NCH], F32)
    BVR = sb("BVR", [1, D], BF16)
    BVst = sb("BVst", [1, D], F32)
    BA = sb("BA", [128, 1], F32)
    KMB = sb("KMB", [128, 1], F32)
    TMPC = sb("TMPC", [128, TT], F32)
    TMPD = sb("TMPD", [128, TT], BF16)
    ZIB = sb("ZIB", [128, TT], BF16)
    OTMP = sb("OTMP", [128, TT], F32)
    XRL = H["XT"][:, 1024:1536]
    PS = [es.enter_context(nc.psum_tensor(f"ps{i}", [128, 512], F32)) for i in range(8)]
    H["PS"] = PS

    P = Prog(nc, es)
    ld = lambda dst, srcap, key: P.op(
        "sync", lambda s, dst=dst, srcap=srcap: s.dma_start(out=dst, in_=srcap),
        writes=[key], dma=True)
    ld(ONES[:], onesC, "ONES")
    ld(RT[:], rt, "RT")
    ld(BQ[:], bq, "BQ")
    ld(BK[:], bk, "BK")
    ld(BO[:], bo, "BO")
    ld(BVst[:], bvrow, "BVst")
    P.op("vector", lambda v: v.tensor_copy(BVR[:], BVst[:]), reads=["BVst"], writes=["BVR"])
    P.op("sync", lambda s: s.dma_start(out=H["XT"][:], in_=ropeA), reads=[], writes=["XT"], dma=True)
    P.op("vector", lambda v: v.tensor_copy(RA[:], H["XT"][:]), reads=["XT"], writes=["RA"])
    P.op("sync", lambda s: s.dma_start(out=H["XT"][:], in_=ropeB), reads=["XT"], writes=["XT"], dma=True)
    P.op("vector", lambda v: v.tensor_copy(RB2[:], H["XT"][:]), reads=["XT"], writes=["RB2"])

    emit_ln(P, H, xin, "xin", Y, "Y")

    for W, wk_, BT, btk, DST, dkey in ((wqt, "wqt", BQ, "BQ", Q, "Q"), (wkt, "wkt", BK, "BK", K, "K")):
        emit_load_weight_bf16(P, H, W, wk_, WBF, "W", D, D)
        for dj in range(NCH):
            for tt in range(NTT):
                pi = (dj * NTT + tt) % 2
                ps = PS[pi]
                for ci in range(NCH):
                    P.op("tensor", lambda t, ps=ps, ci=ci, dj=dj, tt=tt: t.matmul(
                        ps[:, :], WBF[:, ci * D + dj * 128: ci * D + (dj + 1) * 128],
                        Y[:, ci * T + tt * TT: ci * T + (tt + 1) * TT],
                        start=(ci == 0), stop=(ci == NCH - 1)),
                        reads=[("W", ci), ("Y", ci)], writes=[("ps", pi)])
                P.op("scalar", lambda sc, ps=ps, dj=dj, tt=tt, DST=DST: sc.activation(
                    DST[:, dj * T + tt * TT: dj * T + (tt + 1) * TT], ps[:, :], AF.Identity,
                    bias=BT[:, dj:dj + 1]),
                    reads=[("ps", pi), btk], writes=[(dkey, dj)])
    emit_load_weight_bf16(P, H, wvt, "wvt", WBF, "W", D, D)
    for tc in range(16):
        for dh in range(2):
            pi = (tc * 2 + dh) % 2
            ps = PS[pi]
            for ci in range(NCH):
                P.op("tensor", lambda t, ps=ps, ci=ci, tc=tc, dh=dh: t.matmul(
                    ps[:, :], Y[:, ci * T + tc * 128: ci * T + (tc + 1) * 128],
                    WBF[:, ci * D + dh * 512: ci * D + (dh + 1) * 512],
                    start=(ci == 0), stop=False),
                    reads=[("W", ci), ("Y", ci)], writes=[("ps", pi)])
            P.op("tensor", lambda t, ps=ps, tc=tc, dh=dh: t.matmul(
                ps[:, :], ONES[0:1, tc * 8:tc * 8 + 128] if False else ONES[0:1, :],
                BVR[0:1, dh * 512:(dh + 1) * 512], start=False, stop=True),
                reads=["ONES", "BVR"], writes=[("ps", pi)])
            P.op("vector", lambda v, ps=ps, tc=tc, dh=dh: v.tensor_copy(
                V[:, tc * D + dh * 512: tc * D + (dh + 1) * 512], ps[:, :]),
                reads=[("ps", pi)], writes=[("V", tc)])

    for DST, dkey in ((Q, "Q"), (K, "K")):
        for c in range(NCH):
            sl = DST[:, c * T:(c + 1) * T]
            for tt in range(NTT):
                pi = tt % 2
                tsl = slice(tt * TT, (tt + 1) * TT)
                P.op("tensor", lambda t, pi=pi, sl=sl, tsl=tsl: t.matmul(
                    PS[pi][:, :], RT[:], sl[:, tsl], start=True, stop=True),
                    reads=[(dkey, c), "RT"], writes=[("ps", pi)])
                P.op("vector", lambda v, pi=pi, tsl=tsl: v.tensor_mul(
                    TMPC[:], PS[pi][:, :], RB2[:, tsl]),
                    reads=[("ps", pi), "RB2"], writes=["TMPC"])
                P.op("vector", lambda v, sl=sl, tsl=tsl: v.tensor_mul(TMPD[:], sl[:, tsl], RA[:, tsl]),
                     reads=[(dkey, c), "RA"], writes=["TMPD"])
                P.op("vector", lambda v, sl=sl, tsl=tsl: v.tensor_add(sl[:, tsl], TMPD[:], TMPC[:]),
                     reads=["TMPD", "TMPC"], writes=[(dkey, c)])

    ROWS = H["ROWS"]
    qn2 = ROWS[0:1, 0:T]
    kn2 = ROWS[0:1, T:2 * T]
    km2 = ROWS[0:1, 2 * T:2 * T + 8]
    zr = ROWS[0:1, 2 * T + 8:2 * T + 8 + TT]
    RBF = H["ROWSBF"]
    P.fence("scalar", reads=[("Y", c) for c in range(NCH)],
            writes=[("ET", kc) for kc in range(16)])
    P.fence("vector", reads=[("W", ci) for ci in range(NCH)], writes=["MASKS"])
    P.op("sync", lambda s: s.dma_start(
        out=MASKS.rearrange("p (m n) -> p m n", m=4), in_=masksC.rearrange("m p n -> p m n")),
        writes=["MASKS"], dma=True)
    for h in range(16):
        c = h // 2
        hb = (h % 2) * 64
        Qh = Q[:, c * T:(c + 1) * T]
        Kh = K[:, c * T:(c + 1) * T]
        qkey, kkey = ("Q", c), ("K", c)
        for SRCh, skey, dstrow in ((Qh, qkey, qn2), (Kh, kkey, kn2)):
            P.op("scalar", lambda sc, SRCh=SRCh, hb=hb: sc.activation(
                H["XSQB"][hb:hb + 64, :], SRCh[hb:hb + 64, :], AF.Square),
                reads=[skey], writes=["XSQB"])
            for tt in range(NTT):
                pi = 4 + tt % 2
                P.op("tensor", lambda t, tt=tt, pi=pi, hb=hb: t.matmul(
                    PS[pi][0:1, :], ONES[hb:hb + 64, 0:1] if hb == 64 else ONES[0:64, 0:1],
                    H["XSQB"][hb:hb + 64, tt * TT:(tt + 1) * TT], start=True, stop=True),
                    reads=["XSQB", "ONES"], writes=[("ps", pi)])
                P.op("vector", lambda v, tt=tt, pi=pi, dstrow=dstrow: v.tensor_copy(
                    dstrow[0:1, tt * TT:(tt + 1) * TT], PS[pi][0:1, :]),
                    reads=[("ps", pi)], writes=["rows"])
        P.op("vector", lambda v: v.reduce_max(km2[0:1, 0:1], qn2, axis=AX.X), reads=["rows"], writes=["rows"])
        P.op("vector", lambda v: v.reduce_max(km2[0:1, 1:2], kn2, axis=AX.X), reads=["rows"], writes=["rows"])
        P.op("vector", lambda v: v.tensor_add(km2[0:1, 2:3], km2[0:1, 0:1], km2[0:1, 1:2]),
             reads=["rows"], writes=["rows"])
        P.op("vector", lambda v: v.tensor_scalar_mul(km2[0:1, 3:4], km2[0:1, 2:3], -1.0 / 16.0),
             reads=["rows"], writes=["rows"])
        P.op("vector", lambda v: v.tensor_copy(RBF[0:1, 0:1], km2[0:1, 3:4]), reads=["rows"], writes=["rowsbf"])
        P.op("tensor", lambda t: t.matmul(PS[7][:, 0:1], ONES[0:1, :], RBF[0:1, 0:1], start=True, stop=True),
             reads=["rowsbf", "ONES"], writes=[("ps", 7)])
        P.op("vector", lambda v: v.tensor_copy(BA[:], PS[7][:, 0:1]), reads=[("ps", 7)], writes=["BA"])
        for qt in range(NTT):
            nkc = min(16, 4 * qt + 4)
            for kc in range(nkc):
                pi = kc % 2
                ps = PS[pi]
                P.op("tensor", lambda t, ps=ps, kc=kc, qt=qt, Kh=Kh, Qh=Qh, hb=hb: t.matmul(
                    ps[:, :], Kh[hb:hb + 64, kc * 128:(kc + 1) * 128],
                    Qh[hb:hb + 64, qt * TT:(qt + 1) * TT], start=True, stop=True),
                    reads=[kkey, qkey], writes=[("ps", pi)])
                P.op("scalar", lambda sc, ps=ps, kc=kc: sc.activation(
                    ET[:, kc * TT:(kc + 1) * TT], ps[:, :], AF.Exp, bias=BA[:], scale=1.0 / 8.0),
                    reads=[("ps", pi), "BA"], writes=[("ET", kc)])
                mmask = kc - 4 * qt
                if mmask >= 0:
                    P.op("vector", lambda v, kc=kc, mmask=mmask: v.tensor_mul(
                        ET[:, kc * TT:(kc + 1) * TT], ET[:, kc * TT:(kc + 1) * TT],
                        MASKS[:, mmask * 512:(mmask + 1) * 512]),
                        reads=[("ET", kc), "MASKS"], writes=[("ET", kc)])
            for kc in range(nkc):
                P.op("tensor", lambda t, kc=kc, nkc=nkc: t.matmul(
                    PS[6][0:1, :], ONES[:, 0:1], ET[:, kc * TT:(kc + 1) * TT],
                    start=(kc == 0), stop=(kc == nkc - 1)),
                    reads=[("ET", kc), "ONES"], writes=[("ps", 6)])
            P.op("vector", lambda v: v.tensor_copy(zr, PS[6][0:1, :]), reads=[("ps", 6)], writes=["zrow"])
            P.op("vector", lambda v: v.reciprocal(zr, zr), reads=["zrow"], writes=["zrow"])
            P.op("vector", lambda v: v.tensor_copy(RBF[0:1, 0:TT], zr), reads=["zrow"], writes=["rowsbf"])
            P.op("tensor", lambda t: t.matmul(PS[6][:, :], ONES[0:1, :], RBF[0:1, 0:TT],
                                              start=True, stop=True),
                 reads=["rowsbf", "ONES"], writes=[("ps", 6)])
            P.op("vector", lambda v: v.tensor_copy(ZIB[:], PS[6][:, :]), reads=[("ps", 6)], writes=["ZIB"])
            for kc in range(nkc):
                P.op("vector", lambda v, kc=kc: v.tensor_mul(
                    ET[:, kc * TT:(kc + 1) * TT], ET[:, kc * TT:(kc + 1) * TT], ZIB[:]),
                    reads=[("ET", kc), "ZIB"], writes=[("ET", kc)])
            pi = 2 + qt % 2
            ps = PS[pi]
            for kc in range(nkc):
                P.op("tensor", lambda t, ps=ps, kc=kc, h=h: t.matmul(
                    ps[0:64, :], V[:, kc * D + h * 64: kc * D + (h + 1) * 64],
                    ET[:, kc * TT:(kc + 1) * TT], start=(kc == 0), stop=(kc == nkc - 1)),
                    reads=[("V", kc), ("ET", kc)], writes=[("ps", pi)])
            P.op("vector", lambda v, ps=ps, qt=qt, Qh=Qh, hb=hb: v.tensor_copy(
                Qh[hb:hb + 64, qt * TT:(qt + 1) * TT], ps[0:64, :]),
                reads=[("ps", pi)], writes=[qkey])

    P.fence("vector", reads=["MASKS"], writes=[("W", ci) for ci in range(NCH)])
    emit_load_weight_bf16(P, H, wot, "wot", WBF, "W", D, D)
    for dj in range(NCH):
        for tt in range(NTT):
            pi = (dj * NTT + tt) % 2
            ps = PS[pi]
            for ci in range(NCH):
                P.op("tensor", lambda t, ps=ps, ci=ci, dj=dj, tt=tt: t.matmul(
                    ps[:, :], WBF[:, ci * D + dj * 128: ci * D + (dj + 1) * 128],
                    Q[:, ci * T + tt * TT: ci * T + (tt + 1) * TT],
                    start=(ci == 0), stop=(ci == NCH - 1)),
                    reads=[("W", ci), ("Q", ci)], writes=[("ps", pi)])
            xrl = XRL[:, :]
            P.op("sync", lambda s, dj=dj, tt=tt, xrl=xrl: s.dma_start(
                out=xrl, in_=xin[dj * 128:(dj + 1) * 128, tt * TT:(tt + 1) * TT]),
                reads=["xin"], writes=["XT"], dma=True)
            P.op("scalar", lambda sc, ps=ps, dj=dj: sc.activation(
                OTMP[:], ps[:, :], AF.Identity, bias=BO[:, dj:dj + 1]),
                reads=[("ps", pi), "BO"], writes=["OTMP"])
            P.op("vector", lambda v, xrl=xrl: v.tensor_add(OTMP[:], OTMP[:], xrl),
                 reads=["OTMP", "XT"], writes=["OTMP"])
            P.op("gpsimd", lambda g, dj=dj, tt=tt: g.dma_start(
                out=xout[dj * 128:(dj + 1) * 128, tt * TT:(tt + 1) * TT], in_=OTMP[:]),
                reads=["OTMP"], writes=[("xout", dj, tt)], dma=True)

    P.barrier_all_dma_done("gpsimd")
    with nc.Block() as block:
        P.finalize(block)
    es.close()
    return nc


def l1_attn_b_host_inputs(INP, ones, masks):
    H, DH = 16, 64
    Wqkv = INP["L1_attn_b_Wqkv"]
    bqkv = INP["L1_attn_b_bqkv"]
    Wh = Wqkv.reshape(H, 3 * DH, D)
    bh = bqkv.reshape(H, 3 * DH)
    WQ = np.concatenate([Wh[h, 0:DH] for h in range(H)], 0).T
    WK = np.concatenate([Wh[h, DH:2 * DH] for h in range(H)], 0).T
    WV = np.concatenate([Wh[h, 2 * DH:3 * DH] for h in range(H)], 0).T
    bQ = np.concatenate([bh[h, 0:DH] for h in range(H)])
    bK = np.concatenate([bh[h, DH:2 * DH] for h in range(H)])
    bV = np.concatenate([bh[h, 2 * DH:3 * DH] for h in range(H)])
    pos = np.arange(T, dtype=np.float64)
    ang = pos[:, None] / 10000.0 ** (np.arange(0, DH, 2, dtype=np.float64) / DH)
    rope = np.stack([np.cos(ang), np.sin(ang)], -1).reshape(T, DH)
    A = rope.T
    B = 1.0 - rope.T
    Af = np.concatenate([A, A], 0).astype(np.float32)
    Bf = np.concatenate([B, B], 0).astype(np.float32)
    R = np.zeros((64, 64), np.float32)
    for i in range(32):
        R[2 * i, 2 * i + 1] = -1.0
        R[2 * i + 1, 2 * i] = 1.0
    R2 = np.zeros((128, 128), np.float32)
    R2[0:64, 0:64] = R
    R2[64:128, 64:128] = R
    d = {
        "wqt": np.ascontiguousarray(WQ),
        "wkt": np.ascontiguousarray(WK),
        "wvt": np.ascontiguousarray(WV),
        "wot": np.ascontiguousarray(INP["L1_attn_b_Wproj"].T),
        "bq": np.ascontiguousarray(bQ.reshape(NCH, 128).T),
        "bk": np.ascontiguousarray(bK.reshape(NCH, 128).T),
        "bvrow": np.ascontiguousarray(bV.reshape(1, D)),
        "bo": np.ascontiguousarray(INP["L1_attn_b_bproj"].reshape(NCH, 128).T),
        "ropeA": Af, "ropeB": Bf,
        "rt": np.ascontiguousarray(R2.T).astype(ml_dtypes.bfloat16),
        "onesC": ones, "masksC": masks,
    }
    return d



def build_l1_attn_a(niter=10):
    nc = bass.Bass("TRN2", target_bir_lowering=False)
    es = ExitStack()
    dram = lambda n, s, dt=F32: nc.dram_tensor(n, s, dt, kind="ExternalInput").ap()
    xin = dram("xin", [D, T])
    wqt = dram("wqt", [D, D])
    wkt = dram("wkt", [D, D])
    wvt = dram("wvt", [D, D])
    wot = dram("wot", [D, D])
    bq = dram("bq", [128, NCH])
    bk = dram("bk", [128, NCH])
    bvrow = dram("bvrow", [1, D])
    bo = dram("bo", [128, NCH])
    onesC = dram("onesC", [128, 128], BF16)
    identC = dram("identC", [128, 128], BF16)
    xout = nc.dram_tensor("xout", [D, T], F32, kind="ExternalOutput").ap()

    sb = lambda name, shape, dt: es.enter_context(nc.sbuf_tensor(name, shape, dt))
    H = {}
    H["XT"] = sb("XT", [128, T], F32)
    SCR = sb("SCR", [128, 8192], BF16)
    H["XB"] = SCR[:, 0:T]
    H["XSQB"] = SCR[:, T:2 * T]
    H["MBB"] = SCR[:, 2 * T:3 * T]
    H["RBB"] = SCR[:, 3 * T:4 * T]
    EQ = SCR[:, 0:T]
    MQ = SCR[:, T:2 * T]
    WT = SCR[:, 2 * T:3 * T]
    H["ROWS"] = sb("ROWS", [1, 2 * T + 520], F32)
    H["ROWSBF"] = sb("ROWSBF", [1, TT], BF16)
    ONES = sb("ONESs", [128, 128], BF16); H["ONES"] = ONES
    IDENT = sb("IDENT", [128, 128], BF16)
    Y = sb("Y", [128, NCH * T], BF16)
    Q = sb("Q", [128, NCH * T], BF16)
    K = sb("Kt", [128, NCH * T], BF16)
    V = sb("V", [128, 16 * D], BF16)
    H["WST"] = H["XT"][:, 0:1024]
    WBF = sb("WBF", [128, NCH * D], BF16)
    BQ = sb("BQ", [128, NCH], F32)
    BK = sb("BK", [128, NCH], F32)
    BO = sb("BO", [128, NCH], F32)
    BVR = sb("BVR", [1, D], BF16)
    BAr = sb("BAr", [1, T], F32)
    BARB = sb("BARB", [1, T], BF16)
    CRB = sb("CRB", [1, T], BF16)
    VSUM = sb("VSUM", [1, 64], BF16)
    COLS = sb("COLS", [128, 12], F32)
    ZIR = sb("ZIR", [1, 128], BF16)
    CRZ = sb("CRZ", [1, 128], BF16)
    ZCB = sb("ZCB", [128, 1], BF16)
    OTMP = H["XT"][:, 1536:2048]
    XRL = H["XT"][:, 1024:1536]
    PSQ = es.enter_context(nc.psum_tensor("psq", [128, 2048], F32))
    PS = [PSQ[:, i * 512:(i + 1) * 512] for i in range(4)] +         [es.enter_context(nc.psum_tensor(f"ps{i}", [128, 512], F32))[:, :] for i in range(4, 8)]
    H["PS"] = PS

    P = Prog(nc, es)
    ld = lambda dst, srcap, key: P.op(
        "sync", lambda s, dst=dst, srcap=srcap: s.dma_start(out=dst, in_=srcap),
        writes=[key], dma=True)
    ld(ONES[:], onesC, "ONES")
    ld(IDENT[:], identC, "IDENT")
    ld(BQ[:], bq, "BQ")
    ld(BK[:], bk, "BK")
    ld(BO[:], bo, "BO")
    P.op("sync", lambda s: s.dma_start(out=H["XT"][0:1, 0:D], in_=bvrow), reads=["XT"], writes=["XT"], dma=True)
    P.op("vector", lambda v: v.tensor_copy(BVR[:], H["XT"][0:1, 0:D]), reads=["XT"], writes=["BVR"])

    emit_ln(P, H, xin, "xin", Y, "Y")

    for W, wk_, BT, btk, DST, dkey in ((wqt, "wqt", BQ, "BQ", Q, "Q"), (wkt, "wkt", BK, "BK", K, "K")):
        emit_load_weight_bf16(P, H, W, wk_, WBF, "W", D, D)
        for dj in range(NCH):
            for tt in range(NTT):
                pi = 4 + (dj * NTT + tt) % 2
                ps = PS[pi]
                for ci in range(NCH):
                    P.op("tensor", lambda t, ps=ps, ci=ci, dj=dj, tt=tt: t.matmul(
                        ps[:, :], WBF[:, ci * D + dj * 128: ci * D + (dj + 1) * 128],
                        Y[:, ci * T + tt * TT: ci * T + (tt + 1) * TT],
                        start=(ci == 0), stop=(ci == NCH - 1)),
                        reads=[("W", ci), ("Y", ci)], writes=[("ps", pi)])
                P.op("scalar", lambda sc, ps=ps, dj=dj, tt=tt, DST=DST, BT=BT: sc.activation(
                    DST[:, dj * T + tt * TT: dj * T + (tt + 1) * TT], ps[:, :], AF.Identity,
                    bias=BT[:, dj:dj + 1]),
                    reads=[("ps", pi), btk], writes=[(dkey, dj)])
    emit_load_weight_bf16(P, H, wvt, "wvt", WBF, "W", D, D)
    for tc in range(16):
        for dh in range(2):
            pi = 4 + (tc * 2 + dh) % 2
            ps = PS[pi]
            for ci in range(NCH):
                P.op("tensor", lambda t, ps=ps, ci=ci, tc=tc, dh=dh: t.matmul(
                    ps[:, :], Y[:, ci * T + tc * 128: ci * T + (tc + 1) * 128],
                    WBF[:, ci * D + dh * 512: ci * D + (dh + 1) * 512],
                    start=(ci == 0), stop=False),
                    reads=[("W", ci), ("Y", ci)], writes=[("ps", pi)])
            P.op("tensor", lambda t, ps=ps, dh=dh: t.matmul(
                ps[:, :], ONES[0:1, :], BVR[0:1, dh * 512:(dh + 1) * 512], start=False, stop=True),
                reads=["ONES", "BVR"], writes=[("ps", pi)])
            P.op("vector", lambda v, ps=ps, tc=tc, dh=dh: v.tensor_copy(
                V[:, tc * D + dh * 512: tc * D + (dh + 1) * 512], ps[:, :]),
                reads=[("ps", pi)], writes=[("V", tc)])

    ROWS = H["ROWS"]
    qn2 = ROWS[0:1, 0:T]
    kn2 = ROWS[0:1, T:2 * T]
    sc8 = ROWS[0:1, 2 * T:2 * T + 8]
    RBF = H["ROWSBF"]
    P.fence("scalar", reads=[("Y", c) for c in range(NCH)],
            writes=["EQ", "MQ", "WT"])
    P.fence("vector", reads=[("ps", i) for i in range(4)],
            writes=[("psq", kb) for kb in range(4)])
    KEEP = float(int(T * 0.1))

    ba = COLS[:, 0:1]
    cc = COLS[:, 1:2]
    lo = COLS[:, 2:3]
    hi = COLS[:, 3:4]
    tau = COLS[:, 4:5]
    cnt = COLS[:, 5:6]
    take = COLS[:, 6:7]
    d1 = COLS[:, 7:8]
    d2 = COLS[:, 8:9]
    rm = COLS[:, 9:10]
    zz = COLS[:, 10:11]
    zi = COLS[:, 11:12]

    for h in range(16):
        c = h // 2
        hb = (h % 2) * 64
        Qh = Q[:, c * T:(c + 1) * T]
        Kh = K[:, c * T:(c + 1) * T]
        qkey, kkey = ("Q", c), ("K", c)
        for SRCh, skey, dstrow in ((Qh, qkey, qn2), (Kh, kkey, kn2)):
            P.op("scalar", lambda sc, SRCh=SRCh, hb=hb: sc.activation(
                H["XSQB"][hb:hb + 64, :], SRCh[hb:hb + 64, :], AF.Square),
                reads=[skey], writes=["XSQB"])
            for tt in range(NTT):
                pi = 4 + tt % 2
                P.op("tensor", lambda t, tt=tt, pi=pi, hb=hb: t.matmul(
                    PS[pi][0:1, :], ONES[hb:hb + 64, 0:1], H["XSQB"][hb:hb + 64, tt * TT:(tt + 1) * TT],
                    start=True, stop=True),
                    reads=["XSQB", "ONES"], writes=[("ps", pi)])
                P.op("vector", lambda v, tt=tt, pi=pi, dstrow=dstrow: v.tensor_copy(
                    dstrow[0:1, tt * TT:(tt + 1) * TT], PS[pi][0:1, :]),
                    reads=[("ps", pi)], writes=["rows"])
        P.op("vector", lambda v: v.reduce_max(sc8[0:1, 0:1], kn2, axis=AX.X), reads=["rows"], writes=["rows"])
        P.op("vector", lambda v: v.tensor_scalar(BAr[0:1, :], qn2, sc8[0:1, 0:1], -1.0 / 16.0,
                                                 ALU.add, op1=ALU.mult),
             reads=["rows"], writes=["BAr"])
        P.op("vector", lambda v: v.tensor_copy(BARB[0:1, :], BAr[0:1, :]), reads=["BAr"], writes=["BARB"])
        P.op("scalar", lambda sc: sc.activation(CRB[0:1, :], BAr[0:1, :], AF.Exp),
             reads=["BAr"], writes=["CRB"])
        for kc in range(16):
            P.op("tensor", lambda t, kc=kc, h=h: t.matmul(
                PS[6][0:1, 0:64], ONES[:, 0:1], V[:, kc * D + h * 64:(kc * D + (h + 1) * 64)],
                start=(kc == 0), stop=(kc == 15)),
                reads=[("V", kc), "ONES"], writes=[("ps", 6)])
        P.op("vector", lambda v: v.tensor_copy(VSUM[:], PS[6][0:1, 0:64]), reads=[("ps", 6)], writes=["VSUM"])
        for qt in range(16):
            qsl = slice(qt * 128, (qt + 1) * 128)
            P.op("tensor", lambda t, qsl=qsl: t.matmul(
                PS[7][:, 0:1], BARB[0:1, qsl], ONES[0:1, 0:1], start=True, stop=True),
                reads=["BARB", "ONES"], writes=[("ps", 7)])
            P.op("vector", lambda v: v.tensor_copy(ba, PS[7][:, 0:1]), reads=[("ps", 7)], writes=["COLS"])
            P.op("tensor", lambda t, qsl=qsl: t.matmul(
                PS[7][:, 0:1], CRB[0:1, qsl], ONES[0:1, 0:1], start=True, stop=True),
                reads=["CRB", "ONES"], writes=[("ps", 7)])
            P.op("vector", lambda v: v.tensor_copy(cc, PS[7][:, 0:1]), reads=[("ps", 7)], writes=["COLS"])
            for kb in range(4):
                P.op("tensor", lambda t, kb=kb, qt=qt, hb=hb, Qh=Qh, Kh=Kh: t.matmul(
                    PSQ[:, kb * 512:(kb + 1) * 512], Qh[hb:hb + 64, qt * 128:(qt + 1) * 128],
                    Kh[hb:hb + 64, kb * 512:(kb + 1) * 512], start=True, stop=True),
                    reads=[qkey, kkey], writes=[("psq", kb)])
            P.op("scalar", lambda sc: sc.activation(EQ[:, :], PSQ[:, :], AF.Exp, bias=ba,
                                                    scale=1.0 / 8.0),
                reads=[("psq", 0), ("psq", 1), ("psq", 2), ("psq", 3), "COLS"], writes=["EQ"])
            P.op("vector", lambda v: v.reduce_max(rm, EQ[:, :], axis=AX.X), reads=["EQ"], writes=["COLS"])
            P.op("vector", lambda v: v.memset(lo, 0.0), reads=[], writes=["COLS"])
            P.op("vector", lambda v: v.tensor_copy(hi, rm), reads=["COLS"], writes=["COLS"])
            P.op("vector", lambda v: v.tensor_scalar_mul(tau, rm, 0.5), reads=["COLS"], writes=["COLS"])
            for it in range(niter):
                P.op("vector", lambda v: v.tensor_scalar(MQ[:, :], EQ[:, :], tau, 0.0,
                                                         ALU.is_ge, op1=ALU.add, accum_out=cnt),
                     reads=["EQ", "COLS"], writes=["MQ", "COLS"])
                P.op("vector", lambda v: v.tensor_scalar(take, cnt, KEEP - 0.5, None, ALU.is_ge),
                     reads=["COLS"], writes=["COLS"])
                P.op("vector", lambda v: v.tensor_sub(d1, tau, lo), reads=["COLS"], writes=["COLS"])
                P.op("vector", lambda v: v.tensor_sub(d2, hi, tau), reads=["COLS"], writes=["COLS"])
                P.op("vector", lambda v: v.tensor_scalar(lo, take, d1, lo, ALU.mult, op1=ALU.add),
                     reads=["COLS"], writes=["COLS"])
                P.op("vector", lambda v: v.tensor_scalar(hi, take, d2, tau, ALU.mult, op1=ALU.add),
                     reads=["COLS"], writes=["COLS"])
                P.op("vector", lambda v: v.tensor_add(tau, lo, hi), reads=["COLS"], writes=["COLS"])
                P.op("vector", lambda v: v.tensor_scalar_mul(tau, tau, 0.5), reads=["COLS"], writes=["COLS"])
            P.op("vector", lambda v: v.tensor_scalar(MQ[:, :], EQ[:, :], lo, None, ALU.is_ge),
                 reads=["EQ", "COLS"], writes=["MQ"])
            P.op("vector", lambda v: v.scalar_tensor_tensor(EQ[:, :], EQ[:, :], cc, MQ[:, :],
                                                            ALU.subtract, ALU.mult, accum_out=zz),
                 reads=["EQ", "MQ", "COLS"], writes=["EQ", "COLS"])
            P.op("vector", lambda v: v.tensor_scalar(zz, cc, float(T), zz, ALU.mult, op1=ALU.add),
                 reads=["COLS"], writes=["COLS"])
            P.op("vector", lambda v: v.reciprocal(zi, zz), reads=["COLS"], writes=["COLS"])
            P.op("vector", lambda v: v.tensor_scalar(EQ[:, :], EQ[:, :], zi, None, ALU.mult),
                 reads=["EQ", "COLS"], writes=["EQ"])
            P.op("vector", lambda v: v.tensor_copy(ZCB[:], zi), reads=["COLS"], writes=["ZCB"])
            P.op("tensor", lambda t: t.matmul(PS[7][0:1, 0:128], ZCB[:], IDENT[:], start=True, stop=True),
                 reads=["ZCB", "IDENT"], writes=[("ps", 7)])
            P.op("vector", lambda v: v.tensor_copy(ZIR[:], PS[7][0:1, 0:128]), reads=[("ps", 7)], writes=["ZIR"])
            P.op("vector", lambda v, qsl=qsl: v.tensor_mul(CRZ[:], CRB[0:1, qsl], ZIR[:]),
                 reads=["CRB", "ZIR"], writes=["CRZ"])
            pi = 4 + qt % 2
            po = PS[pi]
            for kc in range(16):
                P.op("tensor", lambda t, kc=kc: t.matmul(
                    PS[6][:, 0:128], EQ[:, kc * 128:(kc + 1) * 128], IDENT[:], start=True, stop=True),
                    reads=["EQ", "IDENT"], writes=[("ps", 6)])
                P.op("vector", lambda v, kc=kc: v.tensor_copy(WT[:, kc * 128:(kc + 1) * 128],
                                                              PS[6][:, 0:128]),
                     reads=[("ps", 6)], writes=["WT"])
            for kc in range(16):
                P.op("tensor", lambda t, kc=kc, h=h, po=po: t.matmul(
                    po[0:64, 0:128], V[:, kc * D + h * 64: kc * D + (h + 1) * 64],
                    WT[:, kc * 128:(kc + 1) * 128], start=(kc == 0), stop=False),
                    reads=[("V", kc), "WT"], writes=[("ps", pi)])
            P.op("tensor", lambda t, po=po: t.matmul(
                po[0:64, 0:128], VSUM[0:1, 0:64], CRZ[0:1, :], start=False, stop=True),
                reads=["VSUM", "CRZ"], writes=[("ps", pi)])
            P.op("vector", lambda v, po=po, qt=qt, Qh=Qh, hb=hb: v.tensor_copy(
                Qh[hb:hb + 64, qt * 128:(qt + 1) * 128], po[0:64, 0:128]),
                reads=[("ps", pi)], writes=[qkey])

    P.fence("vector", reads=["EQ", "MQ", "WT"], writes=[("W", ci) for ci in range(NCH)])
    emit_load_weight_bf16(P, H, wot, "wot", WBF, "W", D, D)
    for dj in range(NCH):
        for tt in range(NTT):
            pi = 4 + (dj * NTT + tt) % 2
            ps = PS[pi]
            for ci in range(NCH):
                P.op("tensor", lambda t, ps=ps, ci=ci, dj=dj, tt=tt: t.matmul(
                    ps[:, :], WBF[:, ci * D + dj * 128: ci * D + (dj + 1) * 128],
                    Q[:, ci * T + tt * TT: ci * T + (tt + 1) * TT],
                    start=(ci == 0), stop=(ci == NCH - 1)),
                    reads=[("W", ci), ("Q", ci)], writes=[("ps", pi)])
            xrl = XRL[:, :]
            P.op("sync", lambda s, dj=dj, tt=tt, xrl=xrl: s.dma_start(
                out=xrl, in_=xin[dj * 128:(dj + 1) * 128, tt * TT:(tt + 1) * TT]),
                reads=["xin"], writes=["XT"], dma=True)
            P.op("scalar", lambda sc, ps=ps, dj=dj: sc.activation(
                OTMP[:], ps[:, :], AF.Identity, bias=BO[:, dj:dj + 1]),
                reads=[("ps", pi), "BO", "XT"], writes=["XT"])
            P.op("vector", lambda v, xrl=xrl: v.tensor_add(OTMP[:], OTMP[:], xrl),
                 reads=["XT"], writes=["XT"])
            P.op("gpsimd", lambda g, dj=dj, tt=tt: g.dma_start(
                out=xout[dj * 128:(dj + 1) * 128, tt * TT:(tt + 1) * TT], in_=OTMP[:]),
                reads=["XT"], writes=[("xout", dj, tt)], dma=True)

    P.barrier_all_dma_done("gpsimd")
    with nc.Block() as block:
        P.finalize(block)
    es.close()
    return nc



def build_l1_ffn_a():
    nc = bass.Bass("TRN2", target_bir_lowering=False)
    es = ExitStack()
    dram = lambda n, s, dt=F32: nc.dram_tensor(n, s, dt, kind="ExternalInput").ap()
    xin = dram("xin", [D, T])
    wq = dram("wq", [D, D])
    wkc = dram("wkc", [D, 256])
    wk = dram("wk", [256, D])
    wvc = dram("wvc", [D, 256])
    wv = dram("wv", [256, D])
    wo = dram("wo", [D, D])
    n1w = dram("n1w", [128, NCH])
    ropeC = dram("ropeC", [128, T])
    ropeD = dram("ropeD", [128, T])
    rt = dram("rt", [128, 128], BF16)
    wg10 = dram("wg10", [D, 10])
    big10 = dram("big10", [128, 10])
    onesC = dram("onesC", [128, 128], BF16)
    identC = dram("identC", [128, 128], BF16)
    oXI = nc.dram_tensor("oXI", [D, T], F32, kind="ExternalOutput").ap()
    oXN = nc.dram_tensor("oXN", [D, T], F32, kind="ExternalOutput").ap()
    oSEL = nc.dram_tensor("oSEL", [8, T], F32, kind="ExternalOutput").ap()
    dGL = nc.dram_tensor("dGL", [128, 160], F32, kind="ExternalOutput").ap()

    sb = lambda name, shape, dt: es.enter_context(nc.sbuf_tensor(name, shape, dt))
    H = {}
    H["XT"] = sb("XT", [128, T], F32)
    SCR = sb("SCR", [128, 8192], BF16)
    H["XB"] = SCR[:, 0:T]
    H["XSQB"] = SCR[:, T:2 * T]
    H["MBB"] = SCR[:, 2 * T:3 * T]
    H["RBB"] = SCR[:, 3 * T:4 * T]
    ET = SCR[:, 0:8192]
    H["ROWS"] = sb("ROWS", [1, 2 * T + 520], F32)
    H["ROWSBF"] = sb("ROWSBF", [1, TT], BF16)
    ONES = sb("ONESs", [128, 128], BF16); H["ONES"] = ONES
    IDENT = sb("IDENT", [128, 128], BF16)
    RT = sb("RT", [128, 128], BF16)
    RA = sb("RA", [128, T], BF16)
    RB2 = sb("RB2", [128, T], BF16)
    Y = sb("Y", [128, NCH * T], BF16)
    Q = sb("Q", [128, NCH * T], BF16)
    K = sb("Kt", [128, NCH * T], BF16)
    V = sb("V", [128, 16 * D], BF16)
    T1 = SCR[:, 4096:8192]
    H["WST"] = H["XT"][:, 0:1024]
    WBF = sb("WBF", [128, NCH * D], BF16)
    N1W = sb("N1W", [128, NCH], F32)
    G10H = sb("G10H", [128, NCH * 10], BF16)
    G10L = sb("G10L", [128, NCH * 10], BF16)
    BIG10 = sb("BIG10", [128, 10], F32)
    GL = sb("GL", [128, 10 * 16], F32)
    GS = sb("GS", [128, 24], F32)
    GS2 = sb("GS2", [128, 8], F32)
    WC8 = sb("WC8", [128, 8], BF16)
    SEL8 = sb("SEL8", [8, T], BF16)
    BA = sb("BA", [128, 1], F32)
    TMPC = sb("TMPC", [128, TT], F32)
    TMPD = sb("TMPD", [128, TT], BF16)
    TMPE = sb("TMPE", [128, TT], BF16)
    ZIB = sb("ZIB", [128, TT], BF16)
    OTMP = H["XT"][:, 1536:2048]
    XRL = H["XT"][:, 1024:1536]
    PS = [es.enter_context(nc.psum_tensor(f"ps{i}", [128, 512], F32))[:, :] for i in range(8)]
    H["PS"] = PS

    P = Prog(nc, es)
    ld = lambda dst, srcap, key: P.op(
        "sync", lambda s, dst=dst, srcap=srcap: s.dma_start(out=dst, in_=srcap),
        writes=[key], dma=True)
    ld(ONES[:], onesC, "ONES")
    ld(IDENT[:], identC, "IDENT")
    ld(RT[:], rt, "RT")
    ld(BIG10[:], big10, "BIG10")
    ld(N1W[:], n1w, "N1W")
    P.op("sync", lambda s: s.dma_start(out=H["XT"][:], in_=ropeC), reads=["XT"], writes=["XT"], dma=True)
    P.op("vector", lambda v: v.tensor_copy(RA[:], H["XT"][:]), reads=["XT"], writes=["RA"])
    P.op("sync", lambda s: s.dma_start(out=H["XT"][:], in_=ropeD), reads=["XT"], writes=["XT"], dma=True)
    P.op("vector", lambda v: v.tensor_copy(RB2[:], H["XT"][:]), reads=["XT"], writes=["RB2"])
    P.op("sync", lambda s: s.dma_start(
        out=H["XT"][:, 0:NCH * 10].rearrange("p (c n) -> p c n", c=NCH),
        in_=wg10.rearrange("(c p) n -> p c n", p=128)), reads=["XT"], writes=["XT"], dma=True)
    P.op("vector", lambda v: v.tensor_copy(G10H[:], H["XT"][:, 0:NCH * 10]), reads=["XT"], writes=["G10H"])
    P.op("vector", lambda v: v.tensor_copy(TMPC[:, 0:80], G10H[:]), reads=["G10H"], writes=["TMPC"])
    P.op("vector", lambda v: v.tensor_sub(H["XT"][:, 0:80], H["XT"][:, 0:80], TMPC[:, 0:80]),
         reads=["XT", "TMPC"], writes=["XT"])
    P.op("vector", lambda v: v.tensor_copy(G10L[:], H["XT"][:, 0:NCH * 10]), reads=["XT"], writes=["G10L"])

    emit_ln(P, H, xin, "xin", Y, "Y")

    emit_load_weight_bf16(P, H, wq, "wq", WBF, "W", D, D)
    emit_proj_fm(P, H, WBF, "W", Y, "Y", Q, "Q", NCH, D)
    P.fence("vector", reads=["MBB", "RBB"], writes=[("T1", 0), ("T1", 1)])
    emit_load_weight_bf16(P, H, wkc, "wkc", WBF, "W", D, 256)
    for dj in range(2):
        for tt in range(NTT):
            pi = (dj * NTT + tt) % 2
            ps = PS[pi]
            for ci in range(NCH):
                P.op("tensor", lambda t, ps=ps, ci=ci, dj=dj, tt=tt: t.matmul(
                    ps[:, :], WBF[:, ci * 256 + dj * 128: ci * 256 + (dj + 1) * 128],
                    Y[:, ci * T + tt * TT: ci * T + (tt + 1) * TT],
                    start=(ci == 0), stop=(ci == NCH - 1)),
                    reads=[("W", ci), ("Y", ci)], writes=[("ps", pi)])
            P.op("vector", lambda v, ps=ps, dj=dj, tt=tt: v.tensor_copy(
                T1[:, dj * T + tt * TT: dj * T + (tt + 1) * TT], ps[:, :]),
                reads=[("ps", pi)], writes=[("T1", dj)])
    emit_load_weight_bf16(P, H, wk, "wk", WBF, "W", 256, D)
    for dj in range(NCH):
        for tt in range(NTT):
            pi = (dj * NTT + tt) % 2
            ps = PS[pi]
            for ci in range(2):
                P.op("tensor", lambda t, ps=ps, ci=ci, dj=dj, tt=tt: t.matmul(
                    ps[:, :], WBF[:, ci * D + dj * 128: ci * D + (dj + 1) * 128],
                    T1[:, ci * T + tt * TT: ci * T + (tt + 1) * TT],
                    start=(ci == 0), stop=(ci == 1)),
                    reads=[("W", ci), ("T1", ci)], writes=[("ps", pi)])
            P.op("vector", lambda v, ps=ps, dj=dj, tt=tt: v.tensor_copy(
                K[:, dj * T + tt * TT: dj * T + (tt + 1) * TT], ps[:, :]),
                reads=[("ps", pi)], writes=[("K", dj)])
    emit_load_weight_bf16(P, H, wvc, "wvc", WBF, "W", D, 256)
    for dj in range(2):
        for tt in range(NTT):
            pi = (dj * NTT + tt) % 2
            ps = PS[pi]
            for ci in range(NCH):
                P.op("tensor", lambda t, ps=ps, ci=ci, dj=dj, tt=tt: t.matmul(
                    ps[:, :], WBF[:, ci * 256 + dj * 128: ci * 256 + (dj + 1) * 128],
                    Y[:, ci * T + tt * TT: ci * T + (tt + 1) * TT],
                    start=(ci == 0), stop=(ci == NCH - 1)),
                    reads=[("W", ci), ("Y", ci)], writes=[("ps", pi)])
            P.op("vector", lambda v, ps=ps, dj=dj, tt=tt: v.tensor_copy(
                T1[:, dj * T + tt * TT: dj * T + (tt + 1) * TT], ps[:, :]),
                reads=[("ps", pi)], writes=[("T1", dj)])
    emit_load_weight_bf16(P, H, wv, "wv", WBF, "W", 256, D)
    for tc in range(16):
        for dh in range(2):
            pi = (tc * 2 + dh) % 2
            ps = PS[pi]
            for ci in range(2):
                P.op("tensor", lambda t, ps=ps, ci=ci, tc=tc, dh=dh: t.matmul(
                    ps[:, :], T1[:, ci * T + tc * 128: ci * T + (tc + 1) * 128],
                    WBF[:, ci * D + dh * 512: ci * D + (dh + 1) * 512],
                    start=(ci == 0), stop=(ci == 1)),
                    reads=[("W", ci), ("T1", ci)], writes=[("ps", pi)])
            P.op("vector", lambda v, ps=ps, tc=tc, dh=dh: v.tensor_copy(
                V[:, tc * D + dh * 512: tc * D + (dh + 1) * 512], ps[:, :]),
                reads=[("ps", pi)], writes=[("V", tc)])

    for DST, dkey in ((Q, "Q"), (K, "K")):
        for c in range(NCH):
            sl = DST[:, c * T:(c + 1) * T]
            for tt in range(NTT):
                pi = tt % 2
                tsl = slice(tt * TT, (tt + 1) * TT)
                P.op("tensor", lambda t, pi=pi, sl=sl, tsl=tsl: t.matmul(
                    PS[pi][:, :], RT[:], sl[:, tsl], start=True, stop=True),
                    reads=[(dkey, c), "RT"], writes=[("ps", pi)])
                P.op("vector", lambda v, pi=pi, tsl=tsl: v.tensor_mul(
                    TMPC[:], PS[pi][:, :], RB2[:, tsl]),
                    reads=[("ps", pi), "RB2"], writes=["TMPC"])
                P.op("vector", lambda v, sl=sl, tsl=tsl: v.tensor_mul(TMPD[:], sl[:, tsl], RA[:, tsl]),
                     reads=[(dkey, c), "RA"], writes=["TMPD"])
                P.op("vector", lambda v, sl=sl, tsl=tsl: v.tensor_add(sl[:, tsl], TMPD[:], TMPC[:]),
                     reads=["TMPD", "TMPC"], writes=[(dkey, c)])

    ROWS = H["ROWS"]
    qn2 = ROWS[0:1, 0:T]
    kn2 = ROWS[0:1, T:2 * T]
    km2 = ROWS[0:1, 2 * T:2 * T + 8]
    zr = ROWS[0:1, 2 * T + 8:2 * T + 8 + TT]
    RBF = H["ROWSBF"]
    P.fence("scalar", reads=[("Y", c) for c in range(NCH)] + [("T1", 0), ("T1", 1)],
            writes=[("ET", kc) for kc in range(16)])
    for h in range(16):
        c = h // 2
        hb = (h % 2) * 64
        Qh = Q[:, c * T:(c + 1) * T]
        Kh = K[:, c * T:(c + 1) * T]
        qkey, kkey = ("Q", c), ("K", c)
        for SRCh, skey, dstrow in ((Qh, qkey, qn2), (Kh, kkey, kn2)):
            P.op("scalar", lambda sc, SRCh=SRCh, hb=hb: sc.activation(
                H["XSQB"][hb:hb + 64, :], SRCh[hb:hb + 64, :], AF.Square),
                reads=[skey], writes=["XSQB"])
            for tt in range(NTT):
                pi = 4 + tt % 2
                P.op("tensor", lambda t, tt=tt, pi=pi, hb=hb: t.matmul(
                    PS[pi][0:1, :], ONES[hb:hb + 64, 0:1], H["XSQB"][hb:hb + 64, tt * TT:(tt + 1) * TT],
                    start=True, stop=True),
                    reads=["XSQB", "ONES"], writes=[("ps", pi)])
                P.op("vector", lambda v, tt=tt, pi=pi, dstrow=dstrow: v.tensor_copy(
                    dstrow[0:1, tt * TT:(tt + 1) * TT], PS[pi][0:1, :]),
                    reads=[("ps", pi)], writes=["rows"])
        P.op("vector", lambda v: v.reduce_max(km2[0:1, 0:1], qn2, axis=AX.X), reads=["rows"], writes=["rows"])
        P.op("vector", lambda v: v.reduce_max(km2[0:1, 1:2], kn2, axis=AX.X), reads=["rows"], writes=["rows"])
        P.op("vector", lambda v: v.tensor_add(km2[0:1, 2:3], km2[0:1, 0:1], km2[0:1, 1:2]),
             reads=["rows"], writes=["rows"])
        P.op("vector", lambda v: v.tensor_scalar_mul(km2[0:1, 3:4], km2[0:1, 2:3], -1.0 / 16.0),
             reads=["rows"], writes=["rows"])
        P.op("vector", lambda v: v.tensor_copy(RBF[0:1, 0:1], km2[0:1, 3:4]), reads=["rows"], writes=["rowsbf"])
        P.op("tensor", lambda t: t.matmul(PS[7][:, 0:1], ONES[0:1, :], RBF[0:1, 0:1], start=True, stop=True),
             reads=["rowsbf", "ONES"], writes=[("ps", 7)])
        P.op("vector", lambda v: v.tensor_copy(BA[:], PS[7][:, 0:1]), reads=[("ps", 7)], writes=["BA"])
        for qt in range(NTT):
            for kc in range(16):
                pi = kc % 2
                ps = PS[pi]
                P.op("tensor", lambda t, ps=ps, kc=kc, qt=qt, Kh=Kh, Qh=Qh, hb=hb: t.matmul(
                    ps[:, :], Kh[hb:hb + 64, kc * 128:(kc + 1) * 128],
                    Qh[hb:hb + 64, qt * TT:(qt + 1) * TT], start=True, stop=True),
                    reads=[kkey, qkey], writes=[("ps", pi)])
                P.op("scalar", lambda sc, ps=ps, kc=kc: sc.activation(
                    ET[:, kc * TT:(kc + 1) * TT], ps[:, :], AF.Exp, bias=BA[:], scale=1.0 / 8.0),
                    reads=[("ps", pi), "BA"], writes=[("ET", kc)])
            for kc in range(16):
                P.op("tensor", lambda t, kc=kc: t.matmul(
                    PS[6][0:1, :], ONES[:, 0:1], ET[:, kc * TT:(kc + 1) * TT],
                    start=(kc == 0), stop=(kc == 15)),
                    reads=[("ET", kc), "ONES"], writes=[("ps", 6)])
            P.op("vector", lambda v: v.tensor_copy(zr, PS[6][0:1, :]), reads=[("ps", 6)], writes=["zrow"])
            P.op("vector", lambda v: v.reciprocal(zr, zr), reads=["zrow"], writes=["zrow"])
            P.op("vector", lambda v: v.tensor_copy(RBF[0:1, 0:TT], zr), reads=["zrow"], writes=["rowsbf"])
            P.op("tensor", lambda t: t.matmul(PS[6][:, :], ONES[0:1, :], RBF[0:1, 0:TT],
                                              start=True, stop=True),
                 reads=["rowsbf", "ONES"], writes=[("ps", 6)])
            P.op("vector", lambda v: v.tensor_copy(ZIB[:], PS[6][:, :]), reads=[("ps", 6)], writes=["ZIB"])
            for kc in range(16):
                P.op("vector", lambda v, kc=kc: v.tensor_mul(
                    ET[:, kc * TT:(kc + 1) * TT], ET[:, kc * TT:(kc + 1) * TT], ZIB[:]),
                    reads=[("ET", kc), "ZIB"], writes=[("ET", kc)])
            pi = 2 + qt % 2
            ps = PS[pi]
            for kc in range(16):
                P.op("tensor", lambda t, ps=ps, kc=kc, h=h: t.matmul(
                    ps[0:64, :], V[:, kc * D + h * 64: kc * D + (h + 1) * 64],
                    ET[:, kc * TT:(kc + 1) * TT], start=(kc == 0), stop=(kc == 15)),
                    reads=[("V", kc), ("ET", kc)], writes=[("ps", pi)])
            P.op("vector", lambda v, ps=ps, qt=qt, Qh=Qh, hb=hb: v.tensor_copy(
                Qh[hb:hb + 64, qt * TT:(qt + 1) * TT], ps[0:64, :]),
                reads=[("ps", pi)], writes=[qkey])

    M32 = SCR[:, 0:4096].bitcast(F32)
    R32 = SCR[:, 4096:8192].bitcast(F32)
    P.fence("vector", reads=[("ET", kc) for kc in range(16)],
            writes=["M32", "R32", "SB16A", "SB16B"])
    emit_ln_stats32(P, H, xin, "xin", M32, R32, "M32", "R32", TMPD, TMPE,
                    ka="TMPD", kb="TMPE")
    emit_load_weight_bf16(P, H, wo, "wo", WBF, "W", D, D)
    r0 = ROWS[0:1, 0:T]
    for dj in range(NCH):
        for tt in range(NTT):
            pi = (dj * NTT + tt) % 2
            ps = PS[pi]
            for ci in range(NCH):
                P.op("tensor", lambda t, ps=ps, ci=ci, dj=dj, tt=tt: t.matmul(
                    ps[:, :], WBF[:, ci * D + dj * 128: ci * D + (dj + 1) * 128],
                    Q[:, ci * T + tt * TT: ci * T + (tt + 1) * TT],
                    start=(ci == 0), stop=(ci == NCH - 1)),
                    reads=[("W", ci), ("Q", ci)], writes=[("ps", pi)])
            tsl = slice(tt * TT, (tt + 1) * TT)
            P.op("sync", lambda s, dj=dj, tsl=tsl: s.dma_start(
                out=XRL, in_=xin[dj * 128:(dj + 1) * 128, tsl]),
                reads=["xin", "XT"], writes=["XT"], dma=True)
            P.op("vector", lambda v, tsl=tsl: v.tensor_sub(TMPC[:], XRL, M32[:, tsl]),
                 reads=["XT", "M32"], writes=["TMPC"])
            P.op("vector", lambda v, tsl=tsl: v.tensor_mul(TMPC[:], TMPC[:], R32[:, tsl]),
                 reads=["TMPC", "R32"], writes=["TMPC"])
            P.op("vector", lambda v, ps=ps: v.tensor_add(OTMP[:], ps[:, :], TMPC[:]),
                 reads=[("ps", pi), "TMPC", "XT"], writes=["XT"])
            P.op("gpsimd", lambda g, dj=dj, tt=tt: g.dma_start(
                out=oXI[dj * 128:(dj + 1) * 128, tt * TT:(tt + 1) * TT], in_=OTMP[:]),
                reads=["XT"], writes=[("oXI", dj, tt)], dma=True)
            P.op("scalar", lambda sc: sc.activation(TMPD[:], OTMP[:], AF.Square),
                 reads=["XT"], writes=["TMPD"])
            P.op("tensor", lambda t, dj=dj, tt=tt: t.matmul(
                PS[4 + tt][0:1, :], ONES[:, 0:1], TMPD[:],
                start=(dj == 0), stop=(dj == NCH - 1)),
                reads=["TMPD", "ONES"], writes=[("ps", 4 + tt)])
    P.fence("vector", reads=["M32", "R32"], writes=["RBB"])
    for tt in range(NTT):
        tsl = slice(tt * TT, (tt + 1) * TT)
        P.op("vector", lambda v, tt=tt: v.tensor_scalar_mul(RBF[0:1, 0:TT], PS[4 + tt][0:1, :], 1.0 / D),
             reads=[("ps", 4 + tt)], writes=["rowsbf"])
        pi = tt % 2
        P.op("tensor", lambda t, pi=pi: t.matmul(PS[pi][:, :], ONES[0:1, :], RBF[0:1, 0:TT],
                                                 start=True, stop=True),
             reads=["rowsbf", "ONES"], writes=[("ps", pi)])
        P.op("vector", lambda v, tsl=tsl, pi=pi: v.tensor_copy(H["RBB"][:, tsl], PS[pi][:, :]),
             reads=[("ps", pi)], writes=["RBB"])
    P.op("vector", lambda v: v.tensor_scalar_add(H["RBB"][:], H["RBB"][:], 1e-6),
         reads=["RBB"], writes=["RBB"])
    P.op("scalar", lambda sc: sc.activation(H["RBB"][:], H["RBB"][:], AF.Ln), reads=["RBB"], writes=["RBB"])
    P.op("scalar", lambda sc: sc.activation(H["RBB"][:], H["RBB"][:], AF.Exp, scale=-0.5),
         reads=["RBB"], writes=["RBB"])
    GPS = [PS[4], PS[5]]
    P.fence("vector", reads=[("ps", 4), ("ps", 5), ("ps", 6), ("ps", 7)],
            writes=[("psg", 0), ("psg", 1)])
    P.op("vector", lambda v: v.memset(GL[:], 0.0), writes=["GL"])
    for dj in range(NCH):
        for tt in range(NTT):
            P.op("sync", lambda s, dj=dj, tt=tt: s.dma_start(
                out=OTMP[:], in_=oXI[dj * 128:(dj + 1) * 128, tt * TT:(tt + 1) * TT]),
                reads=[("oXI", dj, tt), "XT"], writes=["XT"], dma=True)
            P.op("vector", lambda v, tt=tt: v.tensor_mul(
                TMPC[:], OTMP[:], H["RBB"][:, tt * TT:(tt + 1) * TT]),
                reads=["XT", "RBB"], writes=["TMPC"])
            P.op("vector", lambda v, dj=dj: v.tensor_scalar(TMPC[:], TMPC[:], N1W[:, dj:dj + 1],
                                                            None, ALU.mult),
                 reads=["TMPC", "N1W"], writes=["TMPC"])
            P.op("gpsimd", lambda g, dj=dj, tt=tt: g.dma_start(
                out=oXN[dj * 128:(dj + 1) * 128, tt * TT:(tt + 1) * TT], in_=TMPC[:]),
                reads=["TMPC"], writes=[("oXN", dj, tt)], dma=True)
            P.op("vector", lambda v, dj=dj, tt=tt: v.tensor_copy(
                Y[:, dj * T + tt * TT: dj * T + (tt + 1) * TT], TMPC[:]),
                reads=["TMPC"], writes=[("Y", dj)])
            P.op("vector", lambda v, dj=dj, tt=tt: v.tensor_copy(
                OTMP[:], Y[:, dj * T + tt * TT: dj * T + (tt + 1) * TT]),
                reads=[("Y", dj), "XT"], writes=["XT"])
            P.op("vector", lambda v: v.tensor_sub(TMPC[:], TMPC[:], OTMP[:]),
                 reads=["TMPC", "XT"], writes=["TMPC"])
            P.op("vector", lambda v: v.tensor_copy(TMPD[:], TMPC[:]), reads=["TMPC"], writes=["TMPD"])
            for ms in range(4):
                mt = tt * 4 + ms
                gp = GPS[mt % 2]
                terms = [
                    (("Y", dj), Y[:, dj * T + tt * TT + ms * 128: dj * T + tt * TT + (ms + 1) * 128],
                     "G10H", G10H),
                    (("Y", dj), Y[:, dj * T + tt * TT + ms * 128: dj * T + tt * TT + (ms + 1) * 128],
                     "G10L", G10L),
                    ("TMPD", TMPD[:, ms * 128:(ms + 1) * 128], "G10H", G10H),
                ]
                for ti, (Lk, Lap, Rk, Rap) in enumerate(terms):
                    P.op("tensor", lambda t, gp=gp, Lap=Lap, Rap=Rap, dj=dj, ti=ti: t.matmul(
                        gp[:, 0:10], Lap, Rap[:, dj * 10:(dj + 1) * 10],
                        start=(ti == 0), stop=(ti == 2)),
                        reads=[Lk, Rk], writes=[("psg", mt % 2)])
                P.op("vector", lambda v, gp=gp, mt=mt: v.tensor_add(
                    GL[:, mt * 10:(mt + 1) * 10], GL[:, mt * 10:(mt + 1) * 10], gp[:, 0:10]),
                    reads=[("psg", mt % 2), "GL"], writes=["GL"])
    for mt in range(16):
        P.op("vector", lambda v, mt=mt: v.tensor_add(GL[:, mt * 10:(mt + 1) * 10],
                                                     GL[:, mt * 10:(mt + 1) * 10], BIG10[:]),
             reads=["GL", "BIG10"], writes=["GL"])
    P.op("gpsimd", lambda g: g.dma_start(out=dGL, in_=GL[:]), reads=["GL"], writes=[("dump",)], dma=True)
    for mt in range(16):
        g2 = GL[:, mt * 10:mt * 10 + 2]
        e8 = GL[:, mt * 10 + 2:mt * 10 + 10]
        dd = GS[:, 0:1]
        p0 = GS[:, 1:2]
        gsel = GS[:, 2:3]
        gidx = GS[:, 3:4]
        onemg = GS[:, 4:5]
        ml = GS[:, 8:16]
        mx = GS[:, 16:17]
        ssum = GS[:, 17:18]
        sinv = GS[:, 18:19]
        m1 = GS[:, 19:20]
        m2 = GS[:, 20:21]
        P.op("vector", lambda v, g2=g2: v.tensor_sub(dd, g2[:, 0:1], g2[:, 1:2]),
             reads=["GL"], writes=["GS"])
        P.op("scalar", lambda sc: sc.activation(p0, dd, AF.Sigmoid), reads=["GS"], writes=["GS"])
        P.op("vector", lambda v: v.tensor_scalar(gsel, p0, -1.0, 1.0, ALU.mult, op1=ALU.add),
             reads=["GS"], writes=["GS"])
        P.op("vector", lambda v: v.tensor_max(gsel, gsel, p0), reads=["GS"], writes=["GS"])
        P.op("vector", lambda v: v.tensor_scalar(gidx, p0, 0.5, None, ALU.is_lt),
             reads=["GS"], writes=["GS"])
        P.op("vector", lambda v: v.tensor_scalar(onemg, gidx, -1.0, 1.0, ALU.mult, op1=ALU.add),
             reads=["GS"], writes=["GS"])
        P.op("vector", lambda v, e8=e8: v.tensor_scalar(GS2[:, 0:4], e8[:, 0:4], 30.0, None, ALU.add),
             reads=["GL"], writes=["GS2"])
        P.op("vector", lambda v: v.tensor_scalar(ml[:, 0:4], GS2[:, 0:4], onemg, None, ALU.mult),
             reads=["GS2", "GS"], writes=["GS"])
        P.op("vector", lambda v, e8=e8: v.tensor_scalar(GS2[:, 0:4], e8[:, 4:8], 30.0, None, ALU.add),
             reads=["GL"], writes=["GS2"])
        P.op("vector", lambda v: v.tensor_scalar(ml[:, 4:8], GS2[:, 0:4], gidx, None, ALU.mult),
             reads=["GS2", "GS"], writes=["GS"])
        P.op("vector", lambda v: v.tensor_scalar_add(ml, ml, -30.0), reads=["GS"], writes=["GS"])
        P.op("vector", lambda v: v.reduce_max(mx, ml, axis=AX.X), reads=["GS"], writes=["GS"])
        P.op("vector", lambda v: v.tensor_scalar(GS2[:, 0:1], mx, -1.0, None, ALU.mult),
             reads=["GS"], writes=["GS2"])
        P.op("scalar", lambda sc: sc.activation(ml, ml, AF.Exp, bias=GS2[:, 0:1]),
             reads=["GS", "GS2"], writes=["GS"])
        P.op("vector", lambda v: v.reduce_sum(ssum, ml, axis=AX.X), reads=["GS"], writes=["GS"])
        P.op("vector", lambda v: v.reciprocal(sinv, ssum), reads=["GS"], writes=["GS"])
        P.op("vector", lambda v: v.tensor_scalar(ml, ml, sinv, None, ALU.mult),
             reads=["GS"], writes=["GS"])
        P.op("vector", lambda v: v.tensor_scalar(ml, ml, gsel, None, ALU.mult),
             reads=["GS"], writes=["GS"])
        P.op("vector", lambda v: v.reduce_max(m1, ml, axis=AX.X), reads=["GS"], writes=["GS"])
        P.op("vector", lambda v: v.tensor_scalar(GS2[:], ml, m1, None, ALU.is_lt),
             reads=["GS"], writes=["GS2"])
        P.op("vector", lambda v: v.tensor_mul(GS2[:], GS2[:], ml), reads=["GS2", "GS"], writes=["GS2"])
        P.op("vector", lambda v: v.reduce_max(m2, GS2[:], axis=AX.X), reads=["GS2"], writes=["GS"])
        P.op("vector", lambda v: v.tensor_scalar(GS2[:], ml, m2, None, ALU.is_ge),
             reads=["GS"], writes=["GS2"])
        P.op("vector", lambda v: v.tensor_mul(GS2[:], GS2[:], ml), reads=["GS2", "GS"], writes=["GS2"])
        P.op("vector", lambda v: v.tensor_copy(WC8[:], GS2[:]), reads=["GS2"], writes=["WC8"])
        pi = 2 + mt % 2
        P.op("tensor", lambda t, pi=pi: t.matmul(PS[pi][0:8, 0:128], WC8[:], IDENT[:],
                                                 start=True, stop=True),
             reads=["WC8", "IDENT"], writes=[("ps", pi)])
        P.op("vector", lambda v, mt=mt, pi=pi: v.tensor_copy(SEL8[:, mt * 128:(mt + 1) * 128],
                                                             PS[pi][0:8, 0:128]),
             reads=[("ps", pi)], writes=["SEL8"])
    SELF = H["XT"][0:8, 0:T]
    P.op("vector", lambda v: v.tensor_copy(SELF, SEL8[:]), reads=["SEL8", "XT"], writes=["XT"])
    P.op("gpsimd", lambda g: g.dma_start(out=oSEL, in_=SELF), reads=["XT"], writes=[("osel",)], dma=True)

    P.barrier_all_dma_done("gpsimd")
    with nc.Block() as block:
        P.finalize(block)
    es.close()
    return nc



def build_l1_ffn_b():
    nc = bass.Bass("TRN2", target_bir_lowering=False)
    es = ExitStack()
    dram = lambda n, s, dt=F32: nc.dram_tensor(n, s, dt, kind="ExternalInput").ap()
    xin = dram("xin", [D, T])
    xiI = dram("xiI", [D, T])
    xnI = dram("xnI", [D, T])
    selI = dram("selI", [8, T])
    wsi = dram("wsi", [D, 4096])
    wso = dram("wso", [2048, D])
    w1 = dram("w1", [D, 4096])
    w2 = dram("w2", [8, 2048, D])
    n2w = dram("n2w", [128, NCH])
    onesC = dram("onesC", [128, 128], BF16)
    ehot8C = dram("ehot8C", [8, 8 * 128], BF16)
    xout = nc.dram_tensor("xout", [D, T], F32, kind="ExternalOutput").ap()

    TH = T // 2
    sb = lambda name, shape, dt: es.enter_context(nc.sbuf_tensor(name, shape, dt))
    H = {}
    H["XT"] = sb("XT", [128, T], F32)
    H["ROWS"] = sb("ROWS", [1, 2 * T + 520], F32)
    H["ROWSBF"] = sb("ROWSBF", [1, TT], BF16)
    ONES = sb("ONESs", [128, 128], BF16); H["ONES"] = ONES
    EHOT8 = sb("EHOT8", [8, 8 * 128], BF16)
    XNB = sb("XNB", [128, NCH * T], BF16)
    SEL8 = sb("SEL8", [8, T], BF16)
    ARENA = sb("ARENA", [128, 16 * 1024], BF16)
    HH1 = sb("HH1", [128, 16 * 1024], BF16)
    OACC = sb("OACC", [128, NCH * 1024], F32)
    H["WST"] = H["XT"][:, 0:1024]
    WBF = sb("WBF", [128, 16 * 512], BF16)
    N2W = sb("N2W", [128, NCH], F32)
    WCBB = sb("WCBB", [128, TH], BF16)
    WCB8 = sb("WCB8", [128, 8 * TH], BF16)
    SX = sb("SX", [128, TH], BF16)
    HX = sb("HX", [128, TH], BF16)
    XSQ = sb("XSQ", [128, TH], BF16)
    RBB = sb("RBB", [128, TH], F32)
    OT = H["XT"][:, 1024:1536]
    XRL = H["XT"][:, 1536:2048]
    PS = [es.enter_context(nc.psum_tensor(f"ps{i}", [128, 512], F32))[:, :] for i in range(8)]
    H["PS"] = PS

    P = Prog(nc, es)
    ld = lambda dst, srcap, key: P.op(
        "sync", lambda s, dst=dst, srcap=srcap: s.dma_start(out=dst, in_=srcap),
        writes=[key], dma=True)
    ld(ONES[:], onesC, "ONES")
    ld(EHOT8[:], ehot8C, "EHOT8")
    ld(N2W[:], n2w, "N2W")
    for c in range(NCH):
        P.op("sync", lambda s, c=c: s.dma_start(out=H["XT"][:], in_=xnI[c * 128:(c + 1) * 128, :]),
             reads=["XT"], writes=["XT"], dma=True)
        P.op("vector", lambda v, c=c: v.tensor_copy(XNB[:, c * T:(c + 1) * T], H["XT"][:]),
             reads=["XT"], writes=[("XNB", c)])
    P.op("sync", lambda s: s.dma_start(out=H["XT"][0:8, 0:T], in_=selI), reads=["XT"], writes=["XT"], dma=True)
    P.op("vector", lambda v: v.tensor_copy(SEL8[:], H["XT"][0:8, 0:T]), reads=["XT"], writes=["SEL8"])

    def swiglu_tiles(P, wA, wAkey, OUTBUF, outkey, mo):
        for dj in range(16):
            for part, co in (("a", dj * 128), ("b", 2048 + dj * 128)):
                for ci in range(NCH):
                    stg = H["WST"][:, 0:128] if part == "a" else H["WST"][:, 128:256]
                    woff = (ci if part == "a" else NCH + ci) * 128
                    P.op("sync", lambda s, ci=ci, co=co, stg=stg: s.dma_start(
                        out=stg, in_=wA[ci * 128:(ci + 1) * 128, co:co + 128]),
                        reads=["XT"], writes=["XT"], dma=True)
                    P.op("vector", lambda v, stg=stg, woff=woff: v.tensor_copy(
                        WBF[:, woff:woff + 128], stg),
                        reads=["XT"], writes=[("W", part, ci)])
            pa = PS[dj % 2]
            pb = PS[2 + dj % 2]
            for ci in range(NCH):
                P.op("tensor", lambda t, pa=pa, ci=ci, mo=mo: t.matmul(
                    pa[:, :], WBF[:, ci * 128:(ci + 1) * 128],
                    XNB[:, ci * T + mo: ci * T + mo + 512], start=(ci == 0), stop=(ci == NCH - 1)),
                    reads=[("W", "a", ci), ("XNB", ci)], writes=[("ps", dj % 2)])
                P.op("tensor", lambda t, pb=pb, ci=ci, mo=mo: t.matmul(
                    pb[:, :], WBF[:, (NCH + ci) * 128:(NCH + ci + 1) * 128],
                    XNB[:, ci * T + mo: ci * T + mo + 512], start=(ci == 0), stop=(ci == NCH - 1)),
                    reads=[("W", "b", ci), ("XNB", ci)], writes=[("ps", 2 + dj % 2)])
            pa2 = PS[4 + dj % 2]
            pb2 = PS[6 + dj % 2]
            for ci in range(NCH):
                P.op("tensor", lambda t, pa2=pa2, ci=ci, mo=mo: t.matmul(
                    pa2[:, :], WBF[:, ci * 128:(ci + 1) * 128],
                    XNB[:, ci * T + mo + 512: ci * T + mo + 1024], start=(ci == 0), stop=(ci == NCH - 1)),
                    reads=[("W", "a", ci), ("XNB", ci)], writes=[("ps", 4 + dj % 2)])
                P.op("tensor", lambda t, pb2=pb2, ci=ci, mo=mo: t.matmul(
                    pb2[:, :], WBF[:, (NCH + ci) * 128:(NCH + ci + 1) * 128],
                    XNB[:, ci * T + mo + 512: ci * T + mo + 1024], start=(ci == 0), stop=(ci == NCH - 1)),
                    reads=[("W", "b", ci), ("XNB", ci)], writes=[("ps", 6 + dj % 2)])
            for half2, (pA, pB, kA, kB) in enumerate(((pa, pb, dj % 2, 2 + dj % 2),
                                                      (pa2, pb2, 4 + dj % 2, 6 + dj % 2))):
                P.op("scalar", lambda sc, pA=pA, half2=half2: sc.activation(
                    SX[:, half2 * 512:(half2 + 1) * 512], pA[:, :], AF.Silu),
                    reads=[("ps", kA)], writes=["SX"])
                P.op("vector", lambda v, pB=pB, half2=half2: v.tensor_copy(
                    HX[:, half2 * 512:(half2 + 1) * 512], pB[:, :]),
                    reads=[("ps", kB)], writes=["HX"])
            P.op("vector", lambda v, dj=dj: v.tensor_mul(
                OUTBUF[:, dj * 1024:(dj + 1) * 1024], SX[:], HX[:]),
                reads=["SX", "HX"], writes=[(outkey, dj)])

    for mh in range(2):
        mo = mh * TH
        swiglu_tiles(P, wsi, "wsi", ARENA, "SHI", mo)
        for doh in range(2):
            for dj in range(16):
                P.op("sync", lambda s, dj=dj, doh=doh: s.dma_start(
                    out=H["WST"][:, 0:512], in_=wso[dj * 128:(dj + 1) * 128, doh * 512:(doh + 1) * 512]),
                    reads=["XT"], writes=["XT"], dma=True)
                P.op("vector", lambda v, dj=dj: v.tensor_copy(WBF[:, 0:512], H["WST"][:, 0:512]),
                     reads=["XT"], writes=[("W2s",)])
                for do4 in range(4):
                    for t2 in range(2):
                        pi = do4 * 2 + t2
                        P.op("tensor", lambda t, pi=pi, dj=dj, doh=doh, do4=do4, t2=t2: t.matmul(
                            PS[pi][:, :], WBF[:, 0:512][:, do4 * 128:(do4 + 1) * 128],
                            ARENA[:, dj * 1024 + t2 * 512: dj * 1024 + (t2 + 1) * 512],
                            start=(dj == 0), stop=(dj == 15)),
                            reads=[("W2s",), ("SHI", dj)], writes=[("ps", pi)])
            for do4 in range(4):
                for t2 in range(2):
                    pi = do4 * 2 + t2
                    do = doh * 4 + do4
                    P.op("vector", lambda v, pi=pi, do=do, t2=t2: v.tensor_copy(
                        OACC[:, do * 1024 + t2 * 512: do * 1024 + (t2 + 1) * 512], PS[pi][:, :]),
                        reads=[("ps", pi)], writes=[("OACC", do)])
        swiglu_tiles(P, w1, "w1", HH1, "HH1", mo)
        for e in range(8):
            for t2 in range(2):
                P.op("tensor", lambda t, e=e, mo=mo, t2=t2: t.matmul(
                    PS[0][:, :], EHOT8[:, e * 128:(e + 1) * 128],
                    SEL8[0:8, mo + t2 * 512:mo + (t2 + 1) * 512],
                    start=True, stop=True), reads=["EHOT8", "SEL8"], writes=[("ps", 0)])
                P.op("vector", lambda v, e=e, t2=t2: v.tensor_copy(
                    WCB8[:, e * TH + t2 * 512: e * TH + (t2 + 1) * 512], PS[0][:, :]),
                    reads=[("ps", 0)], writes=["WCB8"])
        for doh in range(2):
            for e in range(8):
                for dj in range(16):
                    P.op("vector", lambda v, dj=dj, e=e: v.tensor_mul(
                        ARENA[:, dj * 1024:(dj + 1) * 1024], HH1[:, dj * 1024:(dj + 1) * 1024],
                        WCB8[:, e * TH:(e + 1) * TH]),
                        reads=[("HH1", dj), "WCB8"], writes=[("SHI", dj)])
                    P.op("sync", lambda s, dj=dj, doh=doh, e=e: s.dma_start(
                        out=H["WST"][:, 0:512], in_=w2[e, dj * 128:(dj + 1) * 128, doh * 512:(doh + 1) * 512]),
                        reads=["XT"], writes=["XT"], dma=True)
                    P.op("vector", lambda v: v.tensor_copy(WBF[:, 0:512], H["WST"][:, 0:512]),
                         reads=["XT"], writes=[("W2s",)])
                    for do4 in range(4):
                        for t2 in range(2):
                            pi = do4 * 2 + t2
                            P.op("tensor", lambda t, pi=pi, dj=dj, do4=do4, t2=t2, e=e: t.matmul(
                                PS[pi][:, :], WBF[:, 0:512][:, do4 * 128:(do4 + 1) * 128],
                                ARENA[:, dj * 1024 + t2 * 512: dj * 1024 + (t2 + 1) * 512],
                                start=(e == 0 and dj == 0), stop=(e == 7 and dj == 15)),
                                reads=[("W2s",), ("SHI", dj)], writes=[("ps", pi)])
            for do4 in range(4):
                for t2 in range(2):
                    pi = do4 * 2 + t2
                    do = doh * 4 + do4
                    P.op("vector", lambda v, pi=pi, do=do, t2=t2: v.tensor_add(
                        OACC[:, do * 1024 + t2 * 512: do * 1024 + (t2 + 1) * 512],
                        OACC[:, do * 1024 + t2 * 512: do * 1024 + (t2 + 1) * 512], PS[pi][:, :]),
                        reads=[("ps", pi), ("OACC", do)], writes=[("OACC", do)])
        r0 = H["ROWS"][0:1, 0:TH]
        for dj in range(NCH):
            P.op("sync", lambda s, dj=dj, mo=mo: s.dma_start(
                out=H["XT"][:, 0:TH], in_=xnI[dj * 128:(dj + 1) * 128, mo:mo + TH]),
                reads=["XT"], writes=["XT"], dma=True)
            P.op("vector", lambda v, dj=dj: v.tensor_add(
                OACC[:, dj * 1024:(dj + 1) * 1024], OACC[:, dj * 1024:(dj + 1) * 1024],
                H["XT"][:, 0:TH]),
                reads=["XT", ("OACC", dj)], writes=[("OACC", dj)])
            P.op("scalar", lambda sc, dj=dj: sc.activation(
                XSQ[:], OACC[:, dj * 1024:(dj + 1) * 1024], AF.Square),
                reads=[("OACC", dj)], writes=["XSQ"])
            for t2 in range(2):
                P.op("tensor", lambda t, dj=dj, t2=t2: t.matmul(
                    PS[t2][0:1, :], ONES[:, 0:1], XSQ[:, t2 * 512:(t2 + 1) * 512],
                    start=(dj == 0), stop=(dj == NCH - 1)),
                    reads=["XSQ", "ONES"], writes=[("ps", t2)])
        for t2 in range(2):
            P.op("vector", lambda v, t2=t2: v.tensor_scalar_mul(H["ROWSBF"][0:1, 0:TT], PS[t2][0:1, :], 1.0 / D),
                 reads=[("ps", t2)], writes=["rowsbf"])
            P.op("tensor", lambda t, t2=t2: t.matmul(PS[t2][:, :], ONES[0:1, :], H["ROWSBF"][0:1, 0:TT],
                                                     start=True, stop=True),
                 reads=["rowsbf", "ONES"], writes=[("ps", t2)])
            P.op("vector", lambda v, t2=t2: v.tensor_copy(RBB[:, t2 * 512:(t2 + 1) * 512], PS[t2][:, :]),
                 reads=[("ps", t2)], writes=["RBB2"])
        P.op("vector", lambda v: v.tensor_scalar_add(RBB[:], RBB[:], 1e-6), reads=["RBB2"], writes=["RBB2"])
        P.op("scalar", lambda sc: sc.activation(RBB[:], RBB[:], AF.Ln), reads=["RBB2"], writes=["RBB2"])
        P.op("scalar", lambda sc: sc.activation(RBB[:], RBB[:], AF.Exp, scale=-0.5),
             reads=["RBB2"], writes=["RBB2"])
        for dj in range(NCH):
            for t2 in range(2):
                sl = slice(dj * 1024 + t2 * 512, dj * 1024 + (t2 + 1) * 512)
                P.op("sync", lambda s, dj=dj, t2=t2, mo=mo: s.dma_start(
                    out=XRL, in_=xin[dj * 128:(dj + 1) * 128, mo + t2 * 512:mo + (t2 + 1) * 512]),
                    reads=["XT"], writes=["XT"], dma=True)
                P.op("vector", lambda v, sl=sl, t2=t2: v.tensor_mul(
                    OT, OACC[:, sl], RBB[:, t2 * 512:(t2 + 1) * 512]),
                    reads=[("OACC", sl.start // 1024), "RBB2", "XT"], writes=["XT"])
                P.op("vector", lambda v, dj=dj: v.tensor_scalar(OT, OT, N2W[:, dj:dj + 1], None, ALU.mult),
                     reads=["XT", "N2W"], writes=["XT"])
                P.op("vector", lambda v: v.tensor_add(OT, OT, XRL), reads=["XT"], writes=["XT"])
                P.op("gpsimd", lambda g, dj=dj, t2=t2, mo=mo: g.dma_start(
                    out=xout[dj * 128:(dj + 1) * 128, mo + t2 * 512:mo + (t2 + 1) * 512], in_=OT),
                    reads=["XT"], writes=[("xout", dj, mh, t2)], dma=True)

    P.barrier_all_dma_done("gpsimd")
    with nc.Block() as block:
        P.finalize(block)
    es.close()
    return nc


def l1_ffn_host_inputs(INP, ones, ident):
    H, DH = 16, 64
    pos = np.arange(T, dtype=np.float64)
    inv = 1.0 / 10000.0 ** (np.arange(0, DH, 2, dtype=np.float64) / DH)
    ang = pos[:, None] * inv[None, :]
    cosr = np.repeat(np.cos(ang), 2, axis=1).T
    sinr = np.repeat(np.sin(ang), 2, axis=1).T
    Cf = np.concatenate([cosr, cosr], 0).astype(np.float32)
    Df = np.concatenate([sinr, sinr], 0).astype(np.float32)
    R = np.zeros((64, 64), np.float32)
    for i in range(32):
        R[2 * i, 2 * i + 1] = -1.0
        R[2 * i + 1, 2 * i] = 1.0
    R2 = np.zeros((128, 128), np.float32)
    R2[0:64, 0:64] = R
    R2[64:128, 64:128] = R
    wg10 = np.concatenate([INP["L1_ffn_Wg_gate"], INP["L1_ffn_We_gate"]], axis=1)
    big10 = np.tile(np.concatenate([INP["L1_ffn_group_bias"], INP["L1_ffn_expert_bias"]]), (128, 1)).astype(np.float32)
    ehot8 = np.zeros((8, 8 * 128), dtype=ml_dtypes.bfloat16)
    for e in range(8):
        ehot8[e, e * 128:(e + 1) * 128] = 1
    dA = {
        "wq": np.ascontiguousarray(INP["L1_ffn_Wq"]),
        "wkc": np.ascontiguousarray(INP["L1_ffn_Wk_c"]),
        "wk": np.ascontiguousarray(INP["L1_ffn_Wk"]),
        "wvc": np.ascontiguousarray(INP["L1_ffn_Wv_c"]),
        "wv": np.ascontiguousarray(INP["L1_ffn_Wv"]),
        "wo": np.ascontiguousarray(INP["L1_ffn_Wo"]),
        "n1w": np.ascontiguousarray(INP["L1_ffn_norm1_w"].reshape(NCH, 128).T),
        "ropeC": Cf, "ropeD": Df,
        "rt": np.ascontiguousarray(R2.T).astype(ml_dtypes.bfloat16),
        "wg10": np.ascontiguousarray(wg10),
        "big10": big10,
        "onesC": ones, "identC": ident,
    }
    dB = {
        "wsi": np.ascontiguousarray(INP["L1_ffn_W_shared_in"]),
        "wso": np.ascontiguousarray(INP["L1_ffn_W_shared_out"]),
        "w1": np.ascontiguousarray(INP["L1_ffn_W1_shared"]),
        "w2": np.ascontiguousarray(INP["L1_ffn_W2_experts"]),
        "n2w": np.ascontiguousarray(INP["L1_ffn_norm2_w"].reshape(NCH, 128).T),
        "onesC": ones,
        "ehot8C": ehot8,
    }
    return dA, dB


_PROG_CACHE = {}


def kernel(**inputs):
    INP = inputs
    ones, ident, masks = host_constants()
    x = np.asarray(INP["L0_attn_a_x"])
    xs = [np.ascontiguousarray(x[b].T) for b in range(NCORES)]

    def get(name, builder, *a):
        if name not in _PROG_CACHE:
            _PROG_CACHE[name] = builder(*a)
        return _PROG_CACHE[name]

    def run(nc, maps):
        res = run_bass_kernel_spmd(nc, maps, core_ids=list(range(NCORES)))
        return res.results

    nc = get("l0aa", build_l0_attn_a)
    shared = {
        "wq": np.ascontiguousarray(INP["L0_attn_a_Wq"].T),
        "wk": np.ascontiguousarray(INP["L0_attn_a_Wk"].T),
        "wv": np.ascontiguousarray(INP["L0_attn_a_Wv"].T),
        "onesC": ones, "masksC": masks,
    }
    rs = run(nc, [dict(shared, xin=xs[b]) for b in range(NCORES)])
    xs = [np.asarray(r["xout"]) for r in rs]

    pattern = sorted(set(int(t) for t in INP["L0_ffn_index_t"]))
    npad = (-len(pattern)) % 512
    pattern = pattern + [pattern[-1]] * 0 if npad == 0 else pattern
    assert len(pattern) % 512 == 0, "pattern padding not implemented"
    ncf = get(("l0ffn", tuple(pattern)), build_l0_ffn, tuple(pattern))
    sharedf = l0_ffn_host_inputs(INP, pattern, ones, ident)
    masks_b = l0_ffn_masks(INP, pattern)
    rs = run(ncf, [dict(sharedf, xin=xs[b], mskt=masks_b[b]) for b in range(NCORES)])
    xs = [np.asarray(r["xout"]) for r in rs]

    ncb = get("l0ab", build_l0_attn_b)
    sharedb = {
        "wv": np.ascontiguousarray(INP["L0_attn_b_Wv"].T),
        "bv": np.ascontiguousarray(np.asarray(INP["L0_attn_b_bv"]).reshape(NCH, 128).T),
        "onesC": ones,
    }
    rs = run(ncb, [dict(sharedb, xin=xs[b]) for b in range(NCORES)])
    xs = [np.asarray(r["xout"]) for r in rs]

    rs = run(ncf, [dict(sharedf, xin=xs[b], mskt=masks_b[b]) for b in range(NCORES)])
    xs = [np.asarray(r["xout"]) for r in rs]

    nca = get("l1aa", build_l1_attn_a)
    shareda = {
        "wqt": np.ascontiguousarray(INP["L1_attn_a_Wq"]),
        "wkt": np.ascontiguousarray(INP["L1_attn_a_Wk"]),
        "wvt": np.ascontiguousarray(INP["L1_attn_a_Wv"]),
        "wot": np.ascontiguousarray(INP["L1_attn_a_Wo"]),
        "bq": np.ascontiguousarray(np.asarray(INP["L1_attn_a_bq"]).reshape(NCH, 128).T),
        "bk": np.ascontiguousarray(np.asarray(INP["L1_attn_a_bk"]).reshape(NCH, 128).T),
        "bvrow": np.ascontiguousarray(np.asarray(INP["L1_attn_a_bv"]).reshape(1, D)),
        "bo": np.ascontiguousarray(np.asarray(INP["L1_attn_a_bo"]).reshape(NCH, 128).T),
        "onesC": ones, "identC": ident,
    }
    rs = run(nca, [dict(shareda, xin=xs[b]) for b in range(NCORES)])
    xs = [np.asarray(r["xout"]) for r in rs]

    dA, dB = l1_ffn_host_inputs(INP, ones, ident)
    ncfa = get("l1fa", build_l1_ffn_a)
    ncfb = get("l1fb", build_l1_ffn_b)

    def l1ffn(xs):
        rsA = run(ncfa, [dict(dA, xin=xs[b]) for b in range(NCORES)])
        mapsB = []
        for b in range(NCORES):
            d = dict(dB, xin=xs[b])
            d["xiI"] = np.asarray(rsA[b]["oXI"])
            d["xnI"] = np.asarray(rsA[b]["oXN"])
            d["selI"] = np.asarray(rsA[b]["oSEL"])
            mapsB.append(d)
        rsB = run(ncfb, mapsB)
        return [np.asarray(r["xout"]) for r in rsB]

    xs = l1ffn(xs)

    ncab = get("l1ab", build_l1_attn_b)
    sharedab = l1_attn_b_host_inputs(INP, ones, masks)
    rs = run(ncab, [dict(sharedab, xin=xs[b]) for b in range(NCORES)])
    xs = [np.asarray(r["xout"]) for r in rs]

    xs = l1ffn(xs)

    out = np.stack([xs[b].T for b in range(NCORES)]).astype(np.float32)
    return out
    raise NotImplementedError("under construction")
    raise NotImplementedError("under construction")
    raise NotImplementedError("under construction")
    raise NotImplementedError("under construction")




class Pool:

    def __init__(self, ap, width):
        self.ap = ap
        self.width = width
        self.off = 0

    def reset(self):
        self.off = 0

    def carve(self, p, n, dt):
        w = n if dt == BF16 else 2 * n
        self.off = (self.off + 1) & ~1
        v = self.ap[0:p, self.off:self.off + w]
        self.off += w
        assert self.off <= self.width, self.off
        if dt == F32:
            v = v.bitcast(F32)
        return v

def emit_l0aa(P, H, W, pool):
    XT = H["XT"]; ROWS = H["ROWS"]; RQK = H["RQK"]; RBF = H["ROWSBF"]
    ONES = H["ONES"]; IDENT = H["IDENT"]; WBF = H["WBF"]; PS = H["PS"]
    pool.reset()
    debug = False
    dR2 = dBA = dQ = dRows = dEXP = dG9H = dGL = dGLpre = None
    dGS = dSEL9 = dWC9 = dYMH = dZH = None
    xin = W["xin"]
    wq = W["wq"]
    wk = W["wk"]
    wv = W["wv"]
    onesC = W["onesC"]
    masksC = W["masksC"]
    xout = W["xout"]
    YB = H["YB"]
    Y = YB[:, :]
    ET = YB[:, 0:16 * TT]
    MASKS = YB[:, 16 * TT:16 * TT + 2048]
    ZIB = YB[:, 16 * TT + 2048:16 * TT + 2560]
    Q = H["QB"][:, :]; K = H["KB"][:, :]; V = H["VB"][:, :]
    BA = pool.carve(128, 1, F32)
    OTMP = pool.carve(128, TT, F32)
    SC8B = pool.carve(1, 8, F32)
    XRL = XT[:, 1024:2048]
    P.op("sync", lambda s: s.dma_start(out=ONES[:], in_=onesC), writes=["ONES"], dma=True)

    emit_ln(P, H, xin, "xin", Y, "Y")

    emit_load_weight_bf16(P, H, wq, "wq", WBF, "W", D, D)
    emit_proj_fm(P, H, WBF, "W", Y, "Y", Q, "Q", NCH, D)
    emit_load_weight_bf16(P, H, wk, "wk", WBF, "W", D, D)
    emit_proj_fm(P, H, WBF, "W", Y, "Y", K, "K", NCH, D)
    emit_load_weight_bf16(P, H, wv, "wv", WBF, "W", D, D)
    for tc in range(16):
        for dh in range(2):
            pi = (tc * 2 + dh) % 2
            ps = PS[pi]
            for ci in range(NCH):
                P.op("tensor", lambda t, ps=ps, ci=ci, tc=tc, dh=dh: t.matmul(
                    ps[:, :], Y[:, ci * T + tc * 128: ci * T + (tc + 1) * 128],
                    WBF[:, ci * D + dh * 512: ci * D + (dh + 1) * 512],
                    start=(ci == 0), stop=(ci == NCH - 1)),
                    reads=[("W", ci), ("Y", ci)], writes=[("ps", pi)])
            P.op("vector", lambda v, ps=ps, tc=tc, dh=dh: v.tensor_copy(
                V[:, tc * D + dh * 512: tc * D + (dh + 1) * 512], ps[:, :]),
                reads=[("ps", pi)], writes=[("V", tc)])

    ROWS = H["ROWS"]
    r0 = RQK[0:1, 0:T]
    r1 = RQK[0:1, T:2 * T]
    r2 = SC8B
    for idx, (SRC, skey, dst) in enumerate(((Q, "Q", r0), (K, "K", r1))):
        for c in range(NCH):
            P.op("scalar", lambda sc, c=c, SRC=SRC: sc.activation(
                H["XSQB"][:], SRC[:, c * T:(c + 1) * T], AF.Square),
                reads=[(skey, c)], writes=["XSQB"])
            for tt in range(NTT):
                pi = 4 + tt
                P.op("tensor", lambda t, tt=tt, c=c, pi=pi: t.matmul(
                    PS[pi][0:1, :], ONES[:, 0:1], H["XSQB"][:, tt * TT:(tt + 1) * TT],
                    start=(c == 0), stop=(c == NCH - 1)),
                    reads=["XSQB", "ONES"], writes=[("ps", pi)])
        for tt in range(NTT):
            pi = 4 + tt
            P.op("vector", lambda v, tt=tt, pi=pi, dst=dst: v.tensor_copy(
                dst[0:1, tt * TT:(tt + 1) * TT], PS[pi][0:1, :]),
                reads=[("ps", pi)], writes=["rows"])
    P.op("vector", lambda v: v.reduce_max(r2[0:1, 0:1], r0, axis=AX.X), reads=["rows"], writes=["rows"])
    P.op("vector", lambda v: v.reduce_max(r2[0:1, 1:2], r1, axis=AX.X), reads=["rows"], writes=["rows"])
    P.op("vector", lambda v: v.tensor_add(r2[0:1, 2:3], r2[0:1, 0:1], r2[0:1, 1:2]),
         reads=["rows"], writes=["rows"])
    P.op("vector", lambda v: v.tensor_scalar_mul(r2[0:1, 3:4], r2[0:1, 2:3], -1.0 / 64.0),
         reads=["rows"], writes=["rows"])
    RBF = H["ROWSBF"]
    P.op("vector", lambda v: v.tensor_copy(RBF[0:1, 0:1], r2[0:1, 3:4]), reads=["rows"], writes=["rowsbf"])
    P.op("tensor", lambda t: t.matmul(PS[7][:, 0:1], ONES[0:1, :], RBF[0:1, 0:1], start=True, stop=True),
         reads=["rowsbf", "ONES"], writes=[("ps", 7)])
    P.op("vector", lambda v: v.tensor_copy(BA[:], PS[7][:, 0:1]), reads=[("ps", 7)], writes=["BA"])
    if debug:
        P.op("gpsimd", lambda g: g.dma_start(out=dR2, in_=r2), reads=["rows"], writes=[("dump",)], dma=True)
        P.op("gpsimd", lambda g: g.dma_start(out=dBA, in_=BA[:]), reads=["BA"], writes=[("dump",)], dma=True)
        P.op("vector", lambda v: v.tensor_copy(H["XT"][:], Q[:, 0:T]), reads=[("Q", 0)], writes=["XT"])
        P.op("gpsimd", lambda g: g.dma_start(out=dQ, in_=H["XT"][:]), reads=["XT"], writes=[("dump",)], dma=True)
        P.op("gpsimd", lambda g: g.dma_start(out=dRows, in_=ROWS[0:1, 0:2 * T]), reads=["rows"], writes=[("dump",)], dma=True)

    P.fence("scalar", reads=[("Y", c) for c in range(NCH)],
            writes=[("ET", kc) for kc in range(16)] + ["MASKS", "ZIB"])
    P.op("sync", lambda s: s.dma_start(
        out=MASKS.rearrange("p (m n) -> p m n", m=4), in_=masksC.rearrange("m p n -> p m n")),
        writes=["MASKS"], dma=True)

    scale = 1.0 / 32.0
    zr = ROWS[0:1, 8:8 + TT]
    for qt in range(NTT):
        nkc = min(16, 4 * qt + 4)
        for kc in range(nkc):
            pi = kc % 2
            ps = PS[pi]
            for ci in range(NCH):
                P.op("tensor", lambda t, ps=ps, ci=ci, kc=kc, qt=qt: t.matmul(
                    ps[:, :], K[:, ci * T + kc * 128: ci * T + (kc + 1) * 128],
                    Q[:, ci * T + qt * TT: ci * T + (qt + 1) * TT],
                    start=(ci == 0), stop=(ci == NCH - 1)),
                    reads=[("K", ci), ("Q", ci)], writes=[("ps", pi)])
            P.op("scalar", lambda sc, ps=ps, kc=kc: sc.activation(
                ET[:, kc * TT:(kc + 1) * TT], ps[:, :], AF.Exp, bias=BA[:], scale=scale),
                reads=[("ps", pi), "BA"], writes=[("ET", kc)])
            m = kc - 4 * qt
            if m >= 0:
                P.op("vector", lambda v, kc=kc, m=m: v.tensor_mul(
                    ET[:, kc * TT:(kc + 1) * TT], ET[:, kc * TT:(kc + 1) * TT],
                    MASKS[:, m * 512:(m + 1) * 512]),
                    reads=[("ET", kc), "MASKS"], writes=[("ET", kc)])
        for kc in range(nkc):
            P.op("tensor", lambda t, kc=kc, nkc=nkc: t.matmul(
                PS[6][0:1, :], ONES[:, 0:1], ET[:, kc * TT:(kc + 1) * TT],
                start=(kc == 0), stop=(kc == nkc - 1)),
                reads=[("ET", kc), "ONES"], writes=[("ps", 6)])
        P.op("vector", lambda v: v.tensor_copy(zr, PS[6][0:1, :]), reads=[("ps", 6)], writes=["zrow"])
        P.op("vector", lambda v: v.reciprocal(zr, zr), reads=["zrow"], writes=["zrow"])
        P.op("vector", lambda v: v.tensor_copy(RBF[0:1, 0:TT], zr), reads=["zrow"], writes=["rowsbf"])
        P.op("tensor", lambda t: t.matmul(PS[6][:, :], ONES[0:1, :], RBF[0:1, 0:TT],
                                          start=True, stop=True),
             reads=["rowsbf", "ONES"], writes=[("ps", 6)])
        P.op("vector", lambda v: v.tensor_copy(ZIB[:], PS[6][:, :]), reads=[("ps", 6)], writes=["ZIB"])
        for kc in range(nkc):
            P.op("vector", lambda v, kc=kc: v.tensor_mul(
                ET[:, kc * TT:(kc + 1) * TT], ET[:, kc * TT:(kc + 1) * TT], ZIB[:]),
                reads=[("ET", kc), "ZIB"], writes=[("ET", kc)])
        for dj in range(NCH):
            pi = 2 + dj % 2
            ps = PS[pi]
            for kc in range(nkc):
                P.op("tensor", lambda t, ps=ps, kc=kc, dj=dj, nkc=nkc: t.matmul(
                    ps[:, :], V[:, kc * D + dj * 128: kc * D + (dj + 1) * 128],
                    ET[:, kc * TT:(kc + 1) * TT],
                    start=(kc == 0), stop=(kc == nkc - 1)),
                    reads=[("V", kc), ("ET", kc)], writes=[("ps", pi)])
            b = dj % 2
            xrl = XRL[:, b * TT:(b + 1) * TT]
            P.op("sync", lambda s, dj=dj, qt=qt, xrl=xrl: s.dma_start(
                out=xrl, in_=xin[dj * 128:(dj + 1) * 128, qt * TT:(qt + 1) * TT]),
                reads=["xin"], writes=["XT"], dma=True)
            P.op("vector", lambda v, ps=ps, xrl=xrl: v.tensor_add(OTMP[:], ps[:, :], xrl),
                 reads=[("ps", pi), "XT"], writes=["OTMP"])
            P.op("gpsimd", lambda g, dj=dj, qt=qt: g.dma_start(
                out=xout[dj * 128:(dj + 1) * 128, qt * TT:(qt + 1) * TT], in_=OTMP[:]),
                reads=["OTMP"], writes=[("xout", dj, qt)], dma=True)



def emit_l0ffn(P, H, W, pool, pattern):
    XT = H["XT"]; ROWS = H["ROWS"]; RQK = H["RQK"]; RBF = H["ROWSBF"]
    ONES = H["ONES"]; IDENT = H["IDENT"]; WBF = H["WBF"]; PS = H["PS"]
    pool.reset()
    debug = False
    dR2 = dBA = dQ = dRows = dEXP = dG9H = dGL = dGLpre = None
    dGS = dSEL9 = dWC9 = dYMH = dZH = None
    m = len(pattern); nmt = m // 128; nmh = m // 512
    assert 4096 + 6 * m <= 10240, m
    xin = W["xin"]
    xout = W["xout"]
    sw1t = W["sw1t"]
    sw2t = W["sw2t"]
    sw3t = W["sw3t"]
    sb1c = W["sb1c"]
    sb2c = W["sb2c"]
    sb3c = W["sb3c"]
    slng = W["slng"]
    slnb = W["slnb"]
    g9 = W["g9"]
    big9 = W["big9"]
    expb = W["expb"]
    ew1t = W["ew1t"]
    ew2t = W["ew2t"]
    ew3t = W["ew3t"]
    eb1c = W["eb1c"]
    eb2c = W["eb2c"]
    eb3r = W["eb3r"]
    selg = W["selg"]
    selb = W["selb"]
    mskt = W["mskt"]
    onesC = W["onesC"]
    identC = W["identC"]
    ehotC = W["ehotC"]
    YB = H["YB"]
    Y = YB[:, :]
    ZH = YB[:, 0:16384]
    SH = H["QB"][:, 0:4 * T]
    KBt = H["KB"]
    YMH = KBt[:, 0:NCH * m]
    YML = KBt[:, 8192:8192 + NCH * m]
    EXPOUT = YML
    VBt = H["VB"]
    XM32 = VBt[:, 0:2048].bitcast(F32)
    XM32b = VBt[:, 2048:4096].bitcast(F32)
    MBG = VBt[:, 4096:4096 + 2 * m].bitcast(F32)
    RBG = VBt[:, 4096 + 2 * m:4096 + 4 * m].bitcast(F32)
    SW1B = VBt[:, 4096 + 4 * m:4096 + 6 * m].bitcast(F32)
    OT1 = VBt[:, 10240:11264].bitcast(F32)
    OT2 = VBt[:, 11264:12288].bitcast(F32)
    GT1 = VBt[:, 12288:13312].bitcast(F32)
    GT2 = VBt[:, 13312:14336].bitcast(F32)
    EHOT = pool.carve(9, 9 * 128, BF16)
    SX = pool.carve(128, TT, BF16)
    HX = pool.carve(128, TT, BF16)
    SEL9 = pool.carve(9, m, BF16)
    GL = pool.carve(128, 9 * nmt, F32)
    GS = pool.carve(128, 16, F32)
    GS2 = pool.carve(128, 8, F32)
    WC9 = pool.carve(128, 9, BF16)
    MSK = pool.carve(128, nmt, F32)
    WCBB = pool.carve(128, 512, BF16)
    G9H = pool.carve(128, NCH * 9, BF16)
    G9L = pool.carve(128, NCH * 9, BF16)
    BIG9 = pool.carve(128, 9, F32)
    EXPB = pool.carve(128, 8, F32)
    EB1 = pool.carve(128, 32, F32)
    EB2 = pool.carve(128, 32, F32)
    EB3B = pool.carve(8, D, BF16)
    SB1 = pool.carve(128, 4, F32)
    SB2 = pool.carve(128, 4, F32)
    SB3 = pool.carve(128, NCH, F32)
    SLNG = pool.carve(128, 4, F32)
    SLNB = pool.carve(128, 4, F32)
    SELG = pool.carve(128, 4, F32)
    SELB = pool.carve(128, 4, F32)
    XRL = XT[:, 1024:2048]
    ld = lambda dst, srcap, key: P.op(
        "sync", lambda s, dst=dst, srcap=srcap: s.dma_start(out=dst, in_=srcap),
        writes=[key], dma=True)
    ld(ONES[:], onesC, "ONES")
    ld(IDENT[:], identC, "IDENT")
    ld(EHOT[:], ehotC, "EHOT")
    ld(BIG9[:], big9, "BIG9")
    ld(EXPB[:], expb, "EXPB")
    ld(EB1[:], eb1c, "EB1")
    ld(EB2[:], eb2c, "EB2")
    ld(SB1[:], sb1c, "SB1")
    ld(SB2[:], sb2c, "SB2")
    ld(SB3[:], sb3c, "SB3")
    ld(SLNG[:], slng, "SLNG")
    ld(SLNB[:], slnb, "SLNB")
    ld(SELG[:], selg, "SELG")
    ld(SELB[:], selb, "SELB")
    ld(MSK[:], mskt, "MSK")
    P.op("sync", lambda s: s.dma_start(out=XM32[0:8, 0:D], in_=eb3r), writes=["XM32"], dma=True)
    P.op("vector", lambda v: v.tensor_copy(EB3B[:], XM32[0:8, 0:D]), reads=["XM32"], writes=["EB3B"])

    emit_ln(P, H, xin, "xin", Y, "Y")

    for srctile, skey2, dstt, dkey in ((H["MBB"], "MBB", MBG, "MBG"),
                                       (H["RBB"], "RBB", RBG, "RBG")):
        off = 0
        for (st, sp, n) in _runs(pattern):
            P.op("vector", lambda v, st=st, sp=sp, n=n, off=off, srctile=srctile,
                 dstt=dstt: v.tensor_copy(
                dstt[:, off:off + n],
                srctile[:, st:st + sp * n:sp] if sp > 1 else srctile[:, st:st + n]),
                reads=[skey2], writes=[dkey])
            off += n
    for c in range(NCH):
        P.op("sync", lambda s, c=c: s.dma_start(out=H["XT"][:], in_=xin[c * 128:(c + 1) * 128, :]),
             reads=["xin"], writes=["XT"], dma=True)
        off = 0
        for (st, sp, n) in _runs(pattern):
            P.op("vector", lambda v, st=st, sp=sp, n=n, off=off: v.tensor_copy(
                XM32[:, off:off + n],
                H["XT"][:, st:st + sp * n:sp] if sp > 1 else H["XT"][:, st:st + n]),
                reads=["XT"], writes=["XM32"])
            off += n
        P.op("vector", lambda v: v.tensor_sub(XM32[:, 0:m], XM32[:, 0:m], MBG[:, 0:m]),
             reads=["XM32", "MBG"], writes=["XM32"])
        P.op("vector", lambda v: v.tensor_mul(XM32[:, 0:m], XM32[:, 0:m], RBG[:, 0:m]),
             reads=["XM32", "RBG"], writes=["XM32"])
        P.op("vector", lambda v, c=c: v.tensor_copy(YMH[:, c * m:(c + 1) * m], XM32[:, 0:m]),
             reads=["XM32"], writes=[("YMH", c)])
        P.op("vector", lambda v, c=c: v.tensor_copy(XM32b[:, 0:m], YMH[:, c * m:(c + 1) * m]),
             reads=[("YMH", c)], writes=["XM32b"])
        P.op("vector", lambda v: v.tensor_sub(XM32[:, 0:m], XM32[:, 0:m], XM32b[:, 0:m]),
             reads=["XM32", "XM32b"], writes=["XM32"])
        P.op("vector", lambda v, c=c: v.tensor_copy(YML[:, c * m:(c + 1) * m], XM32[:, 0:m]),
             reads=["XM32"], writes=[("YML", c)])

    P.op("sync", lambda s: s.dma_start(
        out=XM32[:, 0:NCH * 9].rearrange("p (c n) -> p c n", c=NCH),
        in_=g9.rearrange("(c p) n -> p c n", p=128)),
        writes=["XM32"], dma=True)
    P.op("vector", lambda v: v.tensor_copy(G9H[:], XM32[:, 0:NCH * 9]), reads=["XM32"], writes=["G9H"])
    P.op("vector", lambda v: v.tensor_copy(XM32b[:, 0:NCH * 9], G9H[:]), reads=["G9H"], writes=["XM32b"])
    P.op("vector", lambda v: v.tensor_sub(XM32[:, 0:NCH * 9], XM32[:, 0:NCH * 9], XM32b[:, 0:NCH * 9]),
         reads=["XM32", "XM32b"], writes=["XM32"])
    P.op("vector", lambda v: v.tensor_copy(G9L[:], XM32[:, 0:NCH * 9]), reads=["XM32"], writes=["G9L"])
    for mt in range(nmt):
        pi = mt % 2
        ps = PS[pi]
        terms = [("YMH", G9H, "G9H"), ("YMH", G9L, "G9L"), ("YML", G9H, "G9H")]
        nsteps = NCH * len(terms)
        k = 0
        for c in range(NCH):
            for (Lk, R, Rk) in terms:
                L = YMH if Lk == "YMH" else YML
                P.op("tensor", lambda t, ps=ps, c=c, mt=mt, L=L, R=R, k=k: t.matmul(
                    ps[:, 0:9], L[:, c * m + mt * 128: c * m + (mt + 1) * 128],
                    R[:, c * 9:(c + 1) * 9], start=(k == 0), stop=(k == nsteps - 1)),
                    reads=[(Lk, c), Rk], writes=[("ps", pi)])
                k += 1
        P.op("vector", lambda v, ps=ps, mt=mt: v.tensor_copy(GL[:, mt * 9:(mt + 1) * 9], ps[:, 0:9]),
             reads=[("ps", pi)], writes=["GL"])
        if debug:
            P.op("gpsimd", lambda g, mt=mt: g.dma_start(
                out=dGLpre[:, mt * 9:(mt + 1) * 9], in_=GL[:, mt * 9:(mt + 1) * 9]),
                reads=["GL"], writes=[("dump",)], dma=True)
        P.op("vector", lambda v, mt=mt: v.tensor_add(
            GL[:, mt * 9:(mt + 1) * 9], GL[:, mt * 9:(mt + 1) * 9], BIG9[:]),
            reads=["GL", "BIG9"], writes=["GL"])
    P.op("scalar", lambda sc: sc.activation(GL[:], GL[:], AF.Sigmoid), reads=["GL"], writes=["GL"])
    for mt in range(nmt):
        g = GL[:, mt * 9:mt * 9 + 8]
        shs = GL[:, mt * 9 + 8:mt * 9 + 9]
        oh = GS[:, 0:8]
        mx = GS[:, 8:9]
        t1 = GS[:, 9:10]
        swc = GS[:, 10:11]
        ewc = GS[:, 11:12]
        P.op("vector", lambda v, g=g: v.tensor_add(oh, g, EXPB[:]), reads=["GL", "EXPB"], writes=["GS"])
        P.op("vector", lambda v: v.reduce_max(mx, oh, axis=AX.X), reads=["GS"], writes=["GS"])
        P.op("vector", lambda v: v.tensor_scalar(oh, oh, mx, None, ALU.is_ge),
             reads=["GS"], writes=["GS"])
        P.op("vector", lambda v, g=g: v.tensor_mul(GS2[:], oh, g), reads=["GS", "GL"], writes=["GS2"])
        P.op("vector", lambda v: v.reduce_sum(t1, GS2[:], axis=AX.X), reads=["GS2"], writes=["GS"])
        P.op("vector", lambda v, shs=shs: v.tensor_sub(swc, shs, t1), reads=["GS", "GL"], writes=["GS"])
        P.op("scalar", lambda sc: sc.activation(swc, swc, AF.Sigmoid), reads=["GS"], writes=["GS"])
        P.op("vector", lambda v: v.tensor_scalar(ewc, swc, -1.0, 1.0, ALU.mult, op1=ALU.add),
             reads=["GS"], writes=["GS"])
        P.op("vector", lambda v: v.tensor_scalar(WC9[:, 0:8], GS[:, 0:8], ewc, None, ALU.mult),
             reads=["GS"], writes=["WC9"])
        P.op("vector", lambda v: v.tensor_scalar(WC9[:, 8:9], ewc, -1.0, None, ALU.mult),
             reads=["GS"], writes=["WC9"])
        P.op("vector", lambda v, mt=mt: v.tensor_scalar(WC9[:], WC9[:], MSK[:, mt:mt + 1], None, ALU.mult),
             reads=["WC9", "MSK"], writes=["WC9"])
        if debug and mt == 0:
            P.op("gpsimd", lambda g: g.dma_start(out=dGS, in_=GS[:]),
                 reads=["GS"], writes=[("dump",)], dma=True)
            P.op("gpsimd", lambda g: g.dma_start(out=dWC9, in_=XM32[:, 0:9]),
                 reads=["XM32"], writes=[("dump",)], dma=True)
            P.op("vector", lambda v: v.tensor_copy(XM32[:, 0:9], WC9[:]),
                 reads=["WC9"], writes=["XM32"])
        pi = 2 + mt % 2
        P.op("tensor", lambda t, pi=pi: t.matmul(PS[pi][0:9, 0:128], WC9[:], IDENT[:],
                                                 start=True, stop=True),
             reads=["WC9", "IDENT"], writes=[("ps", pi)])
        P.op("vector", lambda v, mt=mt, pi=pi: v.tensor_copy(
            SEL9[:, mt * 128:(mt + 1) * 128], PS[pi][0:9, 0:128]),
            reads=[("ps", pi)], writes=["SEL9"])

    emit_load_weight_bf16(P, H, sw1t, "sw1t", WBF, "W", D, 512)
    for dj in range(4):
        for tt in range(NTT):
            pi = (dj * NTT + tt) % 2
            ps = PS[pi]
            for ci in range(NCH):
                P.op("tensor", lambda t, ps=ps, ci=ci, dj=dj, tt=tt: t.matmul(
                    ps[:, :], WBF[:, ci * 512 + dj * 128: ci * 512 + (dj + 1) * 128],
                    Y[:, ci * T + tt * TT: ci * T + (tt + 1) * TT],
                    start=(ci == 0), stop=(ci == NCH - 1)),
                    reads=[("W", ci), ("Y", ci)], writes=[("ps", pi)])
            P.op("scalar", lambda sc, ps=ps, dj=dj, tt=tt: sc.activation(
                SH[:, dj * T + tt * TT: dj * T + (tt + 1) * TT], ps[:, :], AF.Silu,
                bias=SB1[:, dj:dj + 1]),
                reads=[("ps", pi), "SB1"], writes=[("SH", dj)])
    emit_load_weight_bf16(P, H, sw2t, "sw2t", WBF, "W", D, 512)
    for dj in range(4):
        for tt in range(NTT):
            pi = (dj * NTT + tt) % 2
            ps = PS[pi]
            for ci in range(NCH):
                P.op("tensor", lambda t, ps=ps, ci=ci, dj=dj, tt=tt: t.matmul(
                    ps[:, :], WBF[:, ci * 512 + dj * 128: ci * 512 + (dj + 1) * 128],
                    Y[:, ci * T + tt * TT: ci * T + (tt + 1) * TT],
                    start=(ci == 0), stop=(ci == NCH - 1)),
                    reads=[("W", ci), ("Y", ci)], writes=[("ps", pi)])
            P.op("scalar", lambda sc, ps=ps, dj=dj, tt=tt: sc.activation(
                HX[:], ps[:, :], AF.Identity, bias=SB2[:, dj:dj + 1]),
                reads=[("ps", pi), "SB2"], writes=["HX"])
            P.op("vector", lambda v, dj=dj, tt=tt: v.tensor_mul(
                SH[:, dj * T + tt * TT: dj * T + (tt + 1) * TT],
                SH[:, dj * T + tt * TT: dj * T + (tt + 1) * TT], HX[:]),
                reads=[("SH", dj), "HX"], writes=[("SH", dj)])
    emit_ln_sbuf(P, H, SH, "SH", 4, T, 512, gain=SLNG, gkey="SLNG", bias=SLNB, bkey="SLNB")

    P.fence("scalar", reads=[("Y", c) for c in range(NCH)],
            writes=[("ZH", e, dj) for e in range(8) for dj in range(4)])

    P.fence("vector", reads=[("YML", c) for c in range(NCH)],
            writes=[("EXPOUT", dj) for dj in range(NCH)])
    for mh in range(nmh):
        mo = mh * 512
        for e in range(8):
            emit_load_weight_bf16(P, H, ew1t[e], ("ew1", e), WBF, "W", D, 512)
            for dj in range(4):
                pi = dj % 2
                ps = PS[pi]
                for ci in range(NCH):
                    P.op("tensor", lambda t, ps=ps, ci=ci, dj=dj, mo=mo: t.matmul(
                        ps[:, :], WBF[:, ci * 512 + dj * 128: ci * 512 + (dj + 1) * 128],
                        YMH[:, ci * m + mo: ci * m + mo + 512],
                        start=(ci == 0), stop=(ci == NCH - 1)),
                        reads=[("W", ci), ("YMH", ci)], writes=[("ps", pi)])
                P.op("scalar", lambda sc, ps=ps, dj=dj, e=e: sc.activation(
                    SX[:], ps[:, :], AF.Silu, bias=EB1[:, e * 4 + dj:e * 4 + dj + 1]),
                    reads=[("ps", pi), "EB1"], writes=["SX"])
                P.op("vector", lambda v, e=e, dj=dj: v.tensor_copy(
                    ZH[:, (e * 4 + dj) * 512:(e * 4 + dj + 1) * 512], SX[:]),
                    reads=["SX"], writes=[("ZH", e, dj)])
            emit_load_weight_bf16(P, H, ew2t[e], ("ew2", e), WBF, "W", D, 512)
            for dj in range(4):
                pi = dj % 2
                ps = PS[pi]
                for ci in range(NCH):
                    P.op("tensor", lambda t, ps=ps, ci=ci, dj=dj, mo=mo: t.matmul(
                        ps[:, :], WBF[:, ci * 512 + dj * 128: ci * 512 + (dj + 1) * 128],
                        YMH[:, ci * m + mo: ci * m + mo + 512],
                        start=(ci == 0), stop=(ci == NCH - 1)),
                        reads=[("W", ci), ("YMH", ci)], writes=[("ps", pi)])
                P.op("scalar", lambda sc, ps=ps, dj=dj, e=e: sc.activation(
                    HX[:], ps[:, :], AF.Identity, bias=EB2[:, e * 4 + dj:e * 4 + dj + 1]),
                    reads=[("ps", pi), "EB2"], writes=["HX"])
                P.op("vector", lambda v, e=e, dj=dj: v.tensor_mul(
                    ZH[:, (e * 4 + dj) * 512:(e * 4 + dj + 1) * 512],
                    ZH[:, (e * 4 + dj) * 512:(e * 4 + dj + 1) * 512], HX[:]),
                    reads=[("ZH", e, dj), "HX"], writes=[("ZH", e, dj)])
            for dj in range(4):
                P.op("tensor", lambda t, e=e, dj=dj: t.matmul(
                    PS[4][0:1, :], ONES[:, 0:1], ZH[:, (e * 4 + dj) * 512:(e * 4 + dj + 1) * 512],
                    start=(dj == 0), stop=(dj == 3)),
                    reads=[("ZH", e, dj), "ONES"], writes=[("ps", 4)])
                P.op("scalar", lambda sc, e=e, dj=dj: sc.activation(
                    SX[:], ZH[:, (e * 4 + dj) * 512:(e * 4 + dj + 1) * 512], AF.Square),
                    reads=[("ZH", e, dj)], writes=["SX"])
                P.op("tensor", lambda t, dj=dj: t.matmul(
                    PS[5][0:1, :], ONES[:, 0:1], SX[:], start=(dj == 0), stop=(dj == 3)),
                    reads=["SX", "ONES"], writes=[("ps", 5)])
            P.op("vector", lambda v: v.tensor_scalar_mul(H["ROWSBF"][0:1, 0:TT], PS[4][0:1, :], 1.0 / 512),
                 reads=[("ps", 4)], writes=["rowsbf"])
            P.op("tensor", lambda t: t.matmul(PS[6][:, :], ONES[0:1, :], H["ROWSBF"][0:1, 0:TT],
                                              start=True, stop=True),
                 reads=["rowsbf", "ONES"], writes=[("ps", 6)])
            P.op("vector", lambda v: v.tensor_copy(GT1[:], PS[6][:, :]), reads=[("ps", 6)], writes=["GT1"])
            P.op("vector", lambda v: v.tensor_scalar_mul(H["ROWSBF"][0:1, 0:TT], PS[5][0:1, :], 1.0 / 512),
                 reads=[("ps", 5)], writes=["rowsbf"])
            P.op("tensor", lambda t: t.matmul(PS[6][:, :], ONES[0:1, :], H["ROWSBF"][0:1, 0:TT],
                                              start=True, stop=True),
                 reads=["rowsbf", "ONES"], writes=[("ps", 6)])
            P.op("vector", lambda v: v.tensor_copy(GT2[:], PS[6][:, :]), reads=[("ps", 6)], writes=["GT2"])
            P.op("vector", lambda v: v.tensor_mul(OT1[:], GT1[:], GT1[:]), reads=["GT1"], writes=["OT1"])
            P.op("vector", lambda v: v.tensor_sub(GT2[:], GT2[:], OT1[:]), reads=["GT2", "OT1"], writes=["GT2"])
            P.op("vector", lambda v: v.tensor_scalar_add(GT2[:], GT2[:], 1e-5), reads=["GT2"], writes=["GT2"])
            P.op("scalar", lambda sc: sc.activation(GT2[:], GT2[:], AF.Ln), reads=["GT2"], writes=["GT2"])
            P.op("scalar", lambda sc: sc.activation(GT2[:], GT2[:], AF.Exp, scale=-0.5),
                 reads=["GT2"], writes=["GT2"])
            P.op("tensor", lambda t, e=e, mo=mo: t.matmul(
                PS[6][:, :], EHOT[:, e * 128:(e + 1) * 128], SEL9[:, mo:mo + 512],
                start=True, stop=True),
                reads=["EHOT", "SEL9"], writes=[("ps", 6)])
            P.op("vector", lambda v: v.tensor_copy(WCBB[:], PS[6][:, :]), reads=[("ps", 6)], writes=["WCBB"])
            for dj in range(4):
                zsl = ZH[:, (e * 4 + dj) * 512:(e * 4 + dj + 1) * 512]
                P.op("vector", lambda v, zsl=zsl: v.tensor_sub(zsl, zsl, GT1[:]),
                     reads=[("ZH", e, dj), "GT1"], writes=[("ZH", e, dj)])
                P.op("vector", lambda v, zsl=zsl: v.tensor_mul(zsl, zsl, GT2[:]),
                     reads=[("ZH", e, dj), "GT2"], writes=[("ZH", e, dj)])
                P.op("vector", lambda v, zsl=zsl, dj=dj: v.tensor_scalar(
                    zsl, zsl, SELG[:, dj:dj + 1], SELB[:, dj:dj + 1], ALU.mult, op1=ALU.add),
                    reads=[("ZH", e, dj), "SELG", "SELB"], writes=[("ZH", e, dj)])
                P.op("vector", lambda v, zsl=zsl: v.tensor_mul(zsl, zsl, WCBB[:]),
                     reads=[("ZH", e, dj), "WCBB"], writes=[("ZH", e, dj)])
        if debug and mh == 0:
            for e in range(8):
                for dj in range(4):
                    P.op("gpsimd", lambda g, e=e, dj=dj: g.dma_start(
                        out=dZH[:, (e * 4 + dj) * 512:(e * 4 + dj + 1) * 512],
                        in_=ZH[:, (e * 4 + dj) * 512:(e * 4 + dj + 1) * 512]),
                        reads=[("ZH", e, dj)], writes=[("dump",)], dma=True)
        for e in range(8):
            emit_load_weight_bf16(P, H, ew3t[e], ("ew3", e), WBF, "W", 512, D)
            for dj in range(NCH):
                ps = PS[dj]
                for hc in range(4):
                    P.op("tensor", lambda t, ps=ps, e=e, dj=dj, hc=hc: t.matmul(
                        ps[:, :], WBF[:, hc * D + dj * 128: hc * D + (dj + 1) * 128],
                        ZH[:, (e * 4 + hc) * 512:(e * 4 + hc + 1) * 512],
                        start=(e == 0 and hc == 0), stop=False),
                        reads=[("W", hc), ("ZH", e, hc)], writes=[("ps", dj)])
        for dj in range(NCH):
            ps = PS[dj]
            P.op("tensor", lambda t, ps=ps, dj=dj, mo=mo: t.matmul(
                ps[:, :], EB3B[:, dj * 128:(dj + 1) * 128], SEL9[0:8, mo:mo + 512],
                start=False, stop=True),
                reads=["EB3B", "SEL9"], writes=[("ps", dj)])
            P.op("vector", lambda v, ps=ps, dj=dj, mo=mo: v.tensor_copy(
                EXPOUT[:, dj * m + mo: dj * m + mo + 512], ps[:, :]),
                reads=[("ps", dj)], writes=[("EXPOUT", dj)])

    if debug:
        DBF = sb("DBF", [128, 1024], F32)
        def dump(src_ap, rk, dst, n, npart=128):
            for o in range(0, n, 1024):
                w = min(1024, n - o)
                P.op("vector", lambda v, o=o, w=w: v.tensor_copy(DBF[0:npart, 0:w], src_ap[0:npart, o:o + w]),
                     reads=[rk] if not isinstance(rk, list) else rk, writes=["DBF"])
                P.op("gpsimd", lambda g, o=o, w=w: g.dma_start(out=dst[0:npart, o:o + w], in_=DBF[0:npart, 0:w]),
                     reads=["DBF"], writes=[("dump",)], dma=True)
        dump(GL, "GL", dGL, 9 * nmt)
        dump(G9H, "G9H", dG9H, NCH * 9)
        dump(SEL9, "SEL9", dSEL9, m, npart=9)
        dump(EXPOUT, [("EXPOUT", dj) for dj in range(NCH)], dEXP, NCH * m)
        dump(YMH, [("YMH", c) for c in range(NCH)], dYMH, NCH * m)

    for mh in range(nmh):
        pi = mh % 2
        P.op("tensor", lambda t, mh=mh, pi=pi: t.matmul(
            PS[pi][:, :], EHOT[:, 8 * 128:9 * 128], SEL9[:, mh * 512:(mh + 1) * 512],
            start=True, stop=True),
            reads=["EHOT", "SEL9"], writes=[("ps", pi)])
        P.op("vector", lambda v, mh=mh, pi=pi: v.tensor_copy(
            SW1B[:, mh * 512:(mh + 1) * 512], PS[pi][:, :]),
            reads=[("ps", pi)], writes=["SW1B"])

    emit_load_weight_bf16(P, H, sw3t, "sw3t", WBF, "W", 512, D)
    pat = list(pattern)
    for dj in range(NCH):
        for tt in range(NTT):
            pi = (dj * NTT + tt) % 2
            ps = PS[pi]
            for hc in range(4):
                P.op("tensor", lambda t, ps=ps, hc=hc, dj=dj, tt=tt: t.matmul(
                    ps[:, :], WBF[:, hc * D + dj * 128: hc * D + (dj + 1) * 128],
                    SH[:, hc * T + tt * TT: hc * T + (tt + 1) * TT],
                    start=(hc == 0), stop=(hc == 3)),
                    reads=[("W", hc), ("SH", hc)], writes=[("ps", pi)])
            P.op("scalar", lambda sc, ps=ps, dj=dj: sc.activation(
                OT1[:], ps[:, :], AF.Identity, bias=SB3[:, dj:dj + 1]),
                reads=[("ps", pi), "SB3"], writes=["OT1"])
            b = (dj * NTT + tt) % 2
            xrl = XRL[:, b * TT:(b + 1) * TT]
            P.op("sync", lambda s, dj=dj, tt=tt, xrl=xrl: s.dma_start(
                out=xrl, in_=xin[dj * 128:(dj + 1) * 128, tt * TT:(tt + 1) * TT]),
                reads=["xin"], writes=["XT"], dma=True)
            P.op("vector", lambda v, xrl=xrl: v.tensor_add(OT2[:], OT1[:], xrl),
                 reads=["OT1", "XT"], writes=["OT2"])
            lo = tt * TT
            hi = (tt + 1) * TT
            mlo = int(np.searchsorted(pat, lo))
            mhi = int(np.searchsorted(pat, hi))
            ncm = mhi - mlo
            if ncm > 0:
                local = [c - lo for c in pat[mlo:mhi]]
                off = 0
                for (st, sp, n) in _runs(local):
                    P.op("vector", lambda v, st=st, sp=sp, n=n, off=off: v.tensor_copy(
                        GT1[:, off:off + n],
                        OT1[:, st:st + sp * n:sp] if sp > 1 else OT1[:, st:st + n]),
                        reads=["OT1"], writes=["GT1"])
                    off += n
                P.op("vector", lambda v, mlo=mlo, ncm=ncm: v.tensor_mul(
                    GT1[:, 0:ncm], GT1[:, 0:ncm], SW1B[:, mlo:mlo + ncm]),
                    reads=["GT1", "SW1B"], writes=["GT1"])
                off = 0
                for (st, sp, n) in _runs(local):
                    P.op("vector", lambda v, st=st, sp=sp, n=n, off=off: v.tensor_copy(
                        GT2[:, off:off + n],
                        OT2[:, st:st + sp * n:sp] if sp > 1 else OT2[:, st:st + n]),
                        reads=["OT2"], writes=["GT2"])
                    off += n
                P.op("vector", lambda v, ncm=ncm: v.tensor_add(GT1[:, 0:ncm], GT1[:, 0:ncm],
                                                               GT2[:, 0:ncm]),
                     reads=["GT1", "GT2"], writes=["GT1"])
                P.op("vector", lambda v, dj=dj, mlo=mlo, ncm=ncm: v.tensor_copy(
                    GT2[:, 0:ncm], EXPOUT[:, dj * m + mlo: dj * m + mlo + ncm]),
                    reads=[("EXPOUT", dj)], writes=["GT2"])
                P.op("vector", lambda v, ncm=ncm: v.tensor_add(GT1[:, 0:ncm], GT1[:, 0:ncm],
                                                               GT2[:, 0:ncm]),
                     reads=["GT1", "GT2"], writes=["GT1"])
                off = 0
                for (st, sp, n) in _runs(local):
                    P.op("vector", lambda v, st=st, sp=sp, n=n, off=off: v.tensor_copy(
                        OT2[:, st:st + sp * n:sp] if sp > 1 else OT2[:, st:st + n],
                        GT1[:, off:off + n]),
                        reads=["GT1"], writes=["OT2"])
                    off += n
            P.op("gpsimd", lambda g, dj=dj, tt=tt: g.dma_start(
                out=xout[dj * 128:(dj + 1) * 128, tt * TT:(tt + 1) * TT], in_=OT2[:]),
                reads=["OT2"], writes=[("xout", dj, tt)], dma=True)



def emit_l0ab(P, H, W, pool):
    XT = H["XT"]; ROWS = H["ROWS"]; RQK = H["RQK"]; RBF = H["ROWSBF"]
    ONES = H["ONES"]; IDENT = H["IDENT"]; WBF = H["WBF"]; PS = H["PS"]
    pool.reset()
    debug = False
    dR2 = dBA = dQ = dRows = dEXP = dG9H = dGL = dGLpre = None
    dGS = dSEL9 = dWC9 = dYMH = dZH = None
    xin = W["xin"]
    wv = W["wv"]
    bv = W["bv"]
    onesC = W["onesC"]
    xout = W["xout"]
    Y = H["YB"][:, :]
    BIAS = pool.carve(128, NCH, F32)
    OTMP = pool.carve(128, TT, F32)
    XRL = XT[:, 1024:2048]
    P.op("sync", lambda s: s.dma_start(out=ONES[:], in_=onesC), writes=["ONES"], dma=True)
    P.op("sync", lambda s: s.dma_start(out=BIAS[:], in_=bv), writes=["BIAS"], dma=True)

    emit_ln(P, H, xin, "xin", Y, "Y")
    emit_load_weight_bf16(P, H, wv, "wv", WBF, "W", D, D)

    for dj in range(NCH):
        for tt in range(NTT):
            pi = (dj * NTT + tt) % 2
            ps = PS[pi]
            for ci in range(NCH):
                P.op("tensor", lambda t, ps=ps, ci=ci, dj=dj, tt=tt: t.matmul(
                    ps[:, :], WBF[:, ci * D + dj * 128: ci * D + (dj + 1) * 128],
                    Y[:, ci * T + tt * TT: ci * T + (tt + 1) * TT],
                    start=(ci == 0), stop=(ci == NCH - 1)),
                    reads=[("W", ci), ("Y", ci)], writes=[("ps", pi)])
            b = (dj * NTT + tt) % 2
            xrl = XRL[:, b * TT:(b + 1) * TT]
            P.op("sync", lambda s, dj=dj, tt=tt, xrl=xrl: s.dma_start(
                out=xrl, in_=xin[dj * 128:(dj + 1) * 128, tt * TT:(tt + 1) * TT]),
                reads=["xin"], writes=[("XRL", b)], dma=True)
            P.op("scalar", lambda sc, ps=ps, dj=dj: sc.activation(
                OTMP[:], ps[:, :], AF.Identity, bias=BIAS[:, dj:dj + 1]),
                reads=[("ps", pi), "BIAS"], writes=["OTMP"])
            P.op("vector", lambda v, xrl=xrl: v.tensor_add(OTMP[:], OTMP[:], xrl),
                 reads=["OTMP", ("XRL", b)], writes=["OTMP"])
            P.op("gpsimd", lambda g, dj=dj, tt=tt: g.dma_start(
                out=xout[dj * 128:(dj + 1) * 128, tt * TT:(tt + 1) * TT], in_=OTMP[:]),
                reads=["OTMP"], writes=[("xout", dj, tt)], dma=True)



def emit_l1aa(P, H, W, pool, niter=10):
    XT = H["XT"]; ROWS = H["ROWS"]; RQK = H["RQK"]; RBF = H["ROWSBF"]
    ONES = H["ONES"]; IDENT = H["IDENT"]; WBF = H["WBF"]; PS = H["PS"]
    pool.reset()
    debug = False
    dR2 = dBA = dQ = dRows = dEXP = dG9H = dGL = dGLpre = None
    dGS = dSEL9 = dWC9 = dYMH = dZH = None
    xin = W["xin"]
    wqt = W["wqt"]
    wkt = W["wkt"]
    wvt = W["wvt"]
    wot = W["wot"]
    bq = W["bq"]
    bk = W["bk"]
    bvrow = W["bvrow"]
    bo = W["bo"]
    onesC = W["onesC"]
    identC = W["identC"]
    xout = W["xout"]
    SCR = H["SCR"]
    EQ = SCR[:, 0:T]; MQ = SCR[:, T:2 * T]; WT = SCR[:, 2 * T:3 * T]
    PSQ = H["PSQ"]
    Y = H["YB"][:, :]; Q = H["QB"][:, :]; K = H["KB"][:, :]; V = H["VB"][:, :]
    BQ = pool.carve(128, NCH, F32)
    BK = pool.carve(128, NCH, F32)
    BO = pool.carve(128, NCH, F32)
    BVR = pool.carve(1, D, BF16)
    BAr = pool.carve(1, T, BF16)
    CRB = pool.carve(1, T, BF16)
    VSUM = pool.carve(1, 64, BF16)
    COLS = pool.carve(128, 12, F32)
    ZIR = pool.carve(1, 128, BF16)
    CRZ = pool.carve(1, 128, BF16)
    ZCB = pool.carve(128, 1, BF16)
    SC8B = pool.carve(1, 8, F32)
    OTMP = XT[:, 1536:2048]
    XRL = XT[:, 1024:1536]
    ld = lambda dst, srcap, key: P.op(
        "sync", lambda s, dst=dst, srcap=srcap: s.dma_start(out=dst, in_=srcap),
        writes=[key], dma=True)
    ld(ONES[:], onesC, "ONES")
    ld(IDENT[:], identC, "IDENT")
    ld(BQ[:], bq, "BQ")
    ld(BK[:], bk, "BK")
    ld(BO[:], bo, "BO")
    P.op("sync", lambda s: s.dma_start(out=H["XT"][0:1, 0:D], in_=bvrow), reads=["XT"], writes=["XT"], dma=True)
    P.op("vector", lambda v: v.tensor_copy(BVR[:], H["XT"][0:1, 0:D]), reads=["XT"], writes=["BVR"])

    emit_ln(P, H, xin, "xin", Y, "Y")

    for W, wk_, BT, btk, DST, dkey in ((wqt, "wqt", BQ, "BQ", Q, "Q"), (wkt, "wkt", BK, "BK", K, "K")):
        emit_load_weight_bf16(P, H, W, wk_, WBF, "W", D, D)
        for dj in range(NCH):
            for tt in range(NTT):
                pi = 4 + (dj * NTT + tt) % 2
                ps = PS[pi]
                for ci in range(NCH):
                    P.op("tensor", lambda t, ps=ps, ci=ci, dj=dj, tt=tt: t.matmul(
                        ps[:, :], WBF[:, ci * D + dj * 128: ci * D + (dj + 1) * 128],
                        Y[:, ci * T + tt * TT: ci * T + (tt + 1) * TT],
                        start=(ci == 0), stop=(ci == NCH - 1)),
                        reads=[("W", ci), ("Y", ci)], writes=[("ps", pi)])
                P.op("scalar", lambda sc, ps=ps, dj=dj, tt=tt, DST=DST, BT=BT: sc.activation(
                    DST[:, dj * T + tt * TT: dj * T + (tt + 1) * TT], ps[:, :], AF.Identity,
                    bias=BT[:, dj:dj + 1]),
                    reads=[("ps", pi), btk], writes=[(dkey, dj)])
    emit_load_weight_bf16(P, H, wvt, "wvt", WBF, "W", D, D)
    for tc in range(16):
        for dh in range(2):
            pi = 4 + (tc * 2 + dh) % 2
            ps = PS[pi]
            for ci in range(NCH):
                P.op("tensor", lambda t, ps=ps, ci=ci, tc=tc, dh=dh: t.matmul(
                    ps[:, :], Y[:, ci * T + tc * 128: ci * T + (tc + 1) * 128],
                    WBF[:, ci * D + dh * 512: ci * D + (dh + 1) * 512],
                    start=(ci == 0), stop=False),
                    reads=[("W", ci), ("Y", ci)], writes=[("ps", pi)])
            P.op("tensor", lambda t, ps=ps, dh=dh: t.matmul(
                ps[:, :], ONES[0:1, :], BVR[0:1, dh * 512:(dh + 1) * 512], start=False, stop=True),
                reads=["ONES", "BVR"], writes=[("ps", pi)])
            P.op("vector", lambda v, ps=ps, tc=tc, dh=dh: v.tensor_copy(
                V[:, tc * D + dh * 512: tc * D + (dh + 1) * 512], ps[:, :]),
                reads=[("ps", pi)], writes=[("V", tc)])

    ROWS = H["ROWS"]
    qn2 = RQK[0:1, 0:T]
    kn2 = RQK[0:1, T:2 * T]
    sc8 = SC8B
    RBF = H["ROWSBF"]
    P.fence("scalar", reads=[("Y", c) for c in range(NCH)],
            writes=["EQ", "MQ", "WT"])
    P.fence("vector", reads=[("ps", i) for i in range(4)],
            writes=[("psq", kb) for kb in range(4)])
    KEEP = float(int(T * 0.1))

    ba = COLS[:, 0:1]
    cc = COLS[:, 1:2]
    lo = COLS[:, 2:3]
    hi = COLS[:, 3:4]
    tau = COLS[:, 4:5]
    cnt = COLS[:, 5:6]
    take = COLS[:, 6:7]
    d1 = COLS[:, 7:8]
    d2 = COLS[:, 8:9]
    rm = COLS[:, 9:10]
    zz = COLS[:, 10:11]
    zi = COLS[:, 11:12]

    for h in range(16):
        c = h // 2
        hb = (h % 2) * 64
        Qh = Q[:, c * T:(c + 1) * T]
        Kh = K[:, c * T:(c + 1) * T]
        qkey, kkey = ("Q", c), ("K", c)
        for SRCh, skey, dstrow in ((Qh, qkey, qn2), (Kh, kkey, kn2)):
            P.op("scalar", lambda sc, SRCh=SRCh, hb=hb: sc.activation(
                H["XSQB"][hb:hb + 64, :], SRCh[hb:hb + 64, :], AF.Square),
                reads=[skey], writes=["XSQB"])
            for tt in range(NTT):
                pi = 4 + tt % 2
                P.op("tensor", lambda t, tt=tt, pi=pi, hb=hb: t.matmul(
                    PS[pi][0:1, :], ONES[hb:hb + 64, 0:1], H["XSQB"][hb:hb + 64, tt * TT:(tt + 1) * TT],
                    start=True, stop=True),
                    reads=["XSQB", "ONES"], writes=[("ps", pi)])
                P.op("vector", lambda v, tt=tt, pi=pi, dstrow=dstrow: v.tensor_copy(
                    dstrow[0:1, tt * TT:(tt + 1) * TT], PS[pi][0:1, :]),
                    reads=[("ps", pi)], writes=["rows"])
        P.op("vector", lambda v: v.reduce_max(sc8[0:1, 0:1], kn2, axis=AX.X), reads=["rows"], writes=["rows"])
        P.op("vector", lambda v: v.tensor_scalar(BAr[0:1, :], qn2, sc8[0:1, 0:1], -1.0 / 16.0,
                                                 ALU.add, op1=ALU.mult),
             reads=["rows"], writes=["BAr"])
        P.op("scalar", lambda sc: sc.activation(CRB[0:1, :], BAr[0:1, :], AF.Exp),
             reads=["BAr"], writes=["CRB"])
        for kc in range(16):
            P.op("tensor", lambda t, kc=kc, h=h: t.matmul(
                PS[6][0:1, 0:64], ONES[:, 0:1], V[:, kc * D + h * 64:(kc * D + (h + 1) * 64)],
                start=(kc == 0), stop=(kc == 15)),
                reads=[("V", kc), "ONES"], writes=[("ps", 6)])
        P.op("vector", lambda v: v.tensor_copy(VSUM[:], PS[6][0:1, 0:64]), reads=[("ps", 6)], writes=["VSUM"])
        for qt in range(16):
            qsl = slice(qt * 128, (qt + 1) * 128)
            P.op("tensor", lambda t, qsl=qsl: t.matmul(
                PS[7][:, 0:1], BAr[0:1, qsl], ONES[0:1, 0:1], start=True, stop=True),
                reads=["BAr", "ONES"], writes=[("ps", 7)])
            P.op("vector", lambda v: v.tensor_copy(ba, PS[7][:, 0:1]), reads=[("ps", 7)], writes=["COLS"])
            P.op("tensor", lambda t, qsl=qsl: t.matmul(
                PS[7][:, 0:1], CRB[0:1, qsl], ONES[0:1, 0:1], start=True, stop=True),
                reads=["CRB", "ONES"], writes=[("ps", 7)])
            P.op("vector", lambda v: v.tensor_copy(cc, PS[7][:, 0:1]), reads=[("ps", 7)], writes=["COLS"])
            for kb in range(4):
                P.op("tensor", lambda t, kb=kb, qt=qt, hb=hb, Qh=Qh, Kh=Kh: t.matmul(
                    PSQ[:, kb * 512:(kb + 1) * 512], Qh[hb:hb + 64, qt * 128:(qt + 1) * 128],
                    Kh[hb:hb + 64, kb * 512:(kb + 1) * 512], start=True, stop=True),
                    reads=[qkey, kkey], writes=[("psq", kb)])
            P.op("scalar", lambda sc: sc.activation(EQ[:, :], PSQ[:, :], AF.Exp, bias=ba,
                                                    scale=1.0 / 8.0),
                reads=[("psq", 0), ("psq", 1), ("psq", 2), ("psq", 3), "COLS"], writes=["EQ"])
            P.op("vector", lambda v: v.reduce_max(rm, EQ[:, :], axis=AX.X), reads=["EQ"], writes=["COLS"])
            P.op("vector", lambda v: v.memset(lo, 0.0), reads=[], writes=["COLS"])
            P.op("vector", lambda v: v.tensor_copy(hi, rm), reads=["COLS"], writes=["COLS"])
            P.op("vector", lambda v: v.tensor_scalar_mul(tau, rm, 0.5), reads=["COLS"], writes=["COLS"])
            for it in range(niter):
                P.op("vector", lambda v: v.tensor_scalar(MQ[:, :], EQ[:, :], tau, 0.0,
                                                         ALU.is_ge, op1=ALU.add, accum_out=cnt),
                     reads=["EQ", "COLS"], writes=["MQ", "COLS"])
                P.op("vector", lambda v: v.tensor_scalar(take, cnt, KEEP - 0.5, None, ALU.is_ge),
                     reads=["COLS"], writes=["COLS"])
                P.op("vector", lambda v: v.tensor_sub(d1, tau, lo), reads=["COLS"], writes=["COLS"])
                P.op("vector", lambda v: v.tensor_sub(d2, hi, tau), reads=["COLS"], writes=["COLS"])
                P.op("vector", lambda v: v.tensor_scalar(lo, take, d1, lo, ALU.mult, op1=ALU.add),
                     reads=["COLS"], writes=["COLS"])
                P.op("vector", lambda v: v.tensor_scalar(hi, take, d2, tau, ALU.mult, op1=ALU.add),
                     reads=["COLS"], writes=["COLS"])
                P.op("vector", lambda v: v.tensor_add(tau, lo, hi), reads=["COLS"], writes=["COLS"])
                P.op("vector", lambda v: v.tensor_scalar_mul(tau, tau, 0.5), reads=["COLS"], writes=["COLS"])
            P.op("vector", lambda v: v.tensor_scalar(MQ[:, :], EQ[:, :], lo, None, ALU.is_ge),
                 reads=["EQ", "COLS"], writes=["MQ"])
            P.op("vector", lambda v: v.scalar_tensor_tensor(EQ[:, :], EQ[:, :], cc, MQ[:, :],
                                                            ALU.subtract, ALU.mult, accum_out=zz),
                 reads=["EQ", "MQ", "COLS"], writes=["EQ", "COLS"])
            P.op("vector", lambda v: v.tensor_scalar(zz, cc, float(T), zz, ALU.mult, op1=ALU.add),
                 reads=["COLS"], writes=["COLS"])
            P.op("vector", lambda v: v.reciprocal(zi, zz), reads=["COLS"], writes=["COLS"])
            P.op("vector", lambda v: v.tensor_scalar(EQ[:, :], EQ[:, :], zi, None, ALU.mult),
                 reads=["EQ", "COLS"], writes=["EQ"])
            P.op("vector", lambda v: v.tensor_copy(ZCB[:], zi), reads=["COLS"], writes=["ZCB"])
            P.op("tensor", lambda t: t.matmul(PS[7][0:1, 0:128], ZCB[:], IDENT[:], start=True, stop=True),
                 reads=["ZCB", "IDENT"], writes=[("ps", 7)])
            P.op("vector", lambda v: v.tensor_copy(ZIR[:], PS[7][0:1, 0:128]), reads=[("ps", 7)], writes=["ZIR"])
            P.op("vector", lambda v, qsl=qsl: v.tensor_mul(CRZ[:], CRB[0:1, qsl], ZIR[:]),
                 reads=["CRB", "ZIR"], writes=["CRZ"])
            pi = 4 + qt % 2
            po = PS[pi]
            for kc in range(16):
                P.op("tensor", lambda t, kc=kc: t.matmul(
                    PS[6][:, 0:128], EQ[:, kc * 128:(kc + 1) * 128], IDENT[:], start=True, stop=True),
                    reads=["EQ", "IDENT"], writes=[("ps", 6)])
                P.op("vector", lambda v, kc=kc: v.tensor_copy(WT[:, kc * 128:(kc + 1) * 128],
                                                              PS[6][:, 0:128]),
                     reads=[("ps", 6)], writes=["WT"])
            for kc in range(16):
                P.op("tensor", lambda t, kc=kc, h=h, po=po: t.matmul(
                    po[0:64, 0:128], V[:, kc * D + h * 64: kc * D + (h + 1) * 64],
                    WT[:, kc * 128:(kc + 1) * 128], start=(kc == 0), stop=False),
                    reads=[("V", kc), "WT"], writes=[("ps", pi)])
            P.op("tensor", lambda t, po=po: t.matmul(
                po[0:64, 0:128], VSUM[0:1, 0:64], CRZ[0:1, :], start=False, stop=True),
                reads=["VSUM", "CRZ"], writes=[("ps", pi)])
            P.op("vector", lambda v, po=po, qt=qt, Qh=Qh, hb=hb: v.tensor_copy(
                Qh[hb:hb + 64, qt * 128:(qt + 1) * 128], po[0:64, 0:128]),
                reads=[("ps", pi)], writes=[qkey])

    P.fence("vector", reads=["EQ", "MQ", "WT"], writes=[("W", ci) for ci in range(NCH)])
    emit_load_weight_bf16(P, H, wot, "wot", WBF, "W", D, D)
    for dj in range(NCH):
        for tt in range(NTT):
            pi = 4 + (dj * NTT + tt) % 2
            ps = PS[pi]
            for ci in range(NCH):
                P.op("tensor", lambda t, ps=ps, ci=ci, dj=dj, tt=tt: t.matmul(
                    ps[:, :], WBF[:, ci * D + dj * 128: ci * D + (dj + 1) * 128],
                    Q[:, ci * T + tt * TT: ci * T + (tt + 1) * TT],
                    start=(ci == 0), stop=(ci == NCH - 1)),
                    reads=[("W", ci), ("Q", ci)], writes=[("ps", pi)])
            xrl = XRL[:, :]
            P.op("sync", lambda s, dj=dj, tt=tt, xrl=xrl: s.dma_start(
                out=xrl, in_=xin[dj * 128:(dj + 1) * 128, tt * TT:(tt + 1) * TT]),
                reads=["xin"], writes=["XT"], dma=True)
            P.op("scalar", lambda sc, ps=ps, dj=dj: sc.activation(
                OTMP[:], ps[:, :], AF.Identity, bias=BO[:, dj:dj + 1]),
                reads=[("ps", pi), "BO", "XT"], writes=["XT"])
            P.op("vector", lambda v, xrl=xrl: v.tensor_add(OTMP[:], OTMP[:], xrl),
                 reads=["XT"], writes=["XT"])
            P.op("gpsimd", lambda g, dj=dj, tt=tt: g.dma_start(
                out=xout[dj * 128:(dj + 1) * 128, tt * TT:(tt + 1) * TT], in_=OTMP[:]),
                reads=["XT"], writes=[("xout", dj, tt)], dma=True)



def emit_l1ab(P, H, W, pool):
    XT = H["XT"]; ROWS = H["ROWS"]; RQK = H["RQK"]; RBF = H["ROWSBF"]
    ONES = H["ONES"]; IDENT = H["IDENT"]; WBF = H["WBF"]; PS = H["PS"]
    pool.reset()
    debug = False
    dR2 = dBA = dQ = dRows = dEXP = dG9H = dGL = dGLpre = None
    dGS = dSEL9 = dWC9 = dYMH = dZH = None
    xin = W["xin"]
    wqt = W["wqt"]
    wkt = W["wkt"]
    wvt = W["wvt"]
    wot = W["wot"]
    bq = W["bq"]
    bk = W["bk"]
    bvrow = W["bvrow"]
    bo = W["bo"]
    ropeA = W["ropeA"]
    ropeB = W["ropeB"]
    rt = W["rt"]
    onesC = W["onesC"]
    masksC = W["masksC"]
    xout = W["xout"]
    SCR = H["SCR"]
    ET = SCR[:, 0:8192]
    RT = H["RT"]; RA = H["RA"]; RB2 = H["RB2"]
    Y = H["YB"][:, :]; Q = H["QB"][:, :]; K = H["KB"][:, :]; V = H["VB"][:, :]
    MASKS = WBF[:, 0:2048]
    BQ = pool.carve(128, NCH, F32)
    BK = pool.carve(128, NCH, F32)
    BO = pool.carve(128, NCH, F32)
    BVR = pool.carve(1, D, BF16)
    BVst = XT[0:1, 0:1024]
    BA = pool.carve(128, 1, F32)
    KMB = pool.carve(128, 1, F32)
    TMPC = pool.carve(128, TT, F32)
    TMPD = pool.carve(128, TT, BF16)
    ZIB = pool.carve(128, TT, BF16)
    OTMP = pool.carve(128, TT, F32)
    SC8B = pool.carve(1, 8, F32)
    XRL = XT[:, 1024:1536]
    ld = lambda dst, srcap, key: P.op(
        "sync", lambda s, dst=dst, srcap=srcap: s.dma_start(out=dst, in_=srcap),
        writes=[key], dma=True)
    ld(ONES[:], onesC, "ONES")
    ld(RT[:], rt, "RT")
    ld(BQ[:], bq, "BQ")
    ld(BK[:], bk, "BK")
    ld(BO[:], bo, "BO")
    P.op("sync", lambda s: s.dma_start(out=BVst, in_=bvrow), reads=["XT"], writes=["XT"], dma=True)
    P.op("vector", lambda v: v.tensor_copy(BVR[:], BVst), reads=["XT"], writes=["BVR"])
    P.op("sync", lambda s: s.dma_start(out=H["XT"][:], in_=ropeA), reads=[], writes=["XT"], dma=True)
    P.op("vector", lambda v: v.tensor_copy(RA[:], H["XT"][:]), reads=["XT"], writes=["RA"])
    P.op("sync", lambda s: s.dma_start(out=H["XT"][:], in_=ropeB), reads=["XT"], writes=["XT"], dma=True)
    P.op("vector", lambda v: v.tensor_copy(RB2[:], H["XT"][:]), reads=["XT"], writes=["RB2"])

    emit_ln(P, H, xin, "xin", Y, "Y")

    for W, wk_, BT, btk, DST, dkey in ((wqt, "wqt", BQ, "BQ", Q, "Q"), (wkt, "wkt", BK, "BK", K, "K")):
        emit_load_weight_bf16(P, H, W, wk_, WBF, "W", D, D)
        for dj in range(NCH):
            for tt in range(NTT):
                pi = (dj * NTT + tt) % 2
                ps = PS[pi]
                for ci in range(NCH):
                    P.op("tensor", lambda t, ps=ps, ci=ci, dj=dj, tt=tt: t.matmul(
                        ps[:, :], WBF[:, ci * D + dj * 128: ci * D + (dj + 1) * 128],
                        Y[:, ci * T + tt * TT: ci * T + (tt + 1) * TT],
                        start=(ci == 0), stop=(ci == NCH - 1)),
                        reads=[("W", ci), ("Y", ci)], writes=[("ps", pi)])
                P.op("scalar", lambda sc, ps=ps, dj=dj, tt=tt, DST=DST: sc.activation(
                    DST[:, dj * T + tt * TT: dj * T + (tt + 1) * TT], ps[:, :], AF.Identity,
                    bias=BT[:, dj:dj + 1]),
                    reads=[("ps", pi), btk], writes=[(dkey, dj)])
    emit_load_weight_bf16(P, H, wvt, "wvt", WBF, "W", D, D)
    for tc in range(16):
        for dh in range(2):
            pi = (tc * 2 + dh) % 2
            ps = PS[pi]
            for ci in range(NCH):
                P.op("tensor", lambda t, ps=ps, ci=ci, tc=tc, dh=dh: t.matmul(
                    ps[:, :], Y[:, ci * T + tc * 128: ci * T + (tc + 1) * 128],
                    WBF[:, ci * D + dh * 512: ci * D + (dh + 1) * 512],
                    start=(ci == 0), stop=False),
                    reads=[("W", ci), ("Y", ci)], writes=[("ps", pi)])
            P.op("tensor", lambda t, ps=ps, tc=tc, dh=dh: t.matmul(
                ps[:, :], ONES[0:1, tc * 8:tc * 8 + 128] if False else ONES[0:1, :],
                BVR[0:1, dh * 512:(dh + 1) * 512], start=False, stop=True),
                reads=["ONES", "BVR"], writes=[("ps", pi)])
            P.op("vector", lambda v, ps=ps, tc=tc, dh=dh: v.tensor_copy(
                V[:, tc * D + dh * 512: tc * D + (dh + 1) * 512], ps[:, :]),
                reads=[("ps", pi)], writes=[("V", tc)])

    for DST, dkey in ((Q, "Q"), (K, "K")):
        for c in range(NCH):
            sl = DST[:, c * T:(c + 1) * T]
            for tt in range(NTT):
                pi = tt % 2
                tsl = slice(tt * TT, (tt + 1) * TT)
                P.op("tensor", lambda t, pi=pi, sl=sl, tsl=tsl: t.matmul(
                    PS[pi][:, :], RT[:], sl[:, tsl], start=True, stop=True),
                    reads=[(dkey, c), "RT"], writes=[("ps", pi)])
                P.op("vector", lambda v, pi=pi, tsl=tsl: v.tensor_mul(
                    TMPC[:], PS[pi][:, :], RB2[:, tsl]),
                    reads=[("ps", pi), "RB2"], writes=["TMPC"])
                P.op("vector", lambda v, sl=sl, tsl=tsl: v.tensor_mul(TMPD[:], sl[:, tsl], RA[:, tsl]),
                     reads=[(dkey, c), "RA"], writes=["TMPD"])
                P.op("vector", lambda v, sl=sl, tsl=tsl: v.tensor_add(sl[:, tsl], TMPD[:], TMPC[:]),
                     reads=["TMPD", "TMPC"], writes=[(dkey, c)])

    ROWS = H["ROWS"]
    qn2 = RQK[0:1, 0:T]
    kn2 = RQK[0:1, T:2 * T]
    km2 = SC8B
    zr = ROWS[0:1, 8:8 + TT]
    RBF = H["ROWSBF"]
    P.fence("scalar", reads=[("Y", c) for c in range(NCH)],
            writes=[("ET", kc) for kc in range(16)])
    P.fence("vector", reads=[("W", ci) for ci in range(NCH)], writes=["MASKS"])
    P.op("sync", lambda s: s.dma_start(
        out=MASKS.rearrange("p (m n) -> p m n", m=4), in_=masksC.rearrange("m p n -> p m n")),
        writes=["MASKS"], dma=True)
    for h in range(16):
        c = h // 2
        hb = (h % 2) * 64
        Qh = Q[:, c * T:(c + 1) * T]
        Kh = K[:, c * T:(c + 1) * T]
        qkey, kkey = ("Q", c), ("K", c)
        for SRCh, skey, dstrow in ((Qh, qkey, qn2), (Kh, kkey, kn2)):
            P.op("scalar", lambda sc, SRCh=SRCh, hb=hb: sc.activation(
                H["XSQB"][hb:hb + 64, :], SRCh[hb:hb + 64, :], AF.Square),
                reads=[skey], writes=["XSQB"])
            for tt in range(NTT):
                pi = 4 + tt % 2
                P.op("tensor", lambda t, tt=tt, pi=pi, hb=hb: t.matmul(
                    PS[pi][0:1, :], ONES[hb:hb + 64, 0:1] if hb == 64 else ONES[0:64, 0:1],
                    H["XSQB"][hb:hb + 64, tt * TT:(tt + 1) * TT], start=True, stop=True),
                    reads=["XSQB", "ONES"], writes=[("ps", pi)])
                P.op("vector", lambda v, tt=tt, pi=pi, dstrow=dstrow: v.tensor_copy(
                    dstrow[0:1, tt * TT:(tt + 1) * TT], PS[pi][0:1, :]),
                    reads=[("ps", pi)], writes=["rows"])
        P.op("vector", lambda v: v.reduce_max(km2[0:1, 0:1], qn2, axis=AX.X), reads=["rows"], writes=["rows"])
        P.op("vector", lambda v: v.reduce_max(km2[0:1, 1:2], kn2, axis=AX.X), reads=["rows"], writes=["rows"])
        P.op("vector", lambda v: v.tensor_add(km2[0:1, 2:3], km2[0:1, 0:1], km2[0:1, 1:2]),
             reads=["rows"], writes=["rows"])
        P.op("vector", lambda v: v.tensor_scalar_mul(km2[0:1, 3:4], km2[0:1, 2:3], -1.0 / 16.0),
             reads=["rows"], writes=["rows"])
        P.op("vector", lambda v: v.tensor_copy(RBF[0:1, 0:1], km2[0:1, 3:4]), reads=["rows"], writes=["rowsbf"])
        P.op("tensor", lambda t: t.matmul(PS[7][:, 0:1], ONES[0:1, :], RBF[0:1, 0:1], start=True, stop=True),
             reads=["rowsbf", "ONES"], writes=[("ps", 7)])
        P.op("vector", lambda v: v.tensor_copy(BA[:], PS[7][:, 0:1]), reads=[("ps", 7)], writes=["BA"])
        for qt in range(NTT):
            nkc = min(16, 4 * qt + 4)
            for kc in range(nkc):
                pi = kc % 2
                ps = PS[pi]
                P.op("tensor", lambda t, ps=ps, kc=kc, qt=qt, Kh=Kh, Qh=Qh, hb=hb: t.matmul(
                    ps[:, :], Kh[hb:hb + 64, kc * 128:(kc + 1) * 128],
                    Qh[hb:hb + 64, qt * TT:(qt + 1) * TT], start=True, stop=True),
                    reads=[kkey, qkey], writes=[("ps", pi)])
                P.op("scalar", lambda sc, ps=ps, kc=kc: sc.activation(
                    ET[:, kc * TT:(kc + 1) * TT], ps[:, :], AF.Exp, bias=BA[:], scale=1.0 / 8.0),
                    reads=[("ps", pi), "BA"], writes=[("ET", kc)])
                mmask = kc - 4 * qt
                if mmask >= 0:
                    P.op("vector", lambda v, kc=kc, mmask=mmask: v.tensor_mul(
                        ET[:, kc * TT:(kc + 1) * TT], ET[:, kc * TT:(kc + 1) * TT],
                        MASKS[:, mmask * 512:(mmask + 1) * 512]),
                        reads=[("ET", kc), "MASKS"], writes=[("ET", kc)])
            for kc in range(nkc):
                P.op("tensor", lambda t, kc=kc, nkc=nkc: t.matmul(
                    PS[6][0:1, :], ONES[:, 0:1], ET[:, kc * TT:(kc + 1) * TT],
                    start=(kc == 0), stop=(kc == nkc - 1)),
                    reads=[("ET", kc), "ONES"], writes=[("ps", 6)])
            P.op("vector", lambda v: v.tensor_copy(zr, PS[6][0:1, :]), reads=[("ps", 6)], writes=["zrow"])
            P.op("vector", lambda v: v.reciprocal(zr, zr), reads=["zrow"], writes=["zrow"])
            P.op("vector", lambda v: v.tensor_copy(RBF[0:1, 0:TT], zr), reads=["zrow"], writes=["rowsbf"])
            P.op("tensor", lambda t: t.matmul(PS[6][:, :], ONES[0:1, :], RBF[0:1, 0:TT],
                                              start=True, stop=True),
                 reads=["rowsbf", "ONES"], writes=[("ps", 6)])
            P.op("vector", lambda v: v.tensor_copy(ZIB[:], PS[6][:, :]), reads=[("ps", 6)], writes=["ZIB"])
            for kc in range(nkc):
                P.op("vector", lambda v, kc=kc: v.tensor_mul(
                    ET[:, kc * TT:(kc + 1) * TT], ET[:, kc * TT:(kc + 1) * TT], ZIB[:]),
                    reads=[("ET", kc), "ZIB"], writes=[("ET", kc)])
            pi = 2 + qt % 2
            ps = PS[pi]
            for kc in range(nkc):
                P.op("tensor", lambda t, ps=ps, kc=kc, h=h: t.matmul(
                    ps[0:64, :], V[:, kc * D + h * 64: kc * D + (h + 1) * 64],
                    ET[:, kc * TT:(kc + 1) * TT], start=(kc == 0), stop=(kc == nkc - 1)),
                    reads=[("V", kc), ("ET", kc)], writes=[("ps", pi)])
            P.op("vector", lambda v, ps=ps, qt=qt, Qh=Qh, hb=hb: v.tensor_copy(
                Qh[hb:hb + 64, qt * TT:(qt + 1) * TT], ps[0:64, :]),
                reads=[("ps", pi)], writes=[qkey])

    P.fence("vector", reads=["MASKS"], writes=[("W", ci) for ci in range(NCH)])
    emit_load_weight_bf16(P, H, wot, "wot", WBF, "W", D, D)
    for dj in range(NCH):
        for tt in range(NTT):
            pi = (dj * NTT + tt) % 2
            ps = PS[pi]
            for ci in range(NCH):
                P.op("tensor", lambda t, ps=ps, ci=ci, dj=dj, tt=tt: t.matmul(
                    ps[:, :], WBF[:, ci * D + dj * 128: ci * D + (dj + 1) * 128],
                    Q[:, ci * T + tt * TT: ci * T + (tt + 1) * TT],
                    start=(ci == 0), stop=(ci == NCH - 1)),
                    reads=[("W", ci), ("Q", ci)], writes=[("ps", pi)])
            xrl = XRL[:, :]
            P.op("sync", lambda s, dj=dj, tt=tt, xrl=xrl: s.dma_start(
                out=xrl, in_=xin[dj * 128:(dj + 1) * 128, tt * TT:(tt + 1) * TT]),
                reads=["xin"], writes=["XT"], dma=True)
            P.op("scalar", lambda sc, ps=ps, dj=dj: sc.activation(
                OTMP[:], ps[:, :], AF.Identity, bias=BO[:, dj:dj + 1]),
                reads=[("ps", pi), "BO"], writes=["OTMP"])
            P.op("vector", lambda v, xrl=xrl: v.tensor_add(OTMP[:], OTMP[:], xrl),
                 reads=["OTMP", "XT"], writes=["OTMP"])
            P.op("gpsimd", lambda g, dj=dj, tt=tt: g.dma_start(
                out=xout[dj * 128:(dj + 1) * 128, tt * TT:(tt + 1) * TT], in_=OTMP[:]),
                reads=["OTMP"], writes=[("xout", dj, tt)], dma=True)



def emit_l1fa(P, H, W, pool):
    XT = H["XT"]; ROWS = H["ROWS"]; RQK = H["RQK"]; RBF = H["ROWSBF"]
    ONES = H["ONES"]; IDENT = H["IDENT"]; WBF = H["WBF"]; PS = H["PS"]
    pool.reset()
    debug = False
    dR2 = dBA = dQ = dRows = dEXP = dG9H = dGL = dGLpre = None
    dGS = dSEL9 = dWC9 = dYMH = dZH = None
    xin = W["xin"]
    wq = W["wq"]
    wkc = W["wkc"]
    wk = W["wk"]
    wvc = W["wvc"]
    wv = W["wv"]
    wo = W["wo"]
    n1w = W["n1w"]
    ropeC = W["ropeC"]
    ropeD = W["ropeD"]
    rt = W["rt"]
    wg10 = W["wg10"]
    big10 = W["big10"]
    onesC = W["onesC"]
    identC = W["identC"]
    oXI = W["oXI"]
    oXN = W["oXN"]
    oSEL = W["oSEL"]
    dGL = W["dGL"]
    SCR = H["SCR"]
    ET = SCR[:, 0:8192]
    T1 = SCR[:, 4096:8192]
    RT = H["RT"]; RA = H["RA"]; RB2 = H["RB2"]
    Y = H["YB"][:, :]; Q = H["QB"][:, :]; K = H["KB"][:, :]; V = H["VB"][:, :]
    N1W = pool.carve(128, NCH, F32)
    G10H = pool.carve(128, NCH * 10, BF16)
    G10L = pool.carve(128, NCH * 10, BF16)
    BIG10 = pool.carve(128, 10, F32)
    GL = pool.carve(128, 160, F32)
    GS = pool.carve(128, 24, F32)
    GS2 = pool.carve(128, 8, F32)
    WC8 = pool.carve(128, 8, BF16)
    SEL8 = pool.carve(8, T, BF16)
    BA = pool.carve(128, 1, F32)
    TMPC = pool.carve(128, TT, F32)
    TMPD = pool.carve(128, TT, BF16)
    TMPE = pool.carve(128, TT, BF16)
    ZIB = pool.carve(128, TT, BF16)
    SC8B = pool.carve(1, 8, F32)
    OTMP = XT[:, 1536:2048]
    XRL = XT[:, 1024:1536]
    ld = lambda dst, srcap, key: P.op(
        "sync", lambda s, dst=dst, srcap=srcap: s.dma_start(out=dst, in_=srcap),
        writes=[key], dma=True)
    ld(ONES[:], onesC, "ONES")
    ld(IDENT[:], identC, "IDENT")
    ld(RT[:], rt, "RT")
    ld(BIG10[:], big10, "BIG10")
    ld(N1W[:], n1w, "N1W")
    P.op("sync", lambda s: s.dma_start(out=H["XT"][:], in_=ropeC), reads=["XT"], writes=["XT"], dma=True)
    P.op("vector", lambda v: v.tensor_copy(RA[:], H["XT"][:]), reads=["XT"], writes=["RA"])
    P.op("sync", lambda s: s.dma_start(out=H["XT"][:], in_=ropeD), reads=["XT"], writes=["XT"], dma=True)
    P.op("vector", lambda v: v.tensor_copy(RB2[:], H["XT"][:]), reads=["XT"], writes=["RB2"])
    P.op("sync", lambda s: s.dma_start(
        out=H["XT"][:, 0:NCH * 10].rearrange("p (c n) -> p c n", c=NCH),
        in_=wg10.rearrange("(c p) n -> p c n", p=128)), reads=["XT"], writes=["XT"], dma=True)
    P.op("vector", lambda v: v.tensor_copy(G10H[:], H["XT"][:, 0:NCH * 10]), reads=["XT"], writes=["G10H"])
    P.op("vector", lambda v: v.tensor_copy(TMPC[:, 0:80], G10H[:]), reads=["G10H"], writes=["TMPC"])
    P.op("vector", lambda v: v.tensor_sub(H["XT"][:, 0:80], H["XT"][:, 0:80], TMPC[:, 0:80]),
         reads=["XT", "TMPC"], writes=["XT"])
    P.op("vector", lambda v: v.tensor_copy(G10L[:], H["XT"][:, 0:NCH * 10]), reads=["XT"], writes=["G10L"])

    emit_ln(P, H, xin, "xin", Y, "Y")

    emit_load_weight_bf16(P, H, wq, "wq", WBF, "W", D, D)
    emit_proj_fm(P, H, WBF, "W", Y, "Y", Q, "Q", NCH, D)
    P.fence("vector", reads=["MBB", "RBB"], writes=[("T1", 0), ("T1", 1)])
    emit_load_weight_bf16(P, H, wkc, "wkc", WBF, "W", D, 256)
    for dj in range(2):
        for tt in range(NTT):
            pi = (dj * NTT + tt) % 2
            ps = PS[pi]
            for ci in range(NCH):
                P.op("tensor", lambda t, ps=ps, ci=ci, dj=dj, tt=tt: t.matmul(
                    ps[:, :], WBF[:, ci * 256 + dj * 128: ci * 256 + (dj + 1) * 128],
                    Y[:, ci * T + tt * TT: ci * T + (tt + 1) * TT],
                    start=(ci == 0), stop=(ci == NCH - 1)),
                    reads=[("W", ci), ("Y", ci)], writes=[("ps", pi)])
            P.op("vector", lambda v, ps=ps, dj=dj, tt=tt: v.tensor_copy(
                T1[:, dj * T + tt * TT: dj * T + (tt + 1) * TT], ps[:, :]),
                reads=[("ps", pi)], writes=[("T1", dj)])
    emit_load_weight_bf16(P, H, wk, "wk", WBF, "W", 256, D)
    for dj in range(NCH):
        for tt in range(NTT):
            pi = (dj * NTT + tt) % 2
            ps = PS[pi]
            for ci in range(2):
                P.op("tensor", lambda t, ps=ps, ci=ci, dj=dj, tt=tt: t.matmul(
                    ps[:, :], WBF[:, ci * D + dj * 128: ci * D + (dj + 1) * 128],
                    T1[:, ci * T + tt * TT: ci * T + (tt + 1) * TT],
                    start=(ci == 0), stop=(ci == 1)),
                    reads=[("W", ci), ("T1", ci)], writes=[("ps", pi)])
            P.op("vector", lambda v, ps=ps, dj=dj, tt=tt: v.tensor_copy(
                K[:, dj * T + tt * TT: dj * T + (tt + 1) * TT], ps[:, :]),
                reads=[("ps", pi)], writes=[("K", dj)])
    emit_load_weight_bf16(P, H, wvc, "wvc", WBF, "W", D, 256)
    for dj in range(2):
        for tt in range(NTT):
            pi = (dj * NTT + tt) % 2
            ps = PS[pi]
            for ci in range(NCH):
                P.op("tensor", lambda t, ps=ps, ci=ci, dj=dj, tt=tt: t.matmul(
                    ps[:, :], WBF[:, ci * 256 + dj * 128: ci * 256 + (dj + 1) * 128],
                    Y[:, ci * T + tt * TT: ci * T + (tt + 1) * TT],
                    start=(ci == 0), stop=(ci == NCH - 1)),
                    reads=[("W", ci), ("Y", ci)], writes=[("ps", pi)])
            P.op("vector", lambda v, ps=ps, dj=dj, tt=tt: v.tensor_copy(
                T1[:, dj * T + tt * TT: dj * T + (tt + 1) * TT], ps[:, :]),
                reads=[("ps", pi)], writes=[("T1", dj)])
    emit_load_weight_bf16(P, H, wv, "wv", WBF, "W", 256, D)
    for tc in range(16):
        for dh in range(2):
            pi = (tc * 2 + dh) % 2
            ps = PS[pi]
            for ci in range(2):
                P.op("tensor", lambda t, ps=ps, ci=ci, tc=tc, dh=dh: t.matmul(
                    ps[:, :], T1[:, ci * T + tc * 128: ci * T + (tc + 1) * 128],
                    WBF[:, ci * D + dh * 512: ci * D + (dh + 1) * 512],
                    start=(ci == 0), stop=(ci == 1)),
                    reads=[("W", ci), ("T1", ci)], writes=[("ps", pi)])
            P.op("vector", lambda v, ps=ps, tc=tc, dh=dh: v.tensor_copy(
                V[:, tc * D + dh * 512: tc * D + (dh + 1) * 512], ps[:, :]),
                reads=[("ps", pi)], writes=[("V", tc)])

    for DST, dkey in ((Q, "Q"), (K, "K")):
        for c in range(NCH):
            sl = DST[:, c * T:(c + 1) * T]
            for tt in range(NTT):
                pi = tt % 2
                tsl = slice(tt * TT, (tt + 1) * TT)
                P.op("tensor", lambda t, pi=pi, sl=sl, tsl=tsl: t.matmul(
                    PS[pi][:, :], RT[:], sl[:, tsl], start=True, stop=True),
                    reads=[(dkey, c), "RT"], writes=[("ps", pi)])
                P.op("vector", lambda v, pi=pi, tsl=tsl: v.tensor_mul(
                    TMPC[:], PS[pi][:, :], RB2[:, tsl]),
                    reads=[("ps", pi), "RB2"], writes=["TMPC"])
                P.op("vector", lambda v, sl=sl, tsl=tsl: v.tensor_mul(TMPD[:], sl[:, tsl], RA[:, tsl]),
                     reads=[(dkey, c), "RA"], writes=["TMPD"])
                P.op("vector", lambda v, sl=sl, tsl=tsl: v.tensor_add(sl[:, tsl], TMPD[:], TMPC[:]),
                     reads=["TMPD", "TMPC"], writes=[(dkey, c)])

    ROWS = H["ROWS"]
    qn2 = RQK[0:1, 0:T]
    kn2 = RQK[0:1, T:2 * T]
    km2 = SC8B
    zr = ROWS[0:1, 8:8 + TT]
    RBF = H["ROWSBF"]
    P.fence("scalar", reads=[("Y", c) for c in range(NCH)] + [("T1", 0), ("T1", 1)],
            writes=[("ET", kc) for kc in range(16)])
    for h in range(16):
        c = h // 2
        hb = (h % 2) * 64
        Qh = Q[:, c * T:(c + 1) * T]
        Kh = K[:, c * T:(c + 1) * T]
        qkey, kkey = ("Q", c), ("K", c)
        for SRCh, skey, dstrow in ((Qh, qkey, qn2), (Kh, kkey, kn2)):
            P.op("scalar", lambda sc, SRCh=SRCh, hb=hb: sc.activation(
                H["XSQB"][hb:hb + 64, :], SRCh[hb:hb + 64, :], AF.Square),
                reads=[skey], writes=["XSQB"])
            for tt in range(NTT):
                pi = 4 + tt % 2
                P.op("tensor", lambda t, tt=tt, pi=pi, hb=hb: t.matmul(
                    PS[pi][0:1, :], ONES[hb:hb + 64, 0:1], H["XSQB"][hb:hb + 64, tt * TT:(tt + 1) * TT],
                    start=True, stop=True),
                    reads=["XSQB", "ONES"], writes=[("ps", pi)])
                P.op("vector", lambda v, tt=tt, pi=pi, dstrow=dstrow: v.tensor_copy(
                    dstrow[0:1, tt * TT:(tt + 1) * TT], PS[pi][0:1, :]),
                    reads=[("ps", pi)], writes=["rows"])
        P.op("vector", lambda v: v.reduce_max(km2[0:1, 0:1], qn2, axis=AX.X), reads=["rows"], writes=["rows"])
        P.op("vector", lambda v: v.reduce_max(km2[0:1, 1:2], kn2, axis=AX.X), reads=["rows"], writes=["rows"])
        P.op("vector", lambda v: v.tensor_add(km2[0:1, 2:3], km2[0:1, 0:1], km2[0:1, 1:2]),
             reads=["rows"], writes=["rows"])
        P.op("vector", lambda v: v.tensor_scalar_mul(km2[0:1, 3:4], km2[0:1, 2:3], -1.0 / 16.0),
             reads=["rows"], writes=["rows"])
        P.op("vector", lambda v: v.tensor_copy(RBF[0:1, 0:1], km2[0:1, 3:4]), reads=["rows"], writes=["rowsbf"])
        P.op("tensor", lambda t: t.matmul(PS[7][:, 0:1], ONES[0:1, :], RBF[0:1, 0:1], start=True, stop=True),
             reads=["rowsbf", "ONES"], writes=[("ps", 7)])
        P.op("vector", lambda v: v.tensor_copy(BA[:], PS[7][:, 0:1]), reads=[("ps", 7)], writes=["BA"])
        for qt in range(NTT):
            for kc in range(16):
                pi = kc % 2
                ps = PS[pi]
                P.op("tensor", lambda t, ps=ps, kc=kc, qt=qt, Kh=Kh, Qh=Qh, hb=hb: t.matmul(
                    ps[:, :], Kh[hb:hb + 64, kc * 128:(kc + 1) * 128],
                    Qh[hb:hb + 64, qt * TT:(qt + 1) * TT], start=True, stop=True),
                    reads=[kkey, qkey], writes=[("ps", pi)])
                P.op("scalar", lambda sc, ps=ps, kc=kc: sc.activation(
                    ET[:, kc * TT:(kc + 1) * TT], ps[:, :], AF.Exp, bias=BA[:], scale=1.0 / 8.0),
                    reads=[("ps", pi), "BA"], writes=[("ET", kc)])
            for kc in range(16):
                P.op("tensor", lambda t, kc=kc: t.matmul(
                    PS[6][0:1, :], ONES[:, 0:1], ET[:, kc * TT:(kc + 1) * TT],
                    start=(kc == 0), stop=(kc == 15)),
                    reads=[("ET", kc), "ONES"], writes=[("ps", 6)])
            P.op("vector", lambda v: v.tensor_copy(zr, PS[6][0:1, :]), reads=[("ps", 6)], writes=["zrow"])
            P.op("vector", lambda v: v.reciprocal(zr, zr), reads=["zrow"], writes=["zrow"])
            P.op("vector", lambda v: v.tensor_copy(RBF[0:1, 0:TT], zr), reads=["zrow"], writes=["rowsbf"])
            P.op("tensor", lambda t: t.matmul(PS[6][:, :], ONES[0:1, :], RBF[0:1, 0:TT],
                                              start=True, stop=True),
                 reads=["rowsbf", "ONES"], writes=[("ps", 6)])
            P.op("vector", lambda v: v.tensor_copy(ZIB[:], PS[6][:, :]), reads=[("ps", 6)], writes=["ZIB"])
            for kc in range(16):
                P.op("vector", lambda v, kc=kc: v.tensor_mul(
                    ET[:, kc * TT:(kc + 1) * TT], ET[:, kc * TT:(kc + 1) * TT], ZIB[:]),
                    reads=[("ET", kc), "ZIB"], writes=[("ET", kc)])
            pi = 2 + qt % 2
            ps = PS[pi]
            for kc in range(16):
                P.op("tensor", lambda t, ps=ps, kc=kc, h=h: t.matmul(
                    ps[0:64, :], V[:, kc * D + h * 64: kc * D + (h + 1) * 64],
                    ET[:, kc * TT:(kc + 1) * TT], start=(kc == 0), stop=(kc == 15)),
                    reads=[("V", kc), ("ET", kc)], writes=[("ps", pi)])
            P.op("vector", lambda v, ps=ps, qt=qt, Qh=Qh, hb=hb: v.tensor_copy(
                Qh[hb:hb + 64, qt * TT:(qt + 1) * TT], ps[0:64, :]),
                reads=[("ps", pi)], writes=[qkey])

    M32 = SCR[:, 0:4096].bitcast(F32)
    R32 = SCR[:, 4096:8192].bitcast(F32)
    P.fence("vector", reads=[("ET", kc) for kc in range(16)],
            writes=["M32", "R32", "SB16A", "SB16B"])
    emit_ln_stats32(P, H, xin, "xin", M32, R32, "M32", "R32", TMPD, TMPE,
                    ka="TMPD", kb="TMPE")
    emit_load_weight_bf16(P, H, wo, "wo", WBF, "W", D, D)
    for dj in range(NCH):
        for tt in range(NTT):
            pi = (dj * NTT + tt) % 2
            ps = PS[pi]
            for ci in range(NCH):
                P.op("tensor", lambda t, ps=ps, ci=ci, dj=dj, tt=tt: t.matmul(
                    ps[:, :], WBF[:, ci * D + dj * 128: ci * D + (dj + 1) * 128],
                    Q[:, ci * T + tt * TT: ci * T + (tt + 1) * TT],
                    start=(ci == 0), stop=(ci == NCH - 1)),
                    reads=[("W", ci), ("Q", ci)], writes=[("ps", pi)])
            tsl = slice(tt * TT, (tt + 1) * TT)
            P.op("sync", lambda s, dj=dj, tsl=tsl: s.dma_start(
                out=XRL, in_=xin[dj * 128:(dj + 1) * 128, tsl]),
                reads=["xin", "XT"], writes=["XT"], dma=True)
            P.op("vector", lambda v, tsl=tsl: v.tensor_sub(TMPC[:], XRL, M32[:, tsl]),
                 reads=["XT", "M32"], writes=["TMPC"])
            P.op("vector", lambda v, tsl=tsl: v.tensor_mul(TMPC[:], TMPC[:], R32[:, tsl]),
                 reads=["TMPC", "R32"], writes=["TMPC"])
            P.op("vector", lambda v, ps=ps: v.tensor_add(OTMP[:], ps[:, :], TMPC[:]),
                 reads=[("ps", pi), "TMPC", "XT"], writes=["XT"])
            P.op("gpsimd", lambda g, dj=dj, tt=tt: g.dma_start(
                out=oXI[dj * 128:(dj + 1) * 128, tt * TT:(tt + 1) * TT], in_=OTMP[:]),
                reads=["XT"], writes=[("oXI", dj, tt)], dma=True)
            P.op("scalar", lambda sc: sc.activation(TMPD[:], OTMP[:], AF.Square),
                 reads=["XT"], writes=["TMPD"])
            P.op("tensor", lambda t, dj=dj, tt=tt: t.matmul(
                PS[4 + tt][0:1, :], ONES[:, 0:1], TMPD[:],
                start=(dj == 0), stop=(dj == NCH - 1)),
                reads=["TMPD", "ONES"], writes=[("ps", 4 + tt)])
    P.fence("vector", reads=["M32", "R32"], writes=["RBB"])
    for tt in range(NTT):
        tsl = slice(tt * TT, (tt + 1) * TT)
        P.op("vector", lambda v, tt=tt: v.tensor_scalar_mul(RBF[0:1, 0:TT], PS[4 + tt][0:1, :], 1.0 / D),
             reads=[("ps", 4 + tt)], writes=["rowsbf"])
        pi = tt % 2
        P.op("tensor", lambda t, pi=pi: t.matmul(PS[pi][:, :], ONES[0:1, :], RBF[0:1, 0:TT],
                                                 start=True, stop=True),
             reads=["rowsbf", "ONES"], writes=[("ps", pi)])
        P.op("vector", lambda v, tsl=tsl, pi=pi: v.tensor_copy(H["RBB"][:, tsl], PS[pi][:, :]),
             reads=[("ps", pi)], writes=["RBB"])
    P.op("vector", lambda v: v.tensor_scalar_add(H["RBB"][:], H["RBB"][:], 1e-6),
         reads=["RBB"], writes=["RBB"])
    P.op("scalar", lambda sc: sc.activation(H["RBB"][:], H["RBB"][:], AF.Ln), reads=["RBB"], writes=["RBB"])
    P.op("scalar", lambda sc: sc.activation(H["RBB"][:], H["RBB"][:], AF.Exp, scale=-0.5),
         reads=["RBB"], writes=["RBB"])
    GPS = [PS[4], PS[5]]
    P.fence("vector", reads=[("ps", 4), ("ps", 5), ("ps", 6), ("ps", 7)],
            writes=[("psg", 0), ("psg", 1)])
    P.op("vector", lambda v: v.memset(GL[:], 0.0), writes=["GL"])
    for dj in range(NCH):
        for tt in range(NTT):
            P.op("sync", lambda s, dj=dj, tt=tt: s.dma_start(
                out=OTMP[:], in_=oXI[dj * 128:(dj + 1) * 128, tt * TT:(tt + 1) * TT]),
                reads=[("oXI", dj, tt), "XT"], writes=["XT"], dma=True)
            P.op("vector", lambda v, tt=tt: v.tensor_mul(
                TMPC[:], OTMP[:], H["RBB"][:, tt * TT:(tt + 1) * TT]),
                reads=["XT", "RBB"], writes=["TMPC"])
            P.op("vector", lambda v, dj=dj: v.tensor_scalar(TMPC[:], TMPC[:], N1W[:, dj:dj + 1],
                                                            None, ALU.mult),
                 reads=["TMPC", "N1W"], writes=["TMPC"])
            P.op("gpsimd", lambda g, dj=dj, tt=tt: g.dma_start(
                out=oXN[dj * 128:(dj + 1) * 128, tt * TT:(tt + 1) * TT], in_=TMPC[:]),
                reads=["TMPC"], writes=[("oXN", dj, tt)], dma=True)
            P.op("vector", lambda v, dj=dj, tt=tt: v.tensor_copy(
                Y[:, dj * T + tt * TT: dj * T + (tt + 1) * TT], TMPC[:]),
                reads=["TMPC"], writes=[("Y", dj)])
            P.op("vector", lambda v, dj=dj, tt=tt: v.tensor_copy(
                OTMP[:], Y[:, dj * T + tt * TT: dj * T + (tt + 1) * TT]),
                reads=[("Y", dj), "XT"], writes=["XT"])
            P.op("vector", lambda v: v.tensor_sub(TMPC[:], TMPC[:], OTMP[:]),
                 reads=["TMPC", "XT"], writes=["TMPC"])
            P.op("vector", lambda v: v.tensor_copy(TMPD[:], TMPC[:]), reads=["TMPC"], writes=["TMPD"])
            for ms in range(4):
                mt = tt * 4 + ms
                gp = GPS[mt % 2]
                terms = [
                    (("Y", dj), Y[:, dj * T + tt * TT + ms * 128: dj * T + tt * TT + (ms + 1) * 128],
                     "G10H", G10H),
                    (("Y", dj), Y[:, dj * T + tt * TT + ms * 128: dj * T + tt * TT + (ms + 1) * 128],
                     "G10L", G10L),
                    ("TMPD", TMPD[:, ms * 128:(ms + 1) * 128], "G10H", G10H),
                ]
                for ti, (Lk, Lap, Rk, Rap) in enumerate(terms):
                    P.op("tensor", lambda t, gp=gp, Lap=Lap, Rap=Rap, dj=dj, ti=ti: t.matmul(
                        gp[:, 0:10], Lap, Rap[:, dj * 10:(dj + 1) * 10],
                        start=(ti == 0), stop=(ti == 2)),
                        reads=[Lk, Rk], writes=[("psg", mt % 2)])
                P.op("vector", lambda v, gp=gp, mt=mt: v.tensor_add(
                    GL[:, mt * 10:(mt + 1) * 10], GL[:, mt * 10:(mt + 1) * 10], gp[:, 0:10]),
                    reads=[("psg", mt % 2), "GL"], writes=["GL"])
    for mt in range(16):
        P.op("vector", lambda v, mt=mt: v.tensor_add(GL[:, mt * 10:(mt + 1) * 10],
                                                     GL[:, mt * 10:(mt + 1) * 10], BIG10[:]),
             reads=["GL", "BIG10"], writes=["GL"])
    P.op("gpsimd", lambda g: g.dma_start(out=dGL, in_=GL[:]), reads=["GL"], writes=[("dump",)], dma=True)
    for mt in range(16):
        g2 = GL[:, mt * 10:mt * 10 + 2]
        e8 = GL[:, mt * 10 + 2:mt * 10 + 10]
        dd = GS[:, 0:1]
        p0 = GS[:, 1:2]
        gsel = GS[:, 2:3]
        gidx = GS[:, 3:4]
        onemg = GS[:, 4:5]
        ml = GS[:, 8:16]
        mx = GS[:, 16:17]
        ssum = GS[:, 17:18]
        sinv = GS[:, 18:19]
        m1 = GS[:, 19:20]
        m2 = GS[:, 20:21]
        P.op("vector", lambda v, g2=g2: v.tensor_sub(dd, g2[:, 0:1], g2[:, 1:2]),
             reads=["GL"], writes=["GS"])
        P.op("scalar", lambda sc: sc.activation(p0, dd, AF.Sigmoid), reads=["GS"], writes=["GS"])
        P.op("vector", lambda v: v.tensor_scalar(gsel, p0, -1.0, 1.0, ALU.mult, op1=ALU.add),
             reads=["GS"], writes=["GS"])
        P.op("vector", lambda v: v.tensor_max(gsel, gsel, p0), reads=["GS"], writes=["GS"])
        P.op("vector", lambda v: v.tensor_scalar(gidx, p0, 0.5, None, ALU.is_lt),
             reads=["GS"], writes=["GS"])
        P.op("vector", lambda v: v.tensor_scalar(onemg, gidx, -1.0, 1.0, ALU.mult, op1=ALU.add),
             reads=["GS"], writes=["GS"])
        P.op("vector", lambda v, e8=e8: v.tensor_scalar(GS2[:, 0:4], e8[:, 0:4], 30.0, None, ALU.add),
             reads=["GL"], writes=["GS2"])
        P.op("vector", lambda v: v.tensor_scalar(ml[:, 0:4], GS2[:, 0:4], onemg, None, ALU.mult),
             reads=["GS2", "GS"], writes=["GS"])
        P.op("vector", lambda v, e8=e8: v.tensor_scalar(GS2[:, 0:4], e8[:, 4:8], 30.0, None, ALU.add),
             reads=["GL"], writes=["GS2"])
        P.op("vector", lambda v: v.tensor_scalar(ml[:, 4:8], GS2[:, 0:4], gidx, None, ALU.mult),
             reads=["GS2", "GS"], writes=["GS"])
        P.op("vector", lambda v: v.tensor_scalar_add(ml, ml, -30.0), reads=["GS"], writes=["GS"])
        P.op("vector", lambda v: v.reduce_max(mx, ml, axis=AX.X), reads=["GS"], writes=["GS"])
        P.op("vector", lambda v: v.tensor_scalar(GS2[:, 0:1], mx, -1.0, None, ALU.mult),
             reads=["GS"], writes=["GS2"])
        P.op("scalar", lambda sc: sc.activation(ml, ml, AF.Exp, bias=GS2[:, 0:1]),
             reads=["GS", "GS2"], writes=["GS"])
        P.op("vector", lambda v: v.reduce_sum(ssum, ml, axis=AX.X), reads=["GS"], writes=["GS"])
        P.op("vector", lambda v: v.reciprocal(sinv, ssum), reads=["GS"], writes=["GS"])
        P.op("vector", lambda v: v.tensor_scalar(ml, ml, sinv, None, ALU.mult),
             reads=["GS"], writes=["GS"])
        P.op("vector", lambda v: v.tensor_scalar(ml, ml, gsel, None, ALU.mult),
             reads=["GS"], writes=["GS"])
        P.op("vector", lambda v: v.reduce_max(m1, ml, axis=AX.X), reads=["GS"], writes=["GS"])
        P.op("vector", lambda v: v.tensor_scalar(GS2[:], ml, m1, None, ALU.is_lt),
             reads=["GS"], writes=["GS2"])
        P.op("vector", lambda v: v.tensor_mul(GS2[:], GS2[:], ml), reads=["GS2", "GS"], writes=["GS2"])
        P.op("vector", lambda v: v.reduce_max(m2, GS2[:], axis=AX.X), reads=["GS2"], writes=["GS"])
        P.op("vector", lambda v: v.tensor_scalar(GS2[:], ml, m2, None, ALU.is_ge),
             reads=["GS"], writes=["GS2"])
        P.op("vector", lambda v: v.tensor_mul(GS2[:], GS2[:], ml), reads=["GS2", "GS"], writes=["GS2"])
        P.op("vector", lambda v: v.tensor_copy(WC8[:], GS2[:]), reads=["GS2"], writes=["WC8"])
        pi = 2 + mt % 2
        P.op("tensor", lambda t, pi=pi: t.matmul(PS[pi][0:8, 0:128], WC8[:], IDENT[:],
                                                 start=True, stop=True),
             reads=["WC8", "IDENT"], writes=[("ps", pi)])
        P.op("vector", lambda v, mt=mt, pi=pi: v.tensor_copy(SEL8[:, mt * 128:(mt + 1) * 128],
                                                             PS[pi][0:8, 0:128]),
             reads=[("ps", pi)], writes=["SEL8"])
    SELF = H["XT"][0:8, 0:T]
    P.op("vector", lambda v: v.tensor_copy(SELF, SEL8[:]), reads=["SEL8", "XT"], writes=["XT"])
    P.op("gpsimd", lambda g: g.dma_start(out=oSEL, in_=SELF), reads=["XT"], writes=[("osel",)], dma=True)



def emit_l1fb(P, H, W, pool):
    XT = H["XT"]; ROWS = H["ROWS"]; RQK = H["RQK"]; RBF = H["ROWSBF"]
    ONES = H["ONES"]; IDENT = H["IDENT"]; WBF = H["WBF"]; PS = H["PS"]
    pool.reset()
    debug = False
    dR2 = dBA = dQ = dRows = dEXP = dG9H = dGL = dGLpre = None
    dGS = dSEL9 = dWC9 = dYMH = dZH = None
    xin = W["xin"]
    xiI = W["xiI"]
    xnI = W["xnI"]
    selI = W["selI"]
    wsi = W["wsi"]
    wso = W["wso"]
    w1 = W["w1"]
    w2 = W["w2"]
    n2w = W["n2w"]
    onesC = W["onesC"]
    ehot8C = W["ehot8C"]
    xout = W["xout"]
    TH = T // 2
    XNB = H["YB"][:, :]
    ARENA = H["QB"][:, :]
    HH1 = H["KB"][:, :]
    OACC = H["VB"][:, 0:16384].bitcast(F32)
    WCB8 = H["SCR"][:, 0:8192]
    N2W = pool.carve(128, NCH, F32)
    WCBB = pool.carve(128, TH, BF16)
    SX = pool.carve(128, TH, BF16)
    HX = pool.carve(128, TH, BF16)
    XSQ = pool.carve(128, TH, BF16)
    RBB = pool.carve(128, TH, F32)
    SEL8 = pool.carve(8, T, BF16)
    EHOT8 = pool.carve(8, 8 * 128, BF16)
    OT = XT[:, 1024:1536]
    XRL = XT[:, 1536:2048]
    ld = lambda dst, srcap, key: P.op(
        "sync", lambda s, dst=dst, srcap=srcap: s.dma_start(out=dst, in_=srcap),
        writes=[key], dma=True)
    ld(ONES[:], onesC, "ONES")
    ld(EHOT8[:], ehot8C, "EHOT8")
    ld(N2W[:], n2w, "N2W")
    for c in range(NCH):
        P.op("sync", lambda s, c=c: s.dma_start(out=H["XT"][:], in_=xnI[c * 128:(c + 1) * 128, :]),
             reads=["XT"], writes=["XT"], dma=True)
        P.op("vector", lambda v, c=c: v.tensor_copy(XNB[:, c * T:(c + 1) * T], H["XT"][:]),
             reads=["XT"], writes=[("XNB", c)])
    P.op("sync", lambda s: s.dma_start(out=H["XT"][0:8, 0:T], in_=selI), reads=["XT"], writes=["XT"], dma=True)
    P.op("vector", lambda v: v.tensor_copy(SEL8[:], H["XT"][0:8, 0:T]), reads=["XT"], writes=["SEL8"])

    def swiglu_tiles(P, wA, wAkey, OUTBUF, outkey, mo):
        for dj in range(16):
            for part, co in (("a", dj * 128), ("b", 2048 + dj * 128)):
                for ci in range(NCH):
                    stg = H["WST"][:, 0:128] if part == "a" else H["WST"][:, 128:256]
                    woff = (ci if part == "a" else NCH + ci) * 128
                    P.op("sync", lambda s, ci=ci, co=co, stg=stg: s.dma_start(
                        out=stg, in_=wA[ci * 128:(ci + 1) * 128, co:co + 128]),
                        reads=["XT"], writes=["XT"], dma=True)
                    P.op("vector", lambda v, stg=stg, woff=woff: v.tensor_copy(
                        WBF[:, woff:woff + 128], stg),
                        reads=["XT"], writes=[("W", part, ci)])
            pa = PS[dj % 2]
            pb = PS[2 + dj % 2]
            for ci in range(NCH):
                P.op("tensor", lambda t, pa=pa, ci=ci, mo=mo: t.matmul(
                    pa[:, :], WBF[:, ci * 128:(ci + 1) * 128],
                    XNB[:, ci * T + mo: ci * T + mo + 512], start=(ci == 0), stop=(ci == NCH - 1)),
                    reads=[("W", "a", ci), ("XNB", ci)], writes=[("ps", dj % 2)])
                P.op("tensor", lambda t, pb=pb, ci=ci, mo=mo: t.matmul(
                    pb[:, :], WBF[:, (NCH + ci) * 128:(NCH + ci + 1) * 128],
                    XNB[:, ci * T + mo: ci * T + mo + 512], start=(ci == 0), stop=(ci == NCH - 1)),
                    reads=[("W", "b", ci), ("XNB", ci)], writes=[("ps", 2 + dj % 2)])
            pa2 = PS[4 + dj % 2]
            pb2 = PS[6 + dj % 2]
            for ci in range(NCH):
                P.op("tensor", lambda t, pa2=pa2, ci=ci, mo=mo: t.matmul(
                    pa2[:, :], WBF[:, ci * 128:(ci + 1) * 128],
                    XNB[:, ci * T + mo + 512: ci * T + mo + 1024], start=(ci == 0), stop=(ci == NCH - 1)),
                    reads=[("W", "a", ci), ("XNB", ci)], writes=[("ps", 4 + dj % 2)])
                P.op("tensor", lambda t, pb2=pb2, ci=ci, mo=mo: t.matmul(
                    pb2[:, :], WBF[:, (NCH + ci) * 128:(NCH + ci + 1) * 128],
                    XNB[:, ci * T + mo + 512: ci * T + mo + 1024], start=(ci == 0), stop=(ci == NCH - 1)),
                    reads=[("W", "b", ci), ("XNB", ci)], writes=[("ps", 6 + dj % 2)])
            for half2, (pA, pB, kA, kB) in enumerate(((pa, pb, dj % 2, 2 + dj % 2),
                                                      (pa2, pb2, 4 + dj % 2, 6 + dj % 2))):
                P.op("scalar", lambda sc, pA=pA, half2=half2: sc.activation(
                    SX[:, half2 * 512:(half2 + 1) * 512], pA[:, :], AF.Silu),
                    reads=[("ps", kA)], writes=["SX"])
                P.op("vector", lambda v, pB=pB, half2=half2: v.tensor_copy(
                    HX[:, half2 * 512:(half2 + 1) * 512], pB[:, :]),
                    reads=[("ps", kB)], writes=["HX"])
            P.op("vector", lambda v, dj=dj: v.tensor_mul(
                OUTBUF[:, dj * 1024:(dj + 1) * 1024], SX[:], HX[:]),
                reads=["SX", "HX"], writes=[(outkey, dj)])

    for mh in range(2):
        mo = mh * TH
        swiglu_tiles(P, wsi, "wsi", ARENA, "SHI", mo)
        for doh in range(2):
            for dj in range(16):
                P.op("sync", lambda s, dj=dj, doh=doh: s.dma_start(
                    out=H["WST"][:, 0:512], in_=wso[dj * 128:(dj + 1) * 128, doh * 512:(doh + 1) * 512]),
                    reads=["XT"], writes=["XT"], dma=True)
                P.op("vector", lambda v, dj=dj: v.tensor_copy(WBF[:, 0:512], H["WST"][:, 0:512]),
                     reads=["XT"], writes=[("W2s",)])
                for do4 in range(4):
                    for t2 in range(2):
                        pi = do4 * 2 + t2
                        P.op("tensor", lambda t, pi=pi, dj=dj, doh=doh, do4=do4, t2=t2: t.matmul(
                            PS[pi][:, :], WBF[:, 0:512][:, do4 * 128:(do4 + 1) * 128],
                            ARENA[:, dj * 1024 + t2 * 512: dj * 1024 + (t2 + 1) * 512],
                            start=(dj == 0), stop=(dj == 15)),
                            reads=[("W2s",), ("SHI", dj)], writes=[("ps", pi)])
            for do4 in range(4):
                for t2 in range(2):
                    pi = do4 * 2 + t2
                    do = doh * 4 + do4
                    P.op("vector", lambda v, pi=pi, do=do, t2=t2: v.tensor_copy(
                        OACC[:, do * 1024 + t2 * 512: do * 1024 + (t2 + 1) * 512], PS[pi][:, :]),
                        reads=[("ps", pi)], writes=[("OACC", do)])
        swiglu_tiles(P, w1, "w1", HH1, "HH1", mo)
        for e in range(8):
            for t2 in range(2):
                P.op("tensor", lambda t, e=e, mo=mo, t2=t2: t.matmul(
                    PS[0][:, :], EHOT8[:, e * 128:(e + 1) * 128],
                    SEL8[0:8, mo + t2 * 512:mo + (t2 + 1) * 512],
                    start=True, stop=True), reads=["EHOT8", "SEL8"], writes=[("ps", 0)])
                P.op("vector", lambda v, e=e, t2=t2: v.tensor_copy(
                    WCB8[:, e * TH + t2 * 512: e * TH + (t2 + 1) * 512], PS[0][:, :]),
                    reads=[("ps", 0)], writes=["WCB8"])
        for doh in range(2):
            for e in range(8):
                for dj in range(16):
                    P.op("vector", lambda v, dj=dj, e=e: v.tensor_mul(
                        ARENA[:, dj * 1024:(dj + 1) * 1024], HH1[:, dj * 1024:(dj + 1) * 1024],
                        WCB8[:, e * TH:(e + 1) * TH]),
                        reads=[("HH1", dj), "WCB8"], writes=[("SHI", dj)])
                    P.op("sync", lambda s, dj=dj, doh=doh, e=e: s.dma_start(
                        out=H["WST"][:, 0:512], in_=w2[e, dj * 128:(dj + 1) * 128, doh * 512:(doh + 1) * 512]),
                        reads=["XT"], writes=["XT"], dma=True)
                    P.op("vector", lambda v: v.tensor_copy(WBF[:, 0:512], H["WST"][:, 0:512]),
                         reads=["XT"], writes=[("W2s",)])
                    for do4 in range(4):
                        for t2 in range(2):
                            pi = do4 * 2 + t2
                            P.op("tensor", lambda t, pi=pi, dj=dj, do4=do4, t2=t2, e=e: t.matmul(
                                PS[pi][:, :], WBF[:, 0:512][:, do4 * 128:(do4 + 1) * 128],
                                ARENA[:, dj * 1024 + t2 * 512: dj * 1024 + (t2 + 1) * 512],
                                start=(e == 0 and dj == 0), stop=(e == 7 and dj == 15)),
                                reads=[("W2s",), ("SHI", dj)], writes=[("ps", pi)])
            for do4 in range(4):
                for t2 in range(2):
                    pi = do4 * 2 + t2
                    do = doh * 4 + do4
                    P.op("vector", lambda v, pi=pi, do=do, t2=t2: v.tensor_add(
                        OACC[:, do * 1024 + t2 * 512: do * 1024 + (t2 + 1) * 512],
                        OACC[:, do * 1024 + t2 * 512: do * 1024 + (t2 + 1) * 512], PS[pi][:, :]),
                        reads=[("ps", pi), ("OACC", do)], writes=[("OACC", do)])
        r0 = None
        for dj in range(NCH):
            P.op("sync", lambda s, dj=dj, mo=mo: s.dma_start(
                out=H["XT"][:, 0:TH], in_=xnI[dj * 128:(dj + 1) * 128, mo:mo + TH]),
                reads=["XT"], writes=["XT"], dma=True)
            P.op("vector", lambda v, dj=dj: v.tensor_add(
                OACC[:, dj * 1024:(dj + 1) * 1024], OACC[:, dj * 1024:(dj + 1) * 1024],
                H["XT"][:, 0:TH]),
                reads=["XT", ("OACC", dj)], writes=[("OACC", dj)])
            P.op("scalar", lambda sc, dj=dj: sc.activation(
                XSQ[:], OACC[:, dj * 1024:(dj + 1) * 1024], AF.Square),
                reads=[("OACC", dj)], writes=["XSQ"])
            for t2 in range(2):
                P.op("tensor", lambda t, dj=dj, t2=t2: t.matmul(
                    PS[t2][0:1, :], ONES[:, 0:1], XSQ[:, t2 * 512:(t2 + 1) * 512],
                    start=(dj == 0), stop=(dj == NCH - 1)),
                    reads=["XSQ", "ONES"], writes=[("ps", t2)])
        for t2 in range(2):
            P.op("vector", lambda v, t2=t2: v.tensor_scalar_mul(H["ROWSBF"][0:1, 0:TT], PS[t2][0:1, :], 1.0 / D),
                 reads=[("ps", t2)], writes=["rowsbf"])
            P.op("tensor", lambda t, t2=t2: t.matmul(PS[t2][:, :], ONES[0:1, :], H["ROWSBF"][0:1, 0:TT],
                                                     start=True, stop=True),
                 reads=["rowsbf", "ONES"], writes=[("ps", t2)])
            P.op("vector", lambda v, t2=t2: v.tensor_copy(RBB[:, t2 * 512:(t2 + 1) * 512], PS[t2][:, :]),
                 reads=[("ps", t2)], writes=["RBB2"])
        P.op("vector", lambda v: v.tensor_scalar_add(RBB[:], RBB[:], 1e-6), reads=["RBB2"], writes=["RBB2"])
        P.op("scalar", lambda sc: sc.activation(RBB[:], RBB[:], AF.Ln), reads=["RBB2"], writes=["RBB2"])
        P.op("scalar", lambda sc: sc.activation(RBB[:], RBB[:], AF.Exp, scale=-0.5),
             reads=["RBB2"], writes=["RBB2"])
        for dj in range(NCH):
            for t2 in range(2):
                sl = slice(dj * 1024 + t2 * 512, dj * 1024 + (t2 + 1) * 512)
                P.op("sync", lambda s, dj=dj, t2=t2, mo=mo: s.dma_start(
                    out=XRL, in_=xin[dj * 128:(dj + 1) * 128, mo + t2 * 512:mo + (t2 + 1) * 512]),
                    reads=["XT"], writes=["XT"], dma=True)
                P.op("vector", lambda v, sl=sl, t2=t2: v.tensor_mul(
                    OT, OACC[:, sl], RBB[:, t2 * 512:(t2 + 1) * 512]),
                    reads=[("OACC", sl.start // 1024), "RBB2", "XT"], writes=["XT"])
                P.op("vector", lambda v, dj=dj: v.tensor_scalar(OT, OT, N2W[:, dj:dj + 1], None, ALU.mult),
                     reads=["XT", "N2W"], writes=["XT"])
                P.op("vector", lambda v: v.tensor_add(OT, OT, XRL), reads=["XT"], writes=["XT"])
                P.op("gpsimd", lambda g, dj=dj, t2=t2, mo=mo: g.dma_start(
                    out=xout[dj * 128:(dj + 1) * 128, mo + t2 * 512:mo + (t2 + 1) * 512], in_=OT),
                    reads=["XT"], writes=[("xout", dj, mh, t2)], dma=True)





def build_fused(pattern, upto=8, niter=10):
    m = len(pattern)
    nmt = m // 128
    nc = bass.Bass("TRN2", target_bir_lowering=False)
    es = ExitStack()
    din = lambda n, s, dt=F32: nc.dram_tensor(n, s, dt, kind="ExternalInput").ap()

    G = {}
    G["aa"] = {"wq": din("aa_wq", [D, D]), "wk": din("aa_wk", [D, D]),
               "wv": din("aa_wv", [D, D]),
               "onesC": din("aa_onesC", [128, 128], BF16),
               "masksC": din("aa_masksC", [4, 128, 512], BF16)}
    G["f0"] = {"sw1t": din("f0_sw1t", [D, 512]), "sw2t": din("f0_sw2t", [D, 512]),
               "sw3t": din("f0_sw3t", [512, D]), "sb1c": din("f0_sb1c", [128, 4]),
               "sb2c": din("f0_sb2c", [128, 4]), "sb3c": din("f0_sb3c", [128, NCH]),
               "slng": din("f0_slng", [128, 4]), "slnb": din("f0_slnb", [128, 4]),
               "g9": din("f0_g9", [D, 9]), "big9": din("f0_big9", [128, 9]),
               "expb": din("f0_expb", [128, 8]),
               "ew1t": din("f0_ew1t", [8, D, 512]), "ew2t": din("f0_ew2t", [8, D, 512]),
               "ew3t": din("f0_ew3t", [8, 512, D]),
               "eb1c": din("f0_eb1c", [128, 32]), "eb2c": din("f0_eb2c", [128, 32]),
               "eb3r": din("f0_eb3r", [8, D]),
               "selg": din("f0_selg", [128, 4]), "selb": din("f0_selb", [128, 4]),
               "mskt": din("f0_mskt", [128, nmt]),
               "onesC": din("f0_onesC", [128, 128], BF16),
               "identC": din("f0_identC", [128, 128], BF16),
               "ehotC": din("f0_ehotC", [9, 9 * 128], BF16)}
    G["ab"] = {"wv": din("ab_wv", [D, D]), "bv": din("ab_bv", [128, NCH]),
               "onesC": din("ab_onesC", [128, 128], BF16)}
    G["a1"] = {"wqt": din("a1_wqt", [D, D]), "wkt": din("a1_wkt", [D, D]),
               "wvt": din("a1_wvt", [D, D]), "wot": din("a1_wot", [D, D]),
               "bq": din("a1_bq", [128, NCH]), "bk": din("a1_bk", [128, NCH]),
               "bvrow": din("a1_bvrow", [1, D]), "bo": din("a1_bo", [128, NCH]),
               "onesC": din("a1_onesC", [128, 128], BF16),
               "identC": din("a1_identC", [128, 128], BF16)}
    G["b1"] = {"wqt": din("b1_wqt", [D, D]), "wkt": din("b1_wkt", [D, D]),
               "wvt": din("b1_wvt", [D, D]), "wot": din("b1_wot", [D, D]),
               "bq": din("b1_bq", [128, NCH]), "bk": din("b1_bk", [128, NCH]),
               "bvrow": din("b1_bvrow", [1, D]), "bo": din("b1_bo", [128, NCH]),
               "ropeA": din("b1_ropeA", [128, T]), "ropeB": din("b1_ropeB", [128, T]),
               "rt": din("b1_rt", [128, 128], BF16),
               "onesC": din("b1_onesC", [128, 128], BF16),
               "masksC": din("b1_masksC", [4, 128, 512], BF16)}
    G["fa"] = {"wq": din("fa_wq", [D, D]), "wkc": din("fa_wkc", [D, 256]),
               "wk": din("fa_wk", [256, D]), "wvc": din("fa_wvc", [D, 256]),
               "wv": din("fa_wv", [256, D]), "wo": din("fa_wo", [D, D]),
               "n1w": din("fa_n1w", [128, NCH]),
               "ropeC": din("fa_ropeC", [128, T]), "ropeD": din("fa_ropeD", [128, T]),
               "rt": din("fa_rt", [128, 128], BF16),
               "wg10": din("fa_wg10", [D, 10]), "big10": din("fa_big10", [128, 10]),
               "onesC": din("fa_onesC", [128, 128], BF16),
               "identC": din("fa_identC", [128, 128], BF16)}
    G["fb"] = {"wsi": din("fb_wsi", [D, 4096]), "wso": din("fb_wso", [2048, D]),
               "w1": din("fb_w1", [D, 4096]), "w2": din("fb_w2", [8, 2048, D]),
               "n2w": din("fb_n2w", [128, NCH]),
               "onesC": din("fb_onesC", [128, 128], BF16),
               "ehot8C": din("fb_ehot8C", [8, 8 * 128], BF16)}

    xinE = din("xin", [D, T])
    xoutE = nc.dram_tensor("xout", [D, T], F32, kind="ExternalOutput").ap()
    XD = [nc.dram_tensor(f"XD{i}", [D, T], F32, kind="ExternalOutput").ap()
          for i in range(2)]
    oXI = nc.dram_tensor("oXI", [D, T], F32, kind="ExternalOutput").ap()
    oXN = nc.dram_tensor("oXN", [D, T], F32, kind="ExternalOutput").ap()
    oSEL = nc.dram_tensor("oSEL", [8, T], F32, kind="ExternalOutput").ap()
    dGL = nc.dram_tensor("dGL", [128, 160], F32, kind="ExternalOutput").ap()
    G["fa"].update(oXI=oXI, oXN=oXN, oSEL=oSEL, dGL=dGL)
    G["fb"].update(xiI=oXI, xnI=oXN, selI=oSEL)

    sb = lambda name, shape, dt: es.enter_context(nc.sbuf_tensor(name, shape, dt))
    H = {}
    H["XT"] = sb("XT", [128, T], F32)
    SCR = sb("SCR", [128, 8192], BF16)
    H["SCR"] = SCR
    H["XB"] = SCR[:, 0:T]
    H["XSQB"] = SCR[:, T:2 * T]
    H["MBB"] = SCR[:, 2 * T:3 * T]
    H["RBB"] = SCR[:, 3 * T:4 * T]
    H["ROWS"] = sb("ROWS", [1, 520], F32)
    H["RQK"] = sb("RQK", [1, 2 * T], BF16)
    H["ROWSBF"] = sb("ROWSBF", [1, TT], BF16)
    H["ONES"] = sb("ONESs", [128, 128], BF16)
    H["IDENT"] = sb("IDENT", [128, 128], BF16)
    H["RT"] = sb("RT", [128, 128], BF16)
    H["RA"] = sb("RA", [128, T], BF16)
    H["RB2"] = sb("RB2", [128, T], BF16)
    H["YB"] = sb("YB", [128, NCH * T], BF16)
    H["QB"] = sb("QB", [128, NCH * T], BF16)
    H["KB"] = sb("KB", [128, NCH * T], BF16)
    H["VB"] = sb("VB", [128, NCH * T], BF16)
    H["WBF"] = sb("WBF", [128, NCH * D], BF16)
    H["POOL"] = sb("POOL", [128, 9728], BF16)
    H["WST"] = H["XT"][:, 0:1024]
    PSQ = es.enter_context(nc.psum_tensor("psq", [128, 2048], F32))
    H["PSQ"] = PSQ
    PS = [PSQ[:, i * 512:(i + 1) * 512] for i in range(4)] + \
         [es.enter_context(nc.psum_tensor(f"ps{i}", [128, 512], F32))[:, :]
          for i in range(4, 8)]
    H["PS"] = PS

    P = Prog(nc, es)
    pool = Pool(H["POOL"], 9728)

    seq = [("aa", emit_l0aa, True), ("f0", emit_l0ffn, True), ("ab", emit_l0ab, True),
           ("f0", emit_l0ffn, True), ("a1", emit_l1aa, True), ("fa", emit_l1fa, False),
           ("fb", emit_l1fb, True), ("b1", emit_l1ab, True), ("fa", emit_l1fa, False),
           ("fb", emit_l1fb, True)]
    ends = {1: 0, 2: 1, 3: 2, 4: 3, 5: 4, 6: 6, 7: 7, 8: 9}
    last = ends[upto]
    cur = xinE
    k = 0
    for si, (g, fn, adv) in enumerate(seq[:last + 1]):
        Wm = dict(G[g])
        Wm["xin"] = cur
        if adv:
            Wm["xout"] = xoutE if si == last else XD[k % 2]
        if fn is emit_l0ffn:
            fn(P, H, Wm, pool, pattern)
        elif fn is emit_l1aa:
            fn(P, H, Wm, pool, niter=niter)
        else:
            fn(P, H, Wm, pool)
        if adv:
            cur = Wm["xout"]
            k += 1
        if si != last:
            P.sub_barrier()

    P.barrier_all_dma_done("gpsimd")
    with nc.Block() as block:
        P.finalize(block)

    return nc


def fused_host_inputs(INP, pattern):
    ones, ident, masks = host_constants()
    g = {}
    g.update({"aa_wq": np.ascontiguousarray(INP["L0_attn_a_Wq"].T),
              "aa_wk": np.ascontiguousarray(INP["L0_attn_a_Wk"].T),
              "aa_wv": np.ascontiguousarray(INP["L0_attn_a_Wv"].T),
              "aa_onesC": ones, "aa_masksC": masks})
    for kk, vv in l0_ffn_host_inputs(INP, pattern, ones, ident).items():
        g["f0_" + kk] = vv
    g.update({"ab_wv": np.ascontiguousarray(INP["L0_attn_b_Wv"].T),
              "ab_bv": np.ascontiguousarray(
                  np.asarray(INP["L0_attn_b_bv"]).reshape(NCH, 128).T),
              "ab_onesC": ones})
    g.update({"a1_wqt": np.ascontiguousarray(INP["L1_attn_a_Wq"]),
              "a1_wkt": np.ascontiguousarray(INP["L1_attn_a_Wk"]),
              "a1_wvt": np.ascontiguousarray(INP["L1_attn_a_Wv"]),
              "a1_wot": np.ascontiguousarray(INP["L1_attn_a_Wo"]),
              "a1_bq": np.ascontiguousarray(np.asarray(INP["L1_attn_a_bq"]).reshape(NCH, 128).T),
              "a1_bk": np.ascontiguousarray(np.asarray(INP["L1_attn_a_bk"]).reshape(NCH, 128).T),
              "a1_bvrow": np.ascontiguousarray(np.asarray(INP["L1_attn_a_bv"]).reshape(1, D)),
              "a1_bo": np.ascontiguousarray(np.asarray(INP["L1_attn_a_bo"]).reshape(NCH, 128).T),
              "a1_onesC": ones, "a1_identC": ident})
    for kk, vv in l1_attn_b_host_inputs(INP, ones, masks).items():
        g["b1_" + kk] = vv
    dA, dB = l1_ffn_host_inputs(INP, ones, ident)
    for kk, vv in dA.items():
        g["fa_" + kk] = vv
    for kk, vv in dB.items():
        g["fb_" + kk] = vv
    return g


def kernel(**inputs):
    INP = inputs
    x = np.asarray(INP["L0_attn_a_x"])
    xs = [np.ascontiguousarray(x[b].T) for b in range(NCORES)]
    pattern = sorted(set(int(t) for t in INP["L0_ffn_index_t"]))
    assert len(pattern) % 512 == 0, len(pattern)
    key = ("fused", tuple(pattern))
    if key not in _PROG_CACHE:
        _PROG_CACHE[key] = build_fused(tuple(pattern), 8, niter=7)
    nc = _PROG_CACHE[key]
    g = fused_host_inputs(INP, pattern)
    masks_b = l0_ffn_masks(INP, pattern)
    maps = [dict(g, xin=xs[b], f0_mskt=masks_b[b]) for b in range(NCORES)]
    res = run_bass_kernel_spmd(nc, maps, core_ids=list(range(NCORES)))
    out = np.stack([np.asarray(res.results[b]["xout"]).T
                    for b in range(NCORES)]).astype(np.float32)
    return out
